# version 05
# baseline (speedup 1.0000x reference)
import numpy as np
import ml_dtypes
import concourse.bass as bass
import concourse.mybir as mybir
from concourse.bass_utils import run_bass_kernel_spmd
from contextlib import ExitStack

F32 = mybir.dt.float32
BF16 = mybir.dt.bfloat16
AF = mybir.ActivationFunctionType
OP = mybir.AluOpType
AX = mybir.AxisListType

B, T, D, H, Dh, E, FF, C = 2, 8192, 1024, 16, 64, 8, 4096, 128
NCH = T // C
NT = T // 128
QTOK, QNT = 2048, 16
LN_EPS, K_EPS, FFN_EPS = 1e-5, 1e-12, 1e-12


class Em:
    M = {"pe": "pe", "ac": "ac", "dv": "dv", "po": "po", "sy": "sy", "gp": "po"}

    def __init__(self, stream, eng, sems):
        self.stream = stream
        self.eng = eng
        self.sems = sems
        self.cnt = {k: 0 for k in ("pe", "ac", "dv", "po", "sy", "gp")}

    def __call__(self, which, fn, waits=None):
        inc = 16 if which in ("sy", "gp") else 1
        self.cnt[which] += inc
        if Em.M[which] == self.stream:
            if waits:
                for sname, n in waits.items():
                    if n > 0:
                        self.eng.wait_ge(self.sems[sname], n)
            fn(self.eng).then_inc(self.sems[which], inc)
        return self.cnt[which]

    def wait(self, stream, waits):
        if stream == self.stream and waits:
            for sname, n in waits.items():
                if n > 0:
                    self.eng.wait_ge(self.sems[sname], n)


def build_nc():
    nc = bass.Bass("TRN2", target_bir_lowering=False, debug=False)

    xb = nc.dram_tensor("xb", [T, D], F32, kind="ExternalInput").ap()
    xq = nc.dram_tensor("xq", [QTOK, D], F32, kind="ExternalInput").ap()
    qsel = nc.dram_tensor("qsel", [128, 4], F32, kind="ExternalInput").ap()
    wq_in = nc.dram_tensor("wq_in", [D, D], F32, kind="ExternalInput").ap()
    wk_in = nc.dram_tensor("wk_in", [D, D], F32, kind="ExternalInput").ap()
    wv_in = nc.dram_tensor("wv_in", [D, D], F32, kind="ExternalInput").ap()
    wout_in = nc.dram_tensor("wout_in", [D, D], F32, kind="ExternalInput").ap()
    wbeta_in = nc.dram_tensor("wbeta_in", [D, H], F32, kind="ExternalInput").ap()
    bbeta_in = nc.dram_tensor("bbeta_in", [1, H], F32, kind="ExternalInput").ap()
    wr_in = nc.dram_tensor("wr_in", [128, 64], F32, kind="ExternalInput").ap()
    br_in = nc.dram_tensor("br_in", [1, E], F32, kind="ExternalInput").ap()
    wi_in = nc.dram_tensor("wi_in", [E, D, FF], F32, kind="ExternalInput").ap()
    wg_in = nc.dram_tensor("wg_in", [E, D, FF], F32, kind="ExternalInput").ap()
    wo_in = nc.dram_tensor("wo_in", [E, FF, D], F32, kind="ExternalInput").ap()
    bi_in = nc.dram_tensor("bi_in", [128, E * 32], F32, kind="ExternalInput").ap()
    bg_in = nc.dram_tensor("bg_in", [128, E * 32], F32, kind="ExternalInput").ap()
    bo_in = nc.dram_tensor("bo_in", [1, E * D], F32, kind="ExternalInput").ap()
    gam_in = nc.dram_tensor("gam_in", [1, E * D], F32, kind="ExternalInput").ap()
    bet_in = nc.dram_tensor("bet_in", [1, E * D], F32, kind="ExternalInput").ap()
    ident_in = nc.dram_tensor("ident_in", [128, 128], BF16, kind="ExternalInput").ap()
    ident32_in = nc.dram_tensor("ident32_in", [128, 128], F32, kind="ExternalInput").ap()
    ones32_in = nc.dram_tensor("ones32_in", [1, 128], F32, kind="ExternalInput").ap()
    mtril_in = nc.dram_tensor("mtril_in", [128, 128], F32, kind="ExternalInput").ap()
    mtriu_in = nc.dram_tensor("mtriu_in", [128, 128], F32, kind="ExternalInput").ap()
    ones1_in = nc.dram_tensor("ones1_in", [1, 128], BF16, kind="ExternalInput").ap()

    out = nc.dram_tensor("out", [QTOK, D], F32, kind="ExternalOutput").ap()

    q_scr = nc.dram_tensor("q_scr", [T, D], BF16, kind="Internal").ap()
    k_scr = nc.dram_tensor("k_scr", [T, D], BF16, kind="Internal").ap()
    v_scr = nc.dram_tensor("v_scr", [T, D], BF16, kind="Internal").ap()
    oq = [nc.dram_tensor(f"oq{i}", [QTOK, D], BF16, kind="Internal").ap() for i in range(4)]
    h_scr = nc.dram_tensor("h_scr", [4, 32, 128, 512], BF16, kind="Internal").ap()
    s_scr = nc.dram_tensor("s_scr", [QTOK, D], BF16, kind="Internal").ap()
    wo_bscr = nc.dram_tensor("wo_bscr", [E, FF, D], BF16, kind="Internal").ap()

    with ExitStack() as ctx:
        def sbt(name, shape, dt):
            return ctx.enter_context(nc.sbuf_tensor(name, shape, dt))

        def pst(name, shape, dt):
            return ctx.enter_context(nc.psum_tensor(name, shape, dt))

        sid = sbt("sid", [128, 128], BF16)
        sid32 = sbt("sid32", [128, 128], F32)
        ones32 = sbt("ones32", [1, 128], F32)
        mtril = sbt("mtril", [128, 128], F32)
        mtriu = sbt("mtriu", [128, 128], F32)
        ones1 = sbt("ones1", [1, 128], BF16)
        wbeta_bf = sbt("wbeta_bf", [128, H * 8], BF16)
        wr_f = sbt("wr_f", [128, 64], F32)
        br_f = sbt("br_f", [1, E], F32)
        bb_bf = sbt("bb_bf", [1, H], BF16)
        bi_sb = sbt("bi_sb", [128, E * 32], F32)
        bg_sb = sbt("bg_sb", [128, E * 32], F32)
        grow_row = sbt("grow_row", [1, 3072], F32)
        eps_sb = sbt("eps_sb", [128, 2], F32)
        bo_row = sbt("bo_row", [1, D], BF16)
        qsel_sb = sbt("qsel_sb", [128, 4], F32)
        beta_all = sbt("beta_all", [128, NT * H], F32)
        c_sb = sbt("c_sb", [128, QNT * E], F32)
        small = sbt("small", [128, 64], F32)
        xnt_res = sbt("xnt_res", [128, 8 * QTOK], BF16)
        f_res = sbt("f_res", [128, QNT * D], BF16)
        grep_bf = sbt("grep_bf", [128, D], BF16)
        brep_bf = sbt("brep_bf", [128, D], BF16)

        ar_f = sbt("ar_f", [128, 8192], F32)
        ar_bf = sbt("ar_bf", [128, 36864], BF16)

        wstage = ar_f[:, 0:8192]
        xs = [ar_f[:, i * 1024:(i + 1) * 1024] for i in range(2)]
        scr32 = ar_f[:, 2048:3072]
        scr32b = ar_f[:, 3072:4096]
        s_f32 = ar_f[0:64, 4096:5120]
        nbeta_f = ar_f[:, 5120:5136]
        wst_wo = [ar_f[:, 4096 + i * 2048: 4096 + (i + 1) * 2048] for i in range(2)]
        r0sp = [ar_f[:, i * 512:(i + 1) * 512] for i in range(16)]

        wq_bf = ar_bf[:, 0:8192]
        wk_bf = ar_bf[:, 8192:16384]
        wv_bf = ar_bf[:, 16384:24576]
        xn_bf = ar_bf[:, 24576:25600]
        xqt_bf = ar_bf[:, 25600:26624]
        kn_bf = ar_bf[:, 26624:27648]
        qe_bf = ar_bf[:, 27648:28672]
        ve_bf = ar_bf[:, 28672:29696]
        sst_bf = ar_bf[:, 29696:30720]
        wob_st = [ar_bf[:, 30720 + i * 2048: 30720 + (i + 1) * 2048] for i in range(2)]
        ktm = [ar_bf[:, i * 1024:(i + 1) * 1024] for i in range(2)]
        qtm = [ar_bf[:, 2048 + i * 1024: 2048 + (i + 1) * 1024] for i in range(2)]
        vtm = [ar_bf[:, 4096 + i * 1024: 4096 + (i + 1) * 1024] for i in range(2)]
        kt_sb = ar_bf[:, 6144:6400]
        qt_sb = ar_bf[:, 6400:6656]
        gm_sb = ar_bf[:, 6656:7168]
        a_sb = [ar_bf[:, 7168 + p * 512: 7168 + (p + 1) * 512] for p in range(2)]
        b_sb = [ar_bf[:, 8192 + p * 512: 8192 + (p + 1) * 512] for p in range(2)]
        x_sb = [ar_bf[:, 9216 + p * 256: 9216 + (p + 1) * 256] for p in range(2)]
        y_sb = ar_bf[:, 9728:9984]
        u_sb = ar_bf[:, 9984:10240]
        s_bf = ar_bf[:, 10240:10752]
        o_sb = [ar_bf[:, 10752 + i * 1024: 10752 + (i + 1) * 1024] for i in range(2)]
        wout_bf = ar_bf[:, 0:8192]
        tp4 = ar_bf[:, 8192:16384]
        oT = ar_bf[:, 16384:32768]
        sst25 = ar_bf[:, 8192:9216]
        wi_bf = ar_bf[:, 0:8192]
        wg_bf = ar_bf[:, 8192:16384]
        slab = ar_bf[:, 16384:32768]
        wo_half = ar_bf[:, 0:16384]
        hg_bf = ar_bf[:, 32768:33280]
        hs_bf = ar_bf[:, 33280:33792]
        h1_bf = [ar_bf[:, 33792 + i * 512: 33792 + (i + 1) * 512] for i in range(2)]
        yc_bf = ar_bf[:, 34816:35840]
        t2_bf = ar_bf[:, 35840:36864]
        p4_s = ar_bf[:, 0:1024]
        p4_g = ar_bf[:, 1024:2048]
        p4_p = ar_bf[:, 2048:3072]

        pA = pst("pA", [128, 512], F32)
        pB = pst("pB", [128, 512], F32)
        pE = pst("pE", [128, 512], F32)
        pF = pst("pF", [128, 512], F32)
        pX = pst("pX", [128, 512], F32)
        pS = pst("pS", [64, 1024], F32)
        pCb = pst("pCb", [128, 1024], BF16)

        sems = {}
        for name in ("pe", "ac", "dv", "po", "sy", "gp"):
            sems[name] = ctx.enter_context(nc.semaphore(name))
        block = ctx.enter_context(nc.Block())

        wo_rounds = [(e_i, jp) for e_i in range(E) for jp in range(16)]

        def walk(em: Em):
            for dst, src in ((sid, ident_in), (sid32, ident32_in), (ones32, ones32_in),
                             (mtril, mtril_in), (mtriu, mtriu_in), (ones1, ones1_in),
                             (wr_f, wr_in), (br_f, br_in), (bi_sb, bi_in), (bg_sb, bg_in),
                             (qsel_sb, qsel)):
                em("sy", lambda e, d=dst, s=src: e.dma_start(out=d[:], in_=s))
            em("dv", lambda e: e.memset(eps_sb[:, 0:1], LN_EPS))
            em("dv", lambda e: e.memset(eps_sb[:, 1:2], FFN_EPS))
            syn = em("sy", lambda e: e.dma_start(out=wstage[0:1, 0:H], in_=bbeta_in))
            dvn = em("dv", lambda e: e.tensor_copy(out=bb_bf[:], in_=wstage[0:1, 0:H]),
                     {"sy": syn})
            syn = em("sy", lambda e: e.dma_start(
                out=wstage[:, 0:H * 8].rearrange("p (j h) -> p j h", h=H),
                in_=wbeta_in.rearrange("(j p) h -> p j h", p=128)), {"dv": dvn})
            dvn = em("dv", lambda e: e.tensor_copy(out=wbeta_bf[:], in_=wstage[:, 0:H * 8]),
                     {"sy": syn})
            prev = {"dv": dvn, "ac": 0}
            for win, wbf in ((wq_in, wq_bf), (wk_in, wk_bf), (wv_in, wv_bf)):
                syn = em("sy", lambda e, w=win: e.dma_start(
                    out=wstage[:].rearrange("p (j n) -> p j n", n=1024),
                    in_=w.rearrange("(j p) n -> p j n", p=128)), dict(prev))
                dvh = em("dv", lambda e, wb=wbf: e.tensor_copy(out=wb[:, 0:4096],
                                                               in_=wstage[:, 0:4096]),
                         {"sy": syn})
                ach = em("ac", lambda e, wb=wbf: e.copy(out=wb[:, 4096:8192],
                                                        in_=wstage[:, 4096:8192]),
                         {"sy": syn})
                prev = {"dv": dvh, "ac": ach}

            busy = {0: {"dv": prev["dv"], "ac": prev["ac"]},
                    1: {"dv": prev["dv"], "ac": prev["ac"]}}
            wo_gp = {"ld": 0, "st": 0, "dv": prev["dv"], "ac": prev["ac"]}
            wo_round_idx = [0]

            def wo_precast_round():
                if wo_round_idx[0] >= len(wo_rounds):
                    return
                ri = wo_round_idx[0]
                wo_round_idx[0] += 1
                e_i, jp = wo_rounds[ri]
                par = ri % 2
                gpl = em("gp", lambda e: e.dma_start(
                    out=wst_wo[par].rearrange("p (j n) -> p j n", n=1024),
                    in_=wo_in[e_i].rearrange("(j p) n -> p j n", p=128)[:, jp * 2:jp * 2 + 2, :]),
                    {"dv": wo_gp["dv"], "ac": wo_gp["ac"]})
                dvc = em("dv", lambda e: e.tensor_copy(
                    out=wob_st[par][:, 0:1024], in_=wst_wo[par][:, 0:1024]),
                    {"gp": gpl})
                acc = em("ac", lambda e: e.copy(
                    out=wob_st[par][:, 1024:2048], in_=wst_wo[par][:, 1024:2048]),
                    {"gp": gpl})
                gps = em("gp", lambda e: e.dma_start(
                    out=wo_bscr[e_i].rearrange("(j p) n -> p j n", p=128)[:, jp * 2:jp * 2 + 2, :],
                    in_=wob_st[par].rearrange("p (j n) -> p j n", n=1024)),
                    {"dv": dvc, "ac": acc})
                if par == 1:
                    wo_gp["dv"] = dvc
                    wo_gp["ac"] = acc
                    wo_gp["st"] = gps

            def ln_tile(src, row, par, dst_bf):
                w = dict(busy[par])
                syn = em("sy", lambda e: e.dma_start(out=xs[par][:], in_=src[row:row + 128, :]),
                         w if w else None)
                dv1 = em("dv", lambda e: e.reduce_sum(out=small[:, 0:1], in_=xs[par][:],
                                                      axis=AX.X),
                         {"sy": syn, "dv": em.cnt["dv"], "ac": em.cnt["ac"]})
                ac1 = em("ac", lambda e: e.activation(out=scr32[:], in_=xs[par][:],
                                                      func=AF.Square, accum_out=small[:, 1:2]),
                         {"sy": syn, "dv": dv1})
                dv2 = em("dv", lambda e: e.tensor_scalar(out=small[:, 2:3], in0=small[:, 0:1],
                                                         scalar1=1.0 / D, scalar2=None,
                                                         op0=OP.mult), {"dv": dv1, "ac": ac1})
                dv3 = em("dv", lambda e: e.tensor_mul(out=small[:, 3:4], in0=small[:, 2:3],
                                                      in1=small[:, 2:3]), {"dv": dv2})
                dv4 = em("dv", lambda e: e.scalar_tensor_tensor(
                    out=small[:, 4:5], in0=small[:, 1:2], scalar=1.0 / D, in1=small[:, 3:4],
                    op0=OP.mult, op1=OP.subtract), {"dv": dv3})
                ac2 = em("ac", lambda e: e.activation(out=small[:, 5:6], in_=small[:, 4:5],
                                                      func=AF.Sqrt, bias=eps_sb[:, 0:1]), {"dv": dv4})
                dv5 = em("dv", lambda e: e.reciprocal(out=small[:, 6:7], in_=small[:, 5:6]),
                         {"ac": ac2})
                dv6 = em("dv", lambda e: e.tensor_scalar(out=small[:, 7:8], in0=small[:, 2:3],
                                                         scalar1=small[:, 6:7], scalar2=-1.0,
                                                         op0=OP.mult, op1=OP.mult), {"dv": dv5})
                ac3 = em("ac", lambda e: e.activation(out=dst_bf, in_=xs[par][:], func=AF.Identity,
                                                      bias=small[:, 7:8], scale=small[:, 6:7]),
                         {"dv": dv6})
                return syn, ac3

            def transpose8(src_bf, dsts, f32=False):
                last = 0
                pbank = pA if f32 else pCb
                ii = sid32 if f32 else sid
                for halfi in range(2):
                    w_ac = em.cnt["ac"]
                    for j in range(4):
                        jj = halfi * 4 + j
                        em("pe", lambda e, j=j, jj=jj: e.transpose(
                            pbank[:, j * 128:(j + 1) * 128],
                            src_bf[:, jj * 128:(jj + 1) * 128], ii[:]),
                            {"ac": w_ac, "dv": em.cnt["dv"]} if j == 0 else None)
                    pen = em.cnt["pe"]
                    for j in range(4):
                        jj = halfi * 4 + j
                        last = em("ac", lambda e, j=j, jj=jj: e.copy(
                            out=dsts[jj], in_=pbank[:, j * 128:(j + 1) * 128]),
                            {"pe": pen} if j == 0 else None)
                return last

            st_qkv = 0
            for tt in range(NT):
                row = tt * 128
                par = tt % 2
                syn, ac_ln = ln_tile(xb, row, par, xn_bf)
                ac_tr = transpose8(xn_bf, [xqt_bf[:, j * 128:(j + 1) * 128] for j in range(8)])

                for j in range(8):
                    em("pe", lambda e, j=j: e.matmul(
                        pX[:, 0:H], lhsT=xqt_bf[:, j * 128:(j + 1) * 128],
                        rhs=wbeta_bf[:, j * H:(j + 1) * H],
                        start=(j == 0), stop=False),
                        {"ac": ac_tr} if j == 0 else None)
                em("pe", lambda e: e.matmul(pX[:, 0:H], lhsT=ones1[:], rhs=bb_bf[:],
                                            start=False, stop=True))
                em("ac", lambda e, tt=tt: e.activation(
                    out=beta_all[:, tt * H:(tt + 1) * H], in_=pX[:, 0:H], func=AF.Sigmoid),
                    {"pe": em.cnt["pe"]})

                sy0 = em.cnt["sy"]
                for wbf, dst, scrname in ((wq_bf, qe_bf, q_scr), (wv_bf, ve_bf, v_scr)):
                    for half in range(2):
                        w_ac = em.cnt["ac"]
                        for j in range(8):
                            em("pe", lambda e, j=j, half=half, wbf=wbf: e.matmul(
                                pA[:, 0:512], lhsT=xqt_bf[:, j * 128:(j + 1) * 128],
                                rhs=wbf[:, j * 1024 + half * 512: j * 1024 + half * 512 + 512],
                                start=(j == 0), stop=(j == 7)),
                                {"ac": w_ac} if j == 0 else None)
                        em("ac", lambda e, half=half, dst=dst: e.copy(
                            out=dst[:, half * 512:(half + 1) * 512], in_=pA[:]),
                            {"pe": em.cnt["pe"], "sy": sy0})
                    em("sy", lambda e, dst=dst, scrname=scrname, row=row: e.dma_start(
                        out=scrname[row:row + 128, :], in_=dst), {"ac": em.cnt["ac"]})
                for half in range(2):
                    kb = pB if half == 0 else pE
                    for j in range(8):
                        em("pe", lambda e, j=j, half=half, kb=kb: e.matmul(
                            kb[:, 0:512], lhsT=xqt_bf[:, j * 128:(j + 1) * 128],
                            rhs=wk_bf[:, j * 1024 + half * 512: j * 1024 + half * 512 + 512],
                            start=(j == 0), stop=(j == 7)),
                            {"ac": ac_tr, "dv": em.cnt["dv"]} if j == 0 else None)
                pe_k = em.cnt["pe"]
                ac_k1 = em("ac", lambda e: e.activation(out=scr32b[:, 0:512], in_=pB[:],
                                                        func=AF.Square),
                           {"pe": pe_k, "dv": em.cnt["dv"]})
                ac_k2 = em("ac", lambda e: e.activation(out=scr32b[:, 512:1024], in_=pE[:],
                                                        func=AF.Square), {"ac": ac_k1})
                dvk = em("dv", lambda e: e.reduce_sum(
                    out=small[:, 16:32], in_=scr32b[:].rearrange("p (h d) -> p h d", d=64),
                    axis=AX.X), {"ac": ac_k2, "dv": em.cnt["dv"]})
                dvk = em("dv", lambda e: e.tensor_single_scalar(
                    out=small[:, 32:48], in_=small[:, 16:32], scalar=K_EPS * K_EPS, op=OP.max),
                    {"dv": dvk})
                ack = em("ac", lambda e: e.activation(out=small[:, 48:64], in_=small[:, 32:48],
                                                      func=AF.Sqrt), {"dv": dvk})
                dvk = em("dv", lambda e: e.reciprocal(out=small[:, 16:32], in_=small[:, 48:64]),
                         {"ac": ack})
                for hh in range(H):
                    kb = pB if hh < 8 else pE
                    off = (hh % 8) * 64
                    dvk = em("dv", lambda e, hh=hh, kb=kb, off=off: e.tensor_scalar(
                        out=kn_bf[:, hh * 64:(hh + 1) * 64], in0=kb[:, off:off + 64],
                        scalar1=small[:, 16 + hh:17 + hh], scalar2=None, op0=OP.mult),
                        {"dv": dvk, "sy": sy0} if hh == 0 else None)
                st_qkv = em("sy", lambda e, row=row: e.dma_start(
                    out=k_scr[row:row + 128, :], in_=kn_bf), {"dv": dvk})
                wo_precast_round()
                wo_precast_round()
                busy[par] = {"pe": em.cnt["pe"], "dv": em.cnt["dv"], "ac": em.cnt["ac"]}

            for ltt in range(QNT):
                row = ltt * 128
                par = ltt % 2
                syn, ac_ln = ln_tile(xq, row, par, xn_bf)
                ac_tr = transpose8(
                    xn_bf, [xnt_res[:, j * QTOK + ltt * 128: j * QTOK + ltt * 128 + 128]
                            for j in range(8)])
                dvf = em("dv", lambda e: e.tensor_scalar(
                    out=scr32[:], in0=xs[par][:], scalar1=small[:, 2:3], scalar2=small[:, 6:7],
                    op0=OP.subtract, op1=OP.mult), {"dv": em.cnt["dv"], "ac": ac_ln})
                ac_t32 = transpose8(scr32, [scr32b[:, j * 128:(j + 1) * 128] for j in range(8)],
                                    f32=True)
                for j in range(8):
                    em("pe", lambda e, j=j: e.matmul(
                        pX[:, 0:E], lhsT=scr32b[:, j * 128:(j + 1) * 128],
                        rhs=wr_f[:, j * E:(j + 1) * E],
                        start=(j == 0), stop=False), {"ac": ac_t32} if j == 0 else None)
                em("pe", lambda e: e.matmul(pX[:, 0:E], lhsT=ones32[:], rhs=br_f[:],
                                            start=False, stop=True))
                ac_lg = em("ac", lambda e: e.copy(out=small[:, 8:16], in_=pX[:, 0:E]),
                           {"pe": em.cnt["pe"]})
                d1 = em("dv", lambda e: e.reduce_max(out=small[:, 16:17], in_=small[:, 8:16],
                                                     axis=AX.X), {"ac": ac_lg, "dv": em.cnt["dv"]})
                d2 = em("dv", lambda e: e.tensor_scalar(out=small[:, 17:18], in0=small[:, 16:17],
                                                        scalar1=-1.0, scalar2=None, op0=OP.mult),
                        {"dv": d1})
                d3 = em("dv", lambda e: e.tensor_scalar(out=small[:, 24:32], in0=small[:, 8:16],
                                                        scalar1=small[:, 16:17], scalar2=None,
                                                        op0=OP.is_equal), {"dv": d2})
                d4 = em("dv", lambda e: e.scalar_tensor_tensor(
                    out=small[:, 32:40], in0=small[:, 24:32], scalar=-1e30, in1=small[:, 8:16],
                    op0=OP.mult, op1=OP.add), {"dv": d3})
                d5 = em("dv", lambda e: e.reduce_max(out=small[:, 18:19], in_=small[:, 32:40],
                                                     axis=AX.X), {"dv": d4})
                d6 = em("dv", lambda e: e.tensor_scalar(out=small[:, 40:48], in0=small[:, 8:16],
                                                        scalar1=small[:, 18:19], scalar2=None,
                                                        op0=OP.is_ge), {"dv": d5})
                a7 = em("ac", lambda e: e.activation(out=small[:, 48:56], in_=small[:, 8:16],
                                                     func=AF.Exp, bias=small[:, 17:18]),
                        {"dv": d6})
                d8 = em("dv", lambda e: e.tensor_mul(out=small[:, 56:64], in0=small[:, 48:56],
                                                     in1=small[:, 40:48]), {"ac": a7})
                d9 = em("dv", lambda e: e.reduce_sum(out=small[:, 19:20], in_=small[:, 56:64],
                                                     axis=AX.X), {"dv": d8})
                d10 = em("dv", lambda e: e.reciprocal(out=small[:, 20:21], in_=small[:, 19:20]),
                         {"dv": d9})
                em("dv", lambda e, ltt=ltt: e.tensor_scalar(
                    out=c_sb[:, ltt * E:(ltt + 1) * E], in0=small[:, 56:64],
                    scalar1=small[:, 20:21], scalar2=None, op0=OP.mult), {"dv": d10})
                wo_precast_round()
                wo_precast_round()
                busy[par] = {"pe": em.cnt["pe"], "dv": em.cnt["dv"], "ac": em.cnt["ac"]}
            while wo_round_idx[0] < len(wo_rounds):
                wo_precast_round()

            dv_sdone = em("dv", lambda e: e.memset(s_f32, 0.0),
                          {"ac": em.cnt["ac"], "pe": em.cnt["pe"]})
            sready = 0

            def asym_slot(h):
                return (pA if h % 2 == 0 else pB)[:, (h // 2) * 128:(h // 2 + 1) * 128]

            def g_slot(h):
                return (pA if h % 2 == 0 else pB)[:, 256 + (h // 2) * 128: 256 + (h // 2 + 1) * 128]

            def ks_slot(h):
                return (pA if h % 2 == 0 else pB)[:, (h // 2) * 64:(h // 2 + 1) * 64]

            def o_slot(h):
                return (pA if h % 2 == 0 else pB)[:, 128 + (h // 2) * 64: 128 + (h // 2 + 1) * 64]

            def gm_slot(h):
                return gm_sb[:, (h % 2) * 256 + (h // 2) * 128:(h % 2) * 256 + (h // 2 + 1) * 128]

            chunk_done = {0: {}, 1: {}}
            first_w = {"sy": st_qkv, "dv": em.cnt["dv"], "ac": em.cnt["ac"],
                       "pe": em.cnt["pe"]}
            for g in range(NCH):
                row = g * C
                par = g % 2
                w0 = dict(chunk_done[par]) if g >= 2 else dict(first_w)
                syn = em("sy", lambda e, row=row, par=par: e.dma_start(
                    out=ktm[par][:], in_=k_scr[row:row + C, :]), w0)
                em("sy", lambda e, row=row, par=par: e.dma_start(
                    out=qtm[par][:], in_=q_scr[row:row + C, :]))
                ldn = em("sy", lambda e, row=row, par=par: e.dma_start(
                    out=vtm[par][:], in_=v_scr[row:row + C, :]))
                em("dv", lambda e, g=g: e.tensor_scalar(
                    out=nbeta_f, in0=beta_all[:, g * H:(g + 1) * H], scalar1=-1.0,
                    scalar2=None, op0=OP.mult), {"dv": em.cnt["dv"]})

                for sg in range(4):
                    h0 = sg * 4
                    fo = h0 * Dh
                    ac_before = em.cnt["ac"]
                    for bb in range(2):
                        em("pe", lambda e, bb=bb, fo=fo, par=par: e.transpose(
                            pCb[:, bb * 128:(bb + 1) * 128],
                            ktm[par][:, fo + bb * 128: fo + (bb + 1) * 128], sid[:]),
                            {"sy": ldn, "ac": ac_before} if bb == 0 else None)
                    ac_kt = em("ac", lambda e: e.copy(out=kt_sb, in_=pCb[:, 0:256]),
                               {"pe": em.cnt["pe"]})
                    for bb in range(2):
                        em("pe", lambda e, bb=bb, fo=fo, par=par: e.transpose(
                            pCb[:, 256 + bb * 128: 256 + (bb + 1) * 128],
                            qtm[par][:, fo + bb * 128: fo + (bb + 1) * 128], sid[:]),
                            {"ac": ac_kt} if bb == 0 else None)
                    ac_qt = em("ac", lambda e: e.copy(out=qt_sb, in_=pCb[:, 256:512]),
                               {"pe": em.cnt["pe"]})

                    for h in range(4):
                        part = (h % 2) * 64
                        bb = h // 2
                        em("pe", lambda e, h=h, part=part, bb=bb: e.matmul(
                            asym_slot(h),
                            lhsT=kt_sb[part:part + 64, bb * 128:(bb + 1) * 128],
                            rhs=kt_sb[part:part + 64, bb * 128:(bb + 1) * 128],
                            start=(h < 2), stop=False),
                            {"ac": ac_qt, "dv": em.cnt["dv"]} if h == 0 else None)
                    for h in range(4):
                        part = (h % 2) * 64
                        bb = h // 2
                        em("pe", lambda e, h=h, part=part, bb=bb: e.matmul(
                            g_slot(h),
                            lhsT=kt_sb[part:part + 64, bb * 128:(bb + 1) * 128],
                            rhs=qt_sb[part:part + 64, bb * 128:(bb + 1) * 128],
                            start=False, stop=(h >= 2)))
                    pe_ag = em.cnt["pe"]

                    dv_a0 = em.cnt["dv"]
                    for h in range(4):
                        dv_a0 = em("dv", lambda e, h=h, g=g, h0=h0: e.scalar_tensor_tensor(
                            out=a_sb[0][:, h * 128:(h + 1) * 128],
                            in0=asym_slot(h),
                            scalar=beta_all[:, g * H + h0 + h: g * H + h0 + h + 1],
                            in1=mtril[:],
                            op0=OP.mult, op1=OP.mult),
                            {"pe": pe_ag, "dv": dv_a0} if h == 0 else None)
                    dv_gm = dv_a0
                    for h in range(4):
                        dv_gm = em("dv", lambda e, h=h: e.tensor_tensor(
                            out=gm_slot(h), in0=g_slot(h), in1=mtriu[:], op=OP.mult),
                            {"pe": pe_ag, "dv": dv_gm} if h == 0 else None)

                    for h in range(4):
                        em("pe", lambda e, h=h: e.transpose(
                            pCb[:, h * 128:(h + 1) * 128],
                            a_sb[0][:, h * 128:(h + 1) * 128], sid[:]),
                            {"dv": dv_a0, "ac": ac_qt} if h == 0 else None)
                    ac_b0 = em("ac", lambda e: e.copy(out=b_sb[0], in_=pCb[:, 0:512]),
                               {"pe": em.cnt["pe"]})

                    if g > 0:
                        for h in range(4):
                            part = (h % 2) * 64
                            bb = h // 2
                            em("pe", lambda e, h=h, part=part, bb=bb, h0=h0: e.matmul(
                                ks_slot(h),
                                lhsT=kt_sb[part:part + 64, bb * 128:(bb + 1) * 128],
                                rhs=s_bf[part:part + 64,
                                         (h0 + h) // 2 * 64: ((h0 + h) // 2 + 1) * 64],
                                start=(h < 2), stop=(h >= 2)),
                                {"dv": dv_gm, "ac": sready} if h == 0 else None)
                    pe_ks = em.cnt["pe"]

                    dv_y = dv_gm
                    for h in range(4):
                        dv_y = em("dv", lambda e, h=h, fo=fo, par=par, g=g, h0=h0: e.tensor_scalar(
                            out=y_sb[:, h * 64:(h + 1) * 64],
                            in0=vtm[par][:, fo + h * 64: fo + (h + 1) * 64],
                            scalar1=beta_all[:, g * H + h0 + h: g * H + h0 + h + 1],
                            scalar2=None, op0=OP.mult),
                            {"dv": dv_y, "sy": ldn} if h == 0 else None)
                    if g > 0:
                        for h in range(4):
                            dv_y = em("dv", lambda e, h=h, h0=h0: e.scalar_tensor_tensor(
                                out=y_sb[:, h * 64:(h + 1) * 64],
                                in0=ks_slot(h),
                                scalar=nbeta_f[:, h0 + h: h0 + h + 1],
                                in1=y_sb[:, h * 64:(h + 1) * 64],
                                op0=OP.mult, op1=OP.add),
                                {"dv": dv_y, "pe": pe_ks} if h == 0 else None)

                    ac_ab = ac_b0
                    ac_x = 0
                    xcur = y_sb
                    for k in range(7):
                        p = k % 2
                        xdst = u_sb if k == 6 else x_sb[k % 2]
                        for h in range(4):
                            em("pe", lambda e, h=h, xcur=xcur: e.matmul(
                                pX[:, h * 64:(h + 1) * 64], lhsT=sid[:],
                                rhs=xcur[:, h * 64:(h + 1) * 64],
                                start=(h == 0), stop=False),
                                {"dv": dv_y, "ac": max(ac_x, ac_ab)} if h == 0 else None)
                        for h in range(4):
                            em("pe", lambda e, h=h, p=p, xcur=xcur: e.matmul(
                                pX[:, h * 64:(h + 1) * 64],
                                lhsT=b_sb[p][:, h * 128:(h + 1) * 128],
                                rhs=xcur[:, h * 64:(h + 1) * 64],
                                start=False, stop=(h == 3)))
                        ac_x = em("ac", lambda e, xdst=xdst: e.copy(out=xdst, in_=pX[:, 0:256]),
                                  {"pe": em.cnt["pe"]})
                        xcur = xdst
                        if k < 6:
                            if k < 5:
                                for h in range(4):
                                    em("pe", lambda e, h=h, p=p: e.matmul(
                                        pE[:, h * 128:(h + 1) * 128],
                                        lhsT=b_sb[p][:, h * 128:(h + 1) * 128],
                                        rhs=a_sb[p][:, h * 128:(h + 1) * 128],
                                        start=(h == 0), stop=(h == 3)),
                                        {"ac": ac_ab} if h == 0 else None)
                            for h in range(4):
                                em("pe", lambda e, h=h, p=p: e.matmul(
                                    pF[:, h * 128:(h + 1) * 128],
                                    lhsT=a_sb[p][:, h * 128:(h + 1) * 128],
                                    rhs=b_sb[p][:, h * 128:(h + 1) * 128],
                                    start=(h == 0), stop=(h == 3)),
                                    {"ac": ac_ab} if h == 0 else None)
                            pe_sq = em.cnt["pe"]
                            if k < 5:
                                em("ac", lambda e, p=p: e.copy(out=a_sb[1 - p], in_=pE[:]),
                                   {"pe": pe_sq})
                            ac_ab = em("ac", lambda e, p=p: e.copy(out=b_sb[1 - p], in_=pF[:]),
                                       {"pe": pe_sq})
                    ac_u = ac_x

                    for h in range(4):
                        em("pe", lambda e, h=h, g=g: e.matmul(
                            o_slot(h),
                            lhsT=gm_slot(h),
                            rhs=u_sb[:, h * 64:(h + 1) * 64],
                            start=(h < 2), stop=(g == 0 and h >= 2)),
                            {"ac": max(ac_u, sready)} if h == 0 else None)
                    if g > 0:
                        for h in range(4):
                            part = (h % 2) * 64
                            bb = h // 2
                            em("pe", lambda e, h=h, part=part, bb=bb, h0=h0: e.matmul(
                                o_slot(h),
                                lhsT=qt_sb[part:part + 64, bb * 128:(bb + 1) * 128],
                                rhs=s_bf[part:part + 64,
                                         (h0 + h) // 2 * 64: ((h0 + h) // 2 + 1) * 64],
                                start=False, stop=(h >= 2)))
                    pe_o = em.cnt["pe"]
                    ac_o = 0
                    for h in range(4):
                        ac_o = em("ac", lambda e, h=h, fo=fo, par=par: e.copy(
                            out=o_sb[par][:, fo + h * 64: fo + (h + 1) * 64], in_=o_slot(h)),
                            {"pe": pe_o} if h == 0 else None)

                    for h in range(4):
                        em("pe", lambda e, h=h, fo=fo, par=par, h0=h0: e.matmul(
                            pS[0:64, fo + h * 64: fo + (h + 1) * 64],
                            lhsT=ktm[par][:, fo + h * 64: fo + (h + 1) * 64],
                            rhs=u_sb[:, h * 64:(h + 1) * 64],
                            start=((h0 + h) % 8 == 0),
                            stop=((h0 + h) % 8 == 7)),
                            {"ac": ac_o, "dv": dv_sdone} if h == 0 else None)

                dv_sdone = em("dv", lambda e: e.tensor_add(out=s_f32, in0=s_f32, in1=pS[:]),
                              {"pe": em.cnt["pe"], "ac": em.cnt["ac"]})
                sv = s_f32.rearrange("p (h d) -> p h d", d=64)
                em("ac", lambda e, sv=sv: e.copy(
                    out=s_bf[0:64, :].rearrange("p (h d) -> p h d", d=64),
                    in_=sv[:, 0:H:2, :]), {"dv": dv_sdone})
                sready = em("ac", lambda e, sv=sv: e.copy(
                    out=s_bf[64:128, :].rearrange("p (h d) -> p h d", d=64),
                    in_=sv[:, 1:H:2, :]), {"dv": dv_sdone})
                gp_o = em("gp", lambda e, g=g, par=par: e.dma_start(
                    out=oq[g // 16][(g % 16) * 128:(g % 16) * 128 + 128, :], in_=o_sb[par][:]),
                    {"ac": em.cnt["ac"]})
                chunk_done[par] = {"pe": em.cnt["pe"], "dv": em.cnt["dv"], "ac": em.cnt["ac"],
                                   "gp": gp_o, "sy": 0}
            gp_o_all = em.cnt["gp"]

            w_all = {"pe": em.cnt["pe"], "dv": em.cnt["dv"], "ac": em.cnt["ac"],
                     "gp": gp_o_all}
            syn = em("sy", lambda e: e.dma_start(
                out=wstage[:].rearrange("p (j n) -> p j n", n=1024),
                in_=wout_in.rearrange("(j p) n -> p j n", p=128)), w_all)
            dvw = em("dv", lambda e: e.tensor_copy(out=wout_bf[:, 0:4096],
                                                   in_=wstage[:, 0:4096]),
                     {"sy": syn, "dv": em.cnt["dv"], "ac": em.cnt["ac"]})
            acw = em("ac", lambda e: e.copy(out=wout_bf[:, 4096:8192],
                                            in_=wstage[:, 4096:8192]),
                     {"sy": syn, "ac": em.cnt["ac"]})
            dv_sel = dvw
            for j in range(8):
                sy_t = 0
                for i in range(4):
                    sy_t = em("sy", lambda e, i=i, j=j: e.dma_start_transpose(
                        out=tp4[:, i * 2048:(i + 1) * 2048],
                        in_=oq[i][:, j * 128:(j + 1) * 128]),
                        {"gp": gp_o_all, "dv": dv_sel} if i == 0 else None)
                dv_sel = em("dv", lambda e, j=j: e.tensor_scalar(
                    out=oT[:, j * 2048:(j + 1) * 2048], in0=tp4[:, 0:2048],
                    scalar1=qsel_sb[:, 0:1], scalar2=None, op0=OP.mult),
                    {"sy": sy_t, "dv": dv_sel})
                for i in range(1, 4):
                    dv_sel = em("dv", lambda e, i=i, j=j: e.scalar_tensor_tensor(
                        out=oT[:, j * 2048:(j + 1) * 2048], in0=tp4[:, i * 2048:(i + 1) * 2048],
                        scalar=qsel_sb[:, i:i + 1], in1=oT[:, j * 2048:(j + 1) * 2048],
                        op0=OP.mult, op1=OP.add), {"dv": dv_sel})
            sy_sprev = 0
            for ltt in range(QNT):
                for half in range(2):
                    for j in range(8):
                        em("pe", lambda e, j=j, half=half, ltt=ltt: e.matmul(
                            pA[:, 0:512],
                            lhsT=oT[:, j * 2048 + ltt * 128: j * 2048 + ltt * 128 + 128],
                            rhs=wout_bf[:, j * 1024 + half * 512: j * 1024 + half * 512 + 512],
                            start=(j == 0), stop=(j == 7)),
                            {"dv": dv_sel, "ac": max(acw, em.cnt["ac"])} if j == 0 else None)
                    em("ac", lambda e, half=half: e.copy(
                        out=sst25[:, half * 512:(half + 1) * 512], in_=pA[:]),
                        {"pe": em.cnt["pe"], "sy": sy_sprev})
                sy_sprev = em("sy", lambda e, ltt=ltt: e.dma_start(
                    out=s_scr[ltt * 128:(ltt + 1) * 128, :], in_=sst25), {"ac": em.cnt["ac"]})
            sy_sdone = em.cnt["sy"]

            dv_f0 = em("dv", lambda e: e.memset(f_res[:], 0.0),
                       {"ac": em.cnt["ac"], "gp": wo_gp["st"]})
            prev_exp = {"pe": em.cnt["pe"], "dv": dv_f0, "ac": em.cnt["ac"],
                        "gp": em.cnt["gp"], "sy": sy_sdone}
            gp_h = 0
            hpar = 0
            for e_i in range(E):
                wrow = dict(prev_exp)
                em("sy", lambda e, e_i=e_i: e.dma_start(
                    out=grow_row[0:1, 0:1024], in_=gam_in[0:1, e_i * D:(e_i + 1) * D]), wrow)
                em("sy", lambda e, e_i=e_i: e.dma_start(
                    out=grow_row[0:1, 1024:2048], in_=bet_in[0:1, e_i * D:(e_i + 1) * D]))
                sy_r = em("sy", lambda e, e_i=e_i: e.dma_start(
                    out=grow_row[0:1, 2048:3072], in_=bo_in[0:1, e_i * D:(e_i + 1) * D]))
                dv_bo = em("dv", lambda e: e.tensor_copy(
                    out=bo_row[:], in_=grow_row[0:1, 2048:3072]),
                    {"sy": sy_r, "dv": prev_exp["dv"]})
                for half in range(2):
                    em("pe", lambda e, half=half: e.matmul(
                        pE[:, 0:512], lhsT=ones32[:],
                        rhs=grow_row[0:1, half * 512: half * 512 + 512],
                        start=True, stop=True),
                        {"sy": sy_r, "pe": prev_exp["pe"], "ac": prev_exp["ac"],
                         "dv": prev_exp["dv"]} if half == 0 else {"ac": em.cnt["ac"]})
                    em("ac", lambda e, half=half: e.copy(
                        out=grep_bf[:, half * 512:(half + 1) * 512], in_=pE[:]),
                        {"pe": em.cnt["pe"]})
                    em("pe", lambda e, half=half: e.matmul(
                        pE[:, 0:512], lhsT=ones32[:],
                        rhs=grow_row[0:1, 1024 + half * 512: 1024 + half * 512 + 512],
                        start=True, stop=True), {"ac": em.cnt["ac"]})
                    em("ac", lambda e, half=half: e.copy(
                        out=brep_bf[:, half * 512:(half + 1) * 512], in_=pE[:]),
                        {"pe": em.cnt["pe"]})
                ac_gb = em.cnt["ac"]

                prev_cast = {"dv": em.cnt["dv"], "ac": em.cnt["ac"]}
                for fc in range(4):
                    casts = []
                    for gate in range(2):
                        win = wi_in if gate == 0 else wg_in
                        wbf = wi_bf if gate == 0 else wg_bf
                        syw = em("sy", lambda e, win=win, e_i=e_i, fc=fc: e.dma_start(
                            out=wstage[:].rearrange("p (j n) -> p j n", n=1024),
                            in_=win[e_i].rearrange("(j p) n -> p j n", p=128)
                            [:, :, fc * 1024:(fc + 1) * 1024]),
                            {"dv": prev_cast["dv"], "ac": prev_cast["ac"],
                             "pe": em.cnt["pe"]})
                        dvc = em("dv", lambda e, wbf=wbf: e.tensor_copy(
                            out=wbf[:, 0:4096], in_=wstage[:, 0:4096]), {"sy": syw})
                        acc = em("ac", lambda e, wbf=wbf: e.copy(
                            out=wbf[:, 4096:8192], in_=wstage[:, 4096:8192]), {"sy": syw})
                        prev_cast = {"dv": dvc, "ac": acc}
                        casts.append((dvc, acc))
                    (dv_wi, ac_wi), (dv_wg, ac_wg) = casts
                    for st in range(4):
                        for ffb in range(8):
                            for j in range(8):
                                em("pe", lambda e, j=j, ffb=ffb, st=st: e.matmul(
                                    pE[:, 0:512],
                                    lhsT=wi_bf[:, j * 1024 + ffb * 128: j * 1024 + (ffb + 1) * 128],
                                    rhs=xnt_res[:, j * QTOK + st * 512: j * QTOK + (st + 1) * 512],
                                    start=(j == 0), stop=(j == 7)),
                                    {"dv": dv_wi, "ac": max(ac_wi, em.cnt["ac"])} if j == 0 else None)
                            pe_hi = em.cnt["pe"]
                            for j in range(8):
                                em("pe", lambda e, j=j, ffb=ffb, st=st: e.matmul(
                                    pF[:, 0:512],
                                    lhsT=wg_bf[:, j * 1024 + ffb * 128: j * 1024 + (ffb + 1) * 128],
                                    rhs=xnt_res[:, j * QTOK + st * 512: j * QTOK + (st + 1) * 512],
                                    start=(j == 0), stop=(j == 7)),
                                    {"dv": dv_wg, "ac": ac_wg} if j == 0 else None)
                            pe_hg = em.cnt["pe"]
                            bcol = e_i * 32 + fc * 8 + ffb
                            a1 = em("ac", lambda e, bcol=bcol: e.activation(
                                out=hg_bf, in_=pE[:], func=AF.Gelu,
                                bias=bi_sb[:, bcol:bcol + 1]), {"pe": pe_hi, "dv": em.cnt["dv"]})
                            a2 = em("ac", lambda e, bcol=bcol: e.activation(
                                out=hs_bf, in_=pF[:], func=AF.Sigmoid,
                                bias=bg_sb[:, bcol:bcol + 1]), {"pe": pe_hg})
                            dvh = em("dv", lambda e, hpar=hpar: e.tensor_mul(
                                out=h1_bf[hpar], in0=hg_bf, in1=hs_bf),
                                {"ac": a2, "gp": gp_h, "dv": em.cnt["dv"]})
                            gp_h = em("gp", lambda e, st=st, fc=fc, ffb=ffb, hpar=hpar, e_i=e_i:
                                      e.dma_start(out=h_scr[st, fc * 8 + ffb], in_=h1_bf[hpar]),
                                      {"dv": dvh})
                            hpar = 1 - hpar

                epi_dv = em.cnt["dv"]
                epi_ac = em.cnt["ac"]
                for dh in range(2):
                    sy_wo = em("sy", lambda e, dh=dh, e_i=e_i: e.dma_start(
                        out=wo_half.rearrange("p (f n) -> p f n", n=512),
                        in_=wo_bscr[e_i].rearrange("(j p) n -> p j n", p=128)
                        [:, :, dh * 512:(dh + 1) * 512]),
                        {"pe": em.cnt["pe"], "dv": em.cnt["dv"], "ac": em.cnt["ac"],
                         "gp": wo_gp["st"]})
                    for st in range(4):
                        sy_sl = em("sy", lambda e, st=st: e.dma_start(
                            out=slab.rearrange("p (f n) -> p f n", n=512),
                            in_=h_scr[st].rearrange("f p n -> p f n")),
                            {"gp": gp_h, "pe": em.cnt["pe"]})
                        for t4 in range(4):
                            ltt = st * 4 + t4
                            for kq in range(4):
                                for j in range(8):
                                    fb = kq * 8 + j
                                    em("pe", lambda e, fb=fb, t4=t4, dh=dh: e.matmul(
                                        (pA if dh == 0 else pB)[:, 0:512],
                                        lhsT=slab[:, fb * 512 + t4 * 128: fb * 512 + t4 * 128 + 128],
                                        rhs=wo_half[:, fb * 512:(fb + 1) * 512],
                                        start=(kq == 0 and j == 0), stop=False),
                                        {"sy": max(sy_sl, sy_wo), "ac": epi_ac,
                                         "dv": epi_dv} if (kq == 0 and j == 0) else None)
                            for j4 in range(4):
                                em("pe", lambda e, j4=j4, dh=dh, ltt=ltt: e.matmul(
                                    (pA if dh == 0 else pB)[:, j4 * 128:(j4 + 1) * 128],
                                    lhsT=xnt_res[:, (dh * 4 + j4) * QTOK + ltt * 128:
                                                 (dh * 4 + j4) * QTOK + ltt * 128 + 128],
                                    rhs=sid[:], start=False, stop=False))
                            em("pe", lambda e, dh=dh: e.matmul(
                                (pA if dh == 0 else pB)[:, 0:512], lhsT=ones1[:],
                                rhs=bo_row[0:1, dh * 512: dh * 512 + 512],
                                start=False, stop=True))
                            pe_r = em.cnt["pe"]
                            if dh == 0:
                                epi_ac = em("ac", lambda e, ltt=ltt: e.copy(
                                    out=r0sp[ltt], in_=pA[:]), {"pe": pe_r})
                                epi_dv = em.cnt["dv"]
                            else:
                                ccol = ltt * E + e_i
                                d1 = em("dv", lambda e, ltt=ltt: e.reduce_sum(
                                    out=small[:, 0:1], in_=r0sp[ltt], axis=AX.X),
                                    {"pe": pe_r, "dv": em.cnt["dv"], "ac": em.cnt["ac"]})
                                d2 = em("dv", lambda e: e.reduce_sum(
                                    out=small[:, 1:2], in_=pB[:], axis=AX.X), {"dv": d1})
                                d3 = em("dv", lambda e: e.tensor_add(
                                    out=small[:, 2:3], in0=small[:, 0:1], in1=small[:, 1:2]),
                                    {"dv": d2})
                                a1 = em("ac", lambda e, ltt=ltt: e.activation(
                                    out=pE[:], in_=r0sp[ltt], func=AF.Square,
                                    accum_out=small[:, 3:4]), {"dv": d3, "ac": em.cnt["ac"]})
                                a2 = em("ac", lambda e: e.activation(
                                    out=pF[:], in_=pB[:], func=AF.Square,
                                    accum_out=small[:, 4:5]), {"ac": a1})
                                d4 = em("dv", lambda e: e.tensor_add(
                                    out=small[:, 5:6], in0=small[:, 3:4], in1=small[:, 4:5]),
                                    {"ac": a2, "dv": d3})
                                d5 = em("dv", lambda e: e.tensor_scalar(
                                    out=small[:, 6:7], in0=small[:, 2:3], scalar1=1.0 / D,
                                    scalar2=None, op0=OP.mult), {"dv": d4})
                                d6 = em("dv", lambda e: e.tensor_mul(
                                    out=small[:, 7:8], in0=small[:, 6:7], in1=small[:, 6:7]),
                                    {"dv": d5})
                                d7 = em("dv", lambda e: e.scalar_tensor_tensor(
                                    out=small[:, 8:9], in0=small[:, 5:6], scalar=1.0 / D,
                                    in1=small[:, 7:8], op0=OP.mult, op1=OP.subtract), {"dv": d6})
                                a3 = em("ac", lambda e: e.activation(
                                    out=small[:, 9:10], in_=small[:, 8:9], func=AF.Sqrt,
                                    bias=eps_sb[:, 1:2]), {"dv": d7})
                                d8 = em("dv", lambda e: e.reciprocal(
                                    out=small[:, 10:11], in_=small[:, 9:10]), {"ac": a3})
                                d9 = em("dv", lambda e, ccol=ccol: e.tensor_mul(
                                    out=small[:, 11:12], in0=small[:, 10:11],
                                    in1=c_sb[:, ccol:ccol + 1]), {"dv": d8})
                                d10 = em("dv", lambda e: e.tensor_scalar(
                                    out=small[:, 12:13], in0=small[:, 6:7],
                                    scalar1=small[:, 11:12], scalar2=-1.0,
                                    op0=OP.mult, op1=OP.mult), {"dv": d9})
                                a4 = em("ac", lambda e, ltt=ltt: e.activation(
                                    out=yc_bf[:, 0:512], in_=r0sp[ltt], func=AF.Identity,
                                    scale=small[:, 11:12], bias=small[:, 12:13]), {"dv": d10})
                                a5 = em("ac", lambda e: e.activation(
                                    out=yc_bf[:, 512:1024], in_=pB[:], func=AF.Identity,
                                    scale=small[:, 11:12], bias=small[:, 12:13]), {"ac": a4})
                                d11 = em("dv", lambda e: e.tensor_mul(
                                    out=t2_bf, in0=yc_bf, in1=grep_bf[:]), {"ac": a5})
                                d12 = em("dv", lambda e, ltt=ltt, ccol=ccol: e.scalar_tensor_tensor(
                                    out=f_res[:, ltt * D:(ltt + 1) * D], in0=brep_bf[:],
                                    scalar=c_sb[:, ccol:ccol + 1],
                                    in1=f_res[:, ltt * D:(ltt + 1) * D],
                                    op0=OP.mult, op1=OP.add),
                                    {"dv": d11, "po": em.cnt["po"]})
                                po1 = em("po", lambda e, ltt=ltt: e.tensor_add(
                                    out=f_res[:, ltt * D:(ltt + 1) * D],
                                    in0=f_res[:, ltt * D:(ltt + 1) * D], in1=t2_bf),
                                    {"dv": d12})
                                epi_ac = em.cnt["ac"]
                                epi_dv = em.cnt["dv"]
                prev_exp = {"pe": em.cnt["pe"], "dv": em.cnt["dv"], "ac": em.cnt["ac"],
                            "gp": em.cnt["gp"], "sy": em.cnt["sy"]}

            po_all = em.cnt["po"]
            prev_d3 = em.cnt["dv"]
            prev_ac = em.cnt["ac"]
            sy_out = 0
            for ltt in range(QNT):
                par = ltt % 2
                w = {"sy": max(sy_sdone, sy_out), "po": po_all, "dv": prev_d3, "ac": prev_ac}
                sy1 = em("sy", lambda e, ltt=ltt: e.dma_start(
                    out=p4_s, in_=s_scr[ltt * 128:(ltt + 1) * 128, :]), w)
                sy2 = em("sy", lambda e, ltt=ltt, par=par: e.dma_start(
                    out=xs[par][:], in_=xq[ltt * 128:(ltt + 1) * 128, :]))
                a1 = em("ac", lambda e, ltt=ltt: e.activation(
                    out=p4_g, in_=f_res[:, ltt * D:(ltt + 1) * D], func=AF.Silu),
                    {"po": po_all, "ac": prev_ac, "sy": sy2})
                d1 = em("dv", lambda e: e.tensor_mul(out=p4_p, in0=p4_s, in1=p4_g),
                        {"ac": a1, "sy": sy2, "dv": prev_d3})
                d2 = em("dv", lambda e: e.tensor_copy(out=scr32, in_=p4_p), {"dv": d1})
                d3 = em("dv", lambda e, par=par: e.tensor_add(
                    out=scr32b, in0=xs[par][:], in1=scr32), {"dv": d2})
                sy_out = em("sy", lambda e, ltt=ltt: e.dma_start(
                    out=out[ltt * 128:(ltt + 1) * 128, :], in_=scr32b), {"dv": d3})
                prev_d3 = d3
                prev_ac = a1
            em.wait("sy", {"sy": em.cnt["sy"], "gp": em.cnt["gp"]})

        @block.tensor
        def _(tensor):
            walk(Em("pe", tensor, sems))

        @block.scalar
        def _(scalar):
            walk(Em("ac", scalar, sems))

        @block.vector
        def _(vector):
            walk(Em("dv", vector, sems))

        @block.gpsimd
        def _(gpsimd):
            walk(Em("po", gpsimd, sems))

        @block.sync
        def _(sync):
            walk(Em("sy", sync, sems))

    return nc


_NC_CACHE = [None]


def kernel(L0_attn_x, L0_attn_Wq, L0_attn_Wk, L0_attn_Wv, L0_attn_Wbeta, L0_attn_b_beta,
           L0_attn_Wout, L0_ffn_Wr, L0_ffn_br, L0_ffn_Wi, L0_ffn_bi, L0_ffn_Wg, L0_ffn_bg,
           L0_ffn_Wo, L0_ffn_bo, L0_ffn_gamma, L0_ffn_beta):
    x = np.ascontiguousarray(np.asarray(L0_attn_x, dtype=np.float32))
    f32 = lambda a: np.ascontiguousarray(np.asarray(a, dtype=np.float32))
    Wq, Wk, Wv = f32(L0_attn_Wq), f32(L0_attn_Wk), f32(L0_attn_Wv)
    Wbeta, b_beta, Wout = f32(L0_attn_Wbeta), f32(L0_attn_b_beta), f32(L0_attn_Wout)
    Wr, br = f32(L0_ffn_Wr), f32(L0_ffn_br)
    Wi, bi, Wg, bg = f32(L0_ffn_Wi), f32(L0_ffn_bi), f32(L0_ffn_Wg), f32(L0_ffn_bg)
    Wo, bo = f32(L0_ffn_Wo), f32(L0_ffn_bo)
    gamma, beta = f32(L0_ffn_gamma), f32(L0_ffn_beta)

    if _NC_CACHE[0] is None:
        _NC_CACHE[0] = build_nc()
    nc = _NC_CACHE[0]

    wr_r = np.ascontiguousarray(Wr.reshape(8, 128, E).transpose(1, 0, 2).reshape(128, 64))
    bi_r = np.ascontiguousarray(bi.reshape(E, 32, 128).transpose(2, 0, 1).reshape(128, E * 32))
    bg_r = np.ascontiguousarray(bg.reshape(E, 32, 128).transpose(2, 0, 1).reshape(128, E * 32))
    bo_r = np.ascontiguousarray(bo.reshape(1, E * D))
    gam_r = np.ascontiguousarray(gamma.reshape(1, E * D))
    bet_r = np.ascontiguousarray(beta.reshape(1, E * D))
    bb_r = np.ascontiguousarray(b_beta.reshape(1, H))
    br_r = np.ascontiguousarray(br.reshape(1, E))
    ident = np.eye(128, dtype=np.float32).astype(ml_dtypes.bfloat16)
    ident32 = np.eye(128, dtype=np.float32)
    ones_bf = np.ones((1, 128), np.float32).astype(ml_dtypes.bfloat16)
    ones_f = np.ones((1, 128), np.float32)
    m_tril = -np.tril(np.ones((128, 128), np.float32), -1)
    m_triu = np.triu(np.ones((128, 128), np.float32), 0)

    common = dict(wq_in=Wq, wk_in=Wk, wv_in=Wv, wout_in=Wout, wbeta_in=Wbeta,
                  bbeta_in=bb_r, wr_in=wr_r, br_in=br_r, wi_in=Wi, wg_in=Wg, wo_in=Wo,
                  bi_in=bi_r, bg_in=bg_r, bo_in=bo_r, gam_in=gam_r, bet_in=bet_r,
                  ident_in=ident, ident32_in=ident32, ones32_in=ones_f,
                  mtril_in=m_tril, mtriu_in=m_triu, ones1_in=ones_bf)
    in_maps = []
    for c in range(8):
        b, q = c // 4, c % 4
        qs = np.zeros((128, 4), np.float32)
        qs[:, q] = 1.0
        m = dict(common)
        m["xb"] = x[b]
        m["xq"] = np.ascontiguousarray(x[b, q * QTOK:(q + 1) * QTOK])
        m["qsel"] = qs
        in_maps.append(m)

    res = run_bass_kernel_spmd(nc, in_maps, core_ids=list(range(8)))
    outp = np.empty((B, T, D), np.float32)
    for c in range(8):
        b, q = c // 4, c % 4
        outp[b, q * QTOK:(q + 1) * QTOK] = res.results[c]["out"]
    return outp


# version 08
# speedup vs baseline: 1.3451x; 1.3451x over previous
import numpy as np
import ml_dtypes
import concourse.bass as bass
import concourse.mybir as mybir
from concourse.bass_utils import run_bass_kernel_spmd
from contextlib import ExitStack

F32 = mybir.dt.float32
BF16 = mybir.dt.bfloat16
AF = mybir.ActivationFunctionType
OP = mybir.AluOpType
AX = mybir.AxisListType

B, T, D, H, Dh, E, FF, C = 2, 8192, 1024, 16, 64, 8, 4096, 128
HC = 4
HD = HC * Dh
NCH = T // C
NT = T // 128
QTOK, QNT = 2048, 16
LN_EPS, K_EPS, FFN_EPS = 1e-5, 1e-12, 1e-12


class Em:
    M = {"pe": "pe", "ac": "ac", "dv": "dv", "po": "po", "sy": "sy", "gp": "po"}

    def __init__(self, stream, eng, sems):
        self.stream = stream
        self.eng = eng
        self.sems = sems
        self.cnt = {k: 0 for k in ("pe", "ac", "dv", "po", "sy", "gp")}

    def __call__(self, which, fn, waits=None):
        inc = 16 if which in ("sy", "gp") else 1
        self.cnt[which] += inc
        if Em.M[which] == self.stream:
            if waits:
                for sname, n in waits.items():
                    if n > 0:
                        self.eng.wait_ge(self.sems[sname], n)
            fn(self.eng).then_inc(self.sems[which], inc)
        return self.cnt[which]

    def wait(self, stream, waits):
        if stream == self.stream and waits:
            for sname, n in waits.items():
                if n > 0:
                    self.eng.wait_ge(self.sems[sname], n)


def build_nc():
    nc = bass.Bass("TRN2", target_bir_lowering=False, debug=False)

    xb = nc.dram_tensor("xb", [T, D], F32, kind="ExternalInput").ap()
    xq = nc.dram_tensor("xq", [QTOK, D], F32, kind="ExternalInput").ap()
    qsel = nc.dram_tensor("qsel", [128, 4], F32, kind="ExternalInput").ap()
    wq_in = nc.dram_tensor("wq_in", [D, HD], F32, kind="ExternalInput").ap()
    wk_in = nc.dram_tensor("wk_in", [D, HD], F32, kind="ExternalInput").ap()
    wvb_in = nc.dram_tensor("wvb_in", [D, HD + HC], F32, kind="ExternalInput").ap()
    wout_in = nc.dram_tensor("wout_in", [D, D], F32, kind="ExternalInput").ap()
    bbeta_in = nc.dram_tensor("bbeta_in", [1, HC], F32, kind="ExternalInput").ap()
    wr_in = nc.dram_tensor("wr_in", [128, 64], F32, kind="ExternalInput").ap()
    br_in = nc.dram_tensor("br_in", [1, E], F32, kind="ExternalInput").ap()
    wi_in = nc.dram_tensor("wi_in", [E, D, FF], F32, kind="ExternalInput").ap()
    wg_in = nc.dram_tensor("wg_in", [E, D, FF], F32, kind="ExternalInput").ap()
    wo_in = nc.dram_tensor("wo_in", [E, FF, D], F32, kind="ExternalInput").ap()
    bi_in = nc.dram_tensor("bi_in", [128, E * 32], F32, kind="ExternalInput").ap()
    bg_in = nc.dram_tensor("bg_in", [128, E * 32], F32, kind="ExternalInput").ap()
    bo_in = nc.dram_tensor("bo_in", [1, E * D], F32, kind="ExternalInput").ap()
    gam_in = nc.dram_tensor("gam_in", [1, E * D], F32, kind="ExternalInput").ap()
    bet_in = nc.dram_tensor("bet_in", [1, E * D], F32, kind="ExternalInput").ap()
    ident_in = nc.dram_tensor("ident_in", [128, 128], BF16, kind="ExternalInput").ap()
    ident32_in = nc.dram_tensor("ident32_in", [128, 128], F32, kind="ExternalInput").ap()
    ones32_in = nc.dram_tensor("ones32_in", [1, 128], F32, kind="ExternalInput").ap()
    mtril_in = nc.dram_tensor("mtril_in", [128, 128], F32, kind="ExternalInput").ap()
    mtriu_in = nc.dram_tensor("mtriu_in", [128, 128], F32, kind="ExternalInput").ap()
    ones1_in = nc.dram_tensor("ones1_in", [1, 128], BF16, kind="ExternalInput").ap()

    out = nc.dram_tensor("out", [QTOK, D], F32, kind="ExternalOutput").ap()

    q_scr = nc.dram_tensor("q_scr", [T, HD], BF16, kind="Internal").ap()
    k_scr = nc.dram_tensor("k_scr", [T, HD], BF16, kind="Internal").ap()
    v_scr = nc.dram_tensor("v_scr", [T, HD], BF16, kind="Internal").ap()
    o_loc = nc.dram_tensor("o_loc", [T, HD], BF16, kind="Internal").ap()
    o_gath = nc.dram_tensor("o_gath", [4 * T, HD], BF16, kind="Internal").ap()
    h_scr = nc.dram_tensor("h_scr", [4, 32, 128, 512], BF16, kind="Internal").ap()
    wo_bscr = nc.dram_tensor("wo_bscr", [E, FF, D], BF16, kind="Internal").ap()

    with ExitStack() as ctx:
        def sbt(name, shape, dt):
            return ctx.enter_context(nc.sbuf_tensor(name, shape, dt))

        def pst(name, shape, dt):
            return ctx.enter_context(nc.psum_tensor(name, shape, dt))

        sid = sbt("sid", [128, 128], BF16)
        sid32 = sbt("sid32", [128, 128], F32)
        ones32 = sbt("ones32", [1, 128], F32)
        mtril = sbt("mtril", [128, 128], F32)
        mtriu = sbt("mtriu", [128, 128], F32)
        ones1 = sbt("ones1", [1, 128], BF16)
        wr_f = sbt("wr_f", [128, 64], F32)
        br_f = sbt("br_f", [1, E], F32)
        bb_bf = sbt("bb_bf", [1, HC], BF16)
        bi_sb = sbt("bi_sb", [128, E * 32], F32)
        bg_sb = sbt("bg_sb", [128, E * 32], F32)
        grow_row = sbt("grow_row", [1, 3072], F32)
        eps_sb = sbt("eps_sb", [128, 2], F32)
        bo_row = sbt("bo_row", [1, D], BF16)
        qsel_sb = sbt("qsel_sb", [128, 4], F32)
        beta_all = sbt("beta_all", [128, NCH * HC], F32)
        c_sb = sbt("c_sb", [128, QNT * E], F32)
        small = sbt("small", [128, 64], F32)
        xnt_res = sbt("xnt_res", [128, 8 * QTOK], BF16)
        f_res = sbt("f_res", [128, QNT * D], BF16)
        grep_bf = sbt("grep_bf", [128, D], BF16)
        brep_bf = sbt("brep_bf", [128, D], BF16)

        ar_f = sbt("ar_f", [128, 8192], F32)
        ar_bf = sbt("ar_bf", [128, 36864], BF16)

        wstage = ar_f[:, 0:8192]
        xs = [ar_f[:, i * 1024:(i + 1) * 1024] for i in range(2)]
        scr32 = ar_f[:, 2048:3072]
        scr32b = ar_f[:, 3072:4096]
        s_f32 = ar_f[0:64, 4096:4352]
        s32t = ar_f[:, 4096:5120]
        wst_wo = [ar_f[:, 4096 + i * 2048: 4096 + (i + 1) * 2048] for i in range(2)]
        r0sp = [ar_f[:, i * 512:(i + 1) * 512] for i in range(16)]

        wq_bf = ar_bf[:, 0:2048]
        wk_bf = ar_bf[:, 2048:4096]
        wvb_bf = ar_bf[:, 4096:6176]
        xn_bf = ar_bf[:, 6176:7200]
        xqt_bf = ar_bf[:, 7200:8224]
        qe_bf = ar_bf[:, 8224:8480]
        ve_bf = ar_bf[:, 8480:8736]
        kn_bf = ar_bf[:, 8736:8992]
        wob_st = [ar_bf[:, 8992 + i * 2048: 8992 + (i + 1) * 2048] for i in range(2)]
        P2O = 16384
        ktm = [ar_bf[:, P2O + i * 256: P2O + (i + 1) * 256] for i in range(2)]
        qtm = [ar_bf[:, P2O + 512 + i * 256: P2O + 512 + (i + 1) * 256] for i in range(2)]
        vtm = [ar_bf[:, P2O + 1024 + i * 256: P2O + 1024 + (i + 1) * 256] for i in range(2)]
        kt2 = [ar_bf[:, P2O + 1536 + i * 256: P2O + 1536 + (i + 1) * 256] for i in range(2)]
        qt2 = [ar_bf[:, P2O + 2048 + i * 256: P2O + 2048 + (i + 1) * 256] for i in range(2)]
        kht = [ar_bf[:, P2O + 2560 + i * 256: P2O + 2560 + (i + 1) * 256] for i in range(2)]
        gm2 = [ar_bf[:, P2O + 3072 + i * 512: P2O + 3072 + (i + 1) * 512] for i in range(2)]
        wkp = [ar_bf[:, P2O + 4096 + i * 512: P2O + 4096 + (i + 1) * 512] for i in range(2)]
        a2 = [ar_bf[:, P2O + 5120 + i * 512: P2O + 5120 + (i + 1) * 512] for i in range(2)]
        b2 = [ar_bf[:, P2O + 6144 + i * 512: P2O + 6144 + (i + 1) * 512] for i in range(2)]
        x2 = [ar_bf[:, P2O + 7168 + i * 512: P2O + 7168 + (i + 1) * 512] for i in range(2)]
        x0b = ar_bf[:, P2O + 8192: P2O + 8704]
        u2 = ar_bf[:, P2O + 8704: P2O + 8960]
        o2 = [ar_bf[:, P2O + 8960 + i * 256: P2O + 8960 + (i + 1) * 256] for i in range(2)]
        s2bf = ar_bf[:, P2O + 9472: P2O + 9600]
        wout_bf = ar_bf[:, 0:8192]
        tp4 = ar_bf[:, 8192:16384]
        oT = ar_bf[:, 16384:32768]
        p45_g = ar_bf[:, 32768:33792]
        wi_bf = ar_bf[:, 0:8192]
        wg_bf = ar_bf[:, 8192:16384]
        slab = ar_bf[:, 16384:32768]
        wo_half = ar_bf[:, 0:16384]
        hg_bf = ar_bf[:, 32768:33280]
        hs_bf = ar_bf[:, 33280:33792]
        h1_bf = [ar_bf[:, 33792 + i * 512: 33792 + (i + 1) * 512] for i in range(2)]
        yc_bf = ar_bf[:, 34816:35840]
        t2_bf = ar_bf[:, 35840:36864]

        pA = pst("pA", [128, 512], F32)
        pB = pst("pB", [128, 512], F32)
        pE = pst("pE", [128, 512], F32)
        pF = pst("pF", [128, 512], F32)
        pX = pst("pX", [128, 512], F32)
        pG = pst("pG", [128, 512], F32)
        pS2 = pst("pS2", [64, 256], F32)
        pCb = pst("pCb", [128, 1024], BF16)

        sems = {}
        for name in ("pe", "ac", "dv", "po", "sy", "gp", "cc"):
            sems[name] = ctx.enter_context(nc.semaphore(name))
        block = ctx.enter_context(nc.Block())

        wo_rounds = [(e_i, jp) for e_i in range(E) for jp in range(16)]

        def walk(em: Em):
            for dst, src in ((sid, ident_in), (sid32, ident32_in), (ones32, ones32_in),
                             (mtril, mtril_in), (mtriu, mtriu_in), (ones1, ones1_in),
                             (wr_f, wr_in), (br_f, br_in), (bi_sb, bi_in), (bg_sb, bg_in),
                             (qsel_sb, qsel)):
                em("sy", lambda e, d=dst, s=src: e.dma_start(out=d[:], in_=s))
            em("dv", lambda e: e.memset(eps_sb[:, 0:1], LN_EPS))
            em("dv", lambda e: e.memset(eps_sb[:, 1:2], FFN_EPS))
            syn = em("sy", lambda e: e.dma_start(out=wstage[0:1, 0:HC], in_=bbeta_in))
            dvn = em("dv", lambda e: e.tensor_copy(out=bb_bf[:], in_=wstage[0:1, 0:HC]),
                     {"sy": syn})
            prev = {"dv": dvn, "ac": 0}
            for win, wbf, wdt in ((wq_in, wq_bf, HD), (wk_in, wk_bf, HD),
                                  (wvb_in, wvb_bf, HD + HC)):
                syn = em("sy", lambda e, w=win, wdt=wdt: e.dma_start(
                    out=wstage[:, 0:8 * wdt].rearrange("p (j n) -> p j n", n=wdt),
                    in_=w.rearrange("(j p) n -> p j n", p=128)), dict(prev))
                dvh = em("dv", lambda e, wb=wbf, wdt=wdt: e.tensor_copy(
                    out=wb[:, 0:4 * wdt], in_=wstage[:, 0:4 * wdt]), {"sy": syn})
                ach = em("ac", lambda e, wb=wbf, wdt=wdt: e.copy(
                    out=wb[:, 4 * wdt:8 * wdt], in_=wstage[:, 4 * wdt:8 * wdt]),
                    {"sy": syn})
                prev = {"dv": dvh, "ac": ach}

            busy = {0: {"dv": prev["dv"], "ac": prev["ac"]},
                    1: {"dv": prev["dv"], "ac": prev["ac"]}}
            wo_gp = {"ld": 0, "st": 0, "dv": prev["dv"], "ac": prev["ac"]}
            wo_round_idx = [0]

            def wo_precast_round():
                if wo_round_idx[0] >= len(wo_rounds):
                    return
                ri = wo_round_idx[0]
                wo_round_idx[0] += 1
                e_i, jp = wo_rounds[ri]
                par = ri % 2
                gpl = em("gp", lambda e: e.dma_start(
                    out=wst_wo[par].rearrange("p (j n) -> p j n", n=1024),
                    in_=wo_in[e_i].rearrange("(j p) n -> p j n", p=128)[:, jp * 2:jp * 2 + 2, :]),
                    {"dv": wo_gp["dv"], "ac": wo_gp["ac"]})
                dvc = em("dv", lambda e: e.tensor_copy(
                    out=wob_st[par][:, 0:1024], in_=wst_wo[par][:, 0:1024]),
                    {"gp": gpl})
                acc = em("ac", lambda e: e.copy(
                    out=wob_st[par][:, 1024:2048], in_=wst_wo[par][:, 1024:2048]),
                    {"gp": gpl})
                gps = em("gp", lambda e: e.dma_start(
                    out=wo_bscr[e_i].rearrange("(j p) n -> p j n", p=128)[:, jp * 2:jp * 2 + 2, :],
                    in_=wob_st[par].rearrange("p (j n) -> p j n", n=1024)),
                    {"dv": dvc, "ac": acc})
                if par == 1:
                    wo_gp["dv"] = dvc
                    wo_gp["ac"] = acc
                    wo_gp["st"] = gps

            def ln_tile(src, row, par, dst_bf):
                w = dict(busy[par])
                syn = em("sy", lambda e: e.dma_start(out=xs[par][:], in_=src[row:row + 128, :]),
                         w if w else None)
                dv1 = em("dv", lambda e: e.reduce_sum(out=small[:, 0:1], in_=xs[par][:],
                                                      axis=AX.X),
                         {"sy": syn, "dv": em.cnt["dv"], "ac": em.cnt["ac"]})
                ac1 = em("ac", lambda e: e.activation(out=scr32[:], in_=xs[par][:],
                                                      func=AF.Square, accum_out=small[:, 1:2]),
                         {"sy": syn, "dv": dv1})
                dv2 = em("dv", lambda e: e.tensor_scalar(out=small[:, 2:3], in0=small[:, 0:1],
                                                         scalar1=1.0 / D, scalar2=None,
                                                         op0=OP.mult), {"dv": dv1, "ac": ac1})
                dv3 = em("dv", lambda e: e.tensor_mul(out=small[:, 3:4], in0=small[:, 2:3],
                                                      in1=small[:, 2:3]), {"dv": dv2})
                dv4 = em("dv", lambda e: e.scalar_tensor_tensor(
                    out=small[:, 4:5], in0=small[:, 1:2], scalar=1.0 / D, in1=small[:, 3:4],
                    op0=OP.mult, op1=OP.subtract), {"dv": dv3})
                ac2 = em("ac", lambda e: e.activation(out=small[:, 5:6], in_=small[:, 4:5],
                                                      func=AF.Sqrt, bias=eps_sb[:, 0:1]), {"dv": dv4})
                dv5 = em("dv", lambda e: e.reciprocal(out=small[:, 6:7], in_=small[:, 5:6]),
                         {"ac": ac2})
                dv6 = em("dv", lambda e: e.tensor_scalar(out=small[:, 7:8], in0=small[:, 2:3],
                                                         scalar1=small[:, 6:7], scalar2=-1.0,
                                                         op0=OP.mult, op1=OP.mult), {"dv": dv5})
                ac3 = em("ac", lambda e: e.activation(out=dst_bf, in_=xs[par][:], func=AF.Identity,
                                                      bias=small[:, 7:8], scale=small[:, 6:7]),
                         {"dv": dv6})
                return syn, ac3

            def transpose8(src_bf, dsts, f32=False):
                last = 0
                pbank = pA if f32 else pCb
                ii = sid32 if f32 else sid
                for halfi in range(2):
                    w_ac = em.cnt["ac"]
                    for j in range(4):
                        jj = halfi * 4 + j
                        em("pe", lambda e, j=j, jj=jj: e.transpose(
                            pbank[:, j * 128:(j + 1) * 128],
                            src_bf[:, jj * 128:(jj + 1) * 128], ii[:]),
                            {"ac": w_ac, "dv": em.cnt["dv"]} if j == 0 else None)
                    pen = em.cnt["pe"]
                    for j in range(4):
                        jj = halfi * 4 + j
                        last = em("ac", lambda e, j=j, jj=jj: e.copy(
                            out=dsts[jj], in_=pbank[:, j * 128:(j + 1) * 128]),
                            {"pe": pen} if j == 0 else None)
                return last

            st_q = st_v = st_k = 0
            dv_kn_prev = 0
            for tt in range(NT):
                row = tt * 128
                par = tt % 2
                syn, ac_ln = ln_tile(xb, row, par, xn_bf)
                ac_tr = transpose8(xn_bf, [xqt_bf[:, j * 128:(j + 1) * 128] for j in range(8)])

                for j in range(8):
                    em("pe", lambda e, j=j: e.matmul(
                        pA[:, 0:HD], lhsT=xqt_bf[:, j * 128:(j + 1) * 128],
                        rhs=wq_bf[:, j * HD:(j + 1) * HD],
                        start=(j == 0), stop=(j == 7)),
                        {"ac": ac_tr} if j == 0 else None)
                ac_q = em("ac", lambda e: e.copy(out=qe_bf[:], in_=pA[:, 0:HD]),
                          {"pe": em.cnt["pe"], "sy": st_q})
                st_q = em("sy", lambda e, row=row: e.dma_start(
                    out=q_scr[row:row + 128, :], in_=qe_bf), {"ac": ac_q})

                for j in range(7):
                    em("pe", lambda e, j=j: e.matmul(
                        pX[:, 0:HD + HC], lhsT=xqt_bf[:, j * 128:(j + 1) * 128],
                        rhs=wvb_bf[:, j * (HD + HC):(j + 1) * (HD + HC)],
                        start=(j == 0), stop=False),
                        {"ac": ac_tr} if j == 0 else None)
                em("pe", lambda e: e.matmul(pX[:, HD:HD + HC], lhsT=ones1[:], rhs=bb_bf[:],
                                            start=False, stop=False))
                em("pe", lambda e: e.matmul(
                    pX[:, 0:HD + HC], lhsT=xqt_bf[:, 7 * 128:8 * 128],
                    rhs=wvb_bf[:, 7 * (HD + HC):8 * (HD + HC)],
                    start=False, stop=True))
                ac_v = em("ac", lambda e: e.copy(out=ve_bf[:], in_=pX[:, 0:HD]),
                          {"pe": em.cnt["pe"], "sy": st_v})
                em("ac", lambda e, tt=tt: e.activation(
                    out=beta_all[:, tt * HC:(tt + 1) * HC], in_=pX[:, HD:HD + HC],
                    func=AF.Sigmoid))
                st_v = em("sy", lambda e, row=row: e.dma_start(
                    out=v_scr[row:row + 128, :], in_=ve_bf), {"ac": em.cnt["ac"]})

                for j in range(8):
                    em("pe", lambda e, j=j: e.matmul(
                        pB[:, 0:HD], lhsT=xqt_bf[:, j * 128:(j + 1) * 128],
                        rhs=wk_bf[:, j * HD:(j + 1) * HD],
                        start=(j == 0), stop=(j == 7)),
                        {"dv": dv_kn_prev} if j == 0 else None)
                pe_k = em.cnt["pe"]
                ac_k1 = em("ac", lambda e: e.activation(out=scr32b[:, 0:HD], in_=pB[:, 0:HD],
                                                        func=AF.Square),
                           {"pe": pe_k, "dv": em.cnt["dv"]})
                dvk = em("dv", lambda e: e.reduce_sum(
                    out=small[:, 16:16 + HC],
                    in_=scr32b[:, 0:HD].rearrange("p (h d) -> p h d", d=64),
                    axis=AX.X), {"ac": ac_k1, "dv": em.cnt["dv"]})
                dvk = em("dv", lambda e: e.tensor_single_scalar(
                    out=small[:, 20:20 + HC], in_=small[:, 16:16 + HC],
                    scalar=K_EPS * K_EPS, op=OP.max), {"dv": dvk})
                ack = em("ac", lambda e: e.activation(out=small[:, 24:24 + HC],
                                                      in_=small[:, 20:20 + HC],
                                                      func=AF.Sqrt), {"dv": dvk})
                dvk = em("dv", lambda e: e.reciprocal(out=small[:, 28:28 + HC],
                                                      in_=small[:, 24:24 + HC]), {"ac": ack})
                for hh in range(HC):
                    dvk = em("dv", lambda e, hh=hh: e.tensor_scalar(
                        out=kn_bf[:, hh * 64:(hh + 1) * 64], in0=pB[:, hh * 64:(hh + 1) * 64],
                        scalar1=small[:, 28 + hh:29 + hh], scalar2=None, op0=OP.mult),
                        {"dv": dvk, "sy": st_k} if hh == 0 else None)
                dv_kn_prev = dvk
                st_k = em("sy", lambda e, row=row: e.dma_start(
                    out=k_scr[row:row + 128, :], in_=kn_bf), {"dv": dvk})
                wo_precast_round()
                wo_precast_round()
                busy[par] = {"pe": em.cnt["pe"], "dv": em.cnt["dv"], "ac": em.cnt["ac"]}
            st_qkv = max(st_q, st_v, st_k)

            for ltt in range(QNT):
                row = ltt * 128
                par = ltt % 2
                syn, ac_ln = ln_tile(xq, row, par, xn_bf)
                ac_tr = transpose8(
                    xn_bf, [xnt_res[:, j * QTOK + ltt * 128: j * QTOK + ltt * 128 + 128]
                            for j in range(8)])
                dvf = em("dv", lambda e: e.tensor_scalar(
                    out=scr32[:], in0=xs[par][:], scalar1=small[:, 2:3], scalar2=small[:, 6:7],
                    op0=OP.subtract, op1=OP.mult), {"dv": em.cnt["dv"], "ac": ac_ln})
                ac_t32 = transpose8(scr32, [scr32b[:, j * 128:(j + 1) * 128] for j in range(8)],
                                    f32=True)
                for j in range(8):
                    em("pe", lambda e, j=j: e.matmul(
                        pX[:, 0:E], lhsT=scr32b[:, j * 128:(j + 1) * 128],
                        rhs=wr_f[:, j * E:(j + 1) * E],
                        start=(j == 0), stop=False), {"ac": ac_t32} if j == 0 else None)
                em("pe", lambda e: e.matmul(pX[:, 0:E], lhsT=ones32[:], rhs=br_f[:],
                                            start=False, stop=True))
                ac_lg = em("ac", lambda e: e.copy(out=small[:, 8:16], in_=pX[:, 0:E]),
                           {"pe": em.cnt["pe"]})
                d1 = em("dv", lambda e: e.reduce_max(out=small[:, 16:17], in_=small[:, 8:16],
                                                     axis=AX.X), {"ac": ac_lg, "dv": em.cnt["dv"]})
                d2 = em("dv", lambda e: e.tensor_scalar(out=small[:, 17:18], in0=small[:, 16:17],
                                                        scalar1=-1.0, scalar2=None, op0=OP.mult),
                        {"dv": d1})
                d3 = em("dv", lambda e: e.tensor_scalar(out=small[:, 24:32], in0=small[:, 8:16],
                                                        scalar1=small[:, 16:17], scalar2=None,
                                                        op0=OP.is_equal), {"dv": d2})
                d4 = em("dv", lambda e: e.scalar_tensor_tensor(
                    out=small[:, 32:40], in0=small[:, 24:32], scalar=-1e30, in1=small[:, 8:16],
                    op0=OP.mult, op1=OP.add), {"dv": d3})
                d5 = em("dv", lambda e: e.reduce_max(out=small[:, 18:19], in_=small[:, 32:40],
                                                     axis=AX.X), {"dv": d4})
                d6 = em("dv", lambda e: e.tensor_scalar(out=small[:, 40:48], in0=small[:, 8:16],
                                                        scalar1=small[:, 18:19], scalar2=None,
                                                        op0=OP.is_ge), {"dv": d5})
                a7 = em("ac", lambda e: e.activation(out=small[:, 48:56], in_=small[:, 8:16],
                                                     func=AF.Exp, bias=small[:, 17:18]),
                        {"dv": d6})
                d8 = em("dv", lambda e: e.tensor_mul(out=small[:, 56:64], in0=small[:, 48:56],
                                                     in1=small[:, 40:48]), {"ac": a7})
                d9 = em("dv", lambda e: e.reduce_sum(out=small[:, 19:20], in_=small[:, 56:64],
                                                     axis=AX.X), {"dv": d8})
                d10 = em("dv", lambda e: e.reciprocal(out=small[:, 20:21], in_=small[:, 19:20]),
                         {"dv": d9})
                em("dv", lambda e, ltt=ltt: e.tensor_scalar(
                    out=c_sb[:, ltt * E:(ltt + 1) * E], in0=small[:, 56:64],
                    scalar1=small[:, 20:21], scalar2=None, op0=OP.mult), {"dv": d10})
                wo_precast_round()
                wo_precast_round()
                busy[par] = {"pe": em.cnt["pe"], "dv": em.cnt["dv"], "ac": em.cnt["ac"]}
            while wo_round_idx[0] < len(wo_rounds):
                wo_precast_round()

            def kt_h(buf, h):
                return buf[(h % 2) * 64:(h % 2) * 64 + 64, (h // 2) * 128:(h // 2) * 128 + 128]

            def s2_h(h):
                return s2bf[(h % 2) * 64:(h % 2) * 64 + 64, (h // 2) * 64:(h // 2) * 64 + 64]

            def ag_slot(h, off):
                return (pA if h % 2 == 0 else pB)[:, off + (h // 2) * 128:
                                                  off + (h // 2) * 128 + 128]

            def pu_slot(h):
                return (pG if h % 2 == 0 else pX)[:, (h // 2) * 64:(h // 2) * 64 + 64]

            first_w = {"sy": st_qkv, "dv": em.cnt["dv"], "ac": em.cnt["ac"],
                       "pe": em.cnt["pe"]}
            pre_info = {}
            ser_info = {}
            sstate = {"s_ready": 0, "dv_s": 0, "dv_u": 0, "ac_o": 0}
            gp_o = {}

            def pre(g):
                par = g % 2
                info = {}
                w0 = {"sy": st_qkv}
                if g >= 2:
                    w0["pe"] = ser_info[g - 2]["pe"]
                else:
                    w0.update({k: first_w[k] for k in ("pe", "dv", "ac")})
                row = g * C
                syn_k = em("sy", lambda e, row=row, par=par: e.dma_start(
                    out=ktm[par][:], in_=k_scr[row:row + C, :]), w0)
                syn_q = em("sy", lambda e, row=row, par=par: e.dma_start(
                    out=qtm[par][:], in_=q_scr[row:row + C, :]))
                syn_v = em("sy", lambda e, row=row, par=par: e.dma_start(
                    out=vtm[par][:], in_=v_scr[row:row + C, :]))
                info["syn_k"] = syn_k
                wx0 = {"sy": syn_v, "dv": em.cnt["dv"]}
                if g >= 1:
                    wx0["pe"] = pre_info[g - 1]["pe_k0"]
                dvx0 = 0
                for h in range(HC):
                    dvx0 = em("dv", lambda e, h=h, g=g, par=par: e.tensor_scalar(
                        out=x0b[:, h * 128:h * 128 + 64],
                        in0=vtm[par][:, h * 64:(h + 1) * 64],
                        scalar1=beta_all[:, g * HC + h:g * HC + h + 1],
                        scalar2=None, op0=OP.mult), wx0 if h == 0 else None)
                    dvx0 = em("dv", lambda e, h=h, g=g, par=par: e.tensor_scalar(
                        out=x0b[:, h * 128 + 64:h * 128 + 128],
                        in0=ktm[par][:, h * 64:(h + 1) * 64],
                        scalar1=beta_all[:, g * HC + h:g * HC + h + 1],
                        scalar2=None, op0=OP.mult), None)
                info["dvx0"] = dvx0
                wtr = {"sy": syn_q,
                       "ac": pre_info[g - 1]["ac_kht"] if g >= 1 else first_w["ac"]}
                for bb in range(2):
                    em("pe", lambda e, bb=bb, par=par: e.transpose(
                        pCb[:, bb * 128:(bb + 1) * 128],
                        ktm[par][:, bb * 128:(bb + 1) * 128], sid[:]),
                        wtr if bb == 0 else None)
                pen = em.cnt["pe"]
                ac_kt = em("ac", lambda e, par=par: e.copy(out=kt2[par][:], in_=pCb[:, 0:256]),
                           {"pe": pen})
                for bb in range(2):
                    em("pe", lambda e, bb=bb, par=par: e.transpose(
                        pCb[:, 256 + bb * 128: 256 + (bb + 1) * 128],
                        qtm[par][:, bb * 128:(bb + 1) * 128], sid[:]),
                        {"ac": ac_kt} if bb == 0 else None)
                pen = em.cnt["pe"]
                ac_qt = em("ac", lambda e, par=par: e.copy(out=qt2[par][:], in_=pCb[:, 256:512]),
                           {"pe": pen})
                info["ac_qt"] = ac_qt
                wag = {"ac": ac_kt, "dv": sstate["dv_u"]}
                if g >= 1:
                    wag["dv"] = max(wag["dv"], pre_info[g - 1]["dv_g"])
                if g < 2:
                    wag["dv"] = max(wag["dv"], first_w["dv"])
                for h in range(HC):
                    em("pe", lambda e, h=h, par=par: e.matmul(
                        ag_slot(h, 0),
                        lhsT=kt_h(kt2[par], h), rhs=kt_h(kt2[par], h),
                        start=True, stop=True),
                        wag if h == 0 else None)
                for h in range(HC):
                    em("pe", lambda e, h=h, par=par: e.matmul(
                        ag_slot(h, 256),
                        lhsT=kt_h(kt2[par], h), rhs=kt_h(qt2[par], h),
                        start=True, stop=True),
                        {"ac": ac_qt} if h == 0 else None)
                pe_ag = em.cnt["pe"]
                wm = {"pe": pe_ag, "dv": em.cnt["dv"]}
                if g >= 1:
                    wm["pe"] = max(pe_ag, pre_info[g - 1]["pe_all"])
                dv_a = 0
                for h in range(HC):
                    dv_a = em("dv", lambda e, h=h, g=g: e.scalar_tensor_tensor(
                        out=a2[0][:, h * 128:(h + 1) * 128],
                        in0=ag_slot(h, 0),
                        scalar=beta_all[:, g * HC + h:g * HC + h + 1],
                        in1=mtril[:], op0=OP.mult, op1=OP.mult),
                        wm if h == 0 else None)
                dv_g = dv_a
                for h in range(HC):
                    dv_g = em("dv", lambda e, h=h, par=par: e.tensor_tensor(
                        out=gm2[par][:, h * 128:(h + 1) * 128],
                        in0=ag_slot(h, 256), in1=mtriu[:], op=OP.mult),
                        {"pe": pe_ag, "dv": dv_g} if h == 0 else None)
                info["dv_g"] = dv_g
                wb0 = {"dv": dv_a, "ac": ac_qt}
                for h in range(HC):
                    em("pe", lambda e, h=h: e.transpose(
                        pCb[:, 512 + h * 128: 512 + (h + 1) * 128],
                        a2[0][:, h * 128:(h + 1) * 128], sid[:]),
                        wb0 if h == 0 else None)
                pen = em.cnt["pe"]
                ac_b = em("ac", lambda e: e.copy(out=b2[0][:], in_=pCb[:, 512:1024]),
                          {"pe": pen})
                xcur = x0b
                ac_prev_ab = ac_b
                dv_prev_x = dvx0
                dv_prev_b = 0
                pe_k0 = 0
                for k in range(7):
                    p = k % 2
                    xdst = wkp[par] if k == 6 else x2[k % 2]
                    wk_ = {"dv": max(dv_prev_x, dv_prev_b), "ac": ac_prev_ab}
                    for h in range(HC):
                        em("pe", lambda e, h=h, xcur=xcur: e.matmul(
                            pX[:, h * 128:(h + 1) * 128], lhsT=sid[:],
                            rhs=xcur[:, h * 128:(h + 1) * 128],
                            start=(h == 0), stop=False),
                            wk_ if h == 0 else None)
                    for h in range(HC):
                        em("pe", lambda e, h=h, p=p, xcur=xcur: e.matmul(
                            pX[:, h * 128:(h + 1) * 128],
                            lhsT=b2[p][:, h * 128:(h + 1) * 128],
                            rhs=xcur[:, h * 128:(h + 1) * 128],
                            start=False, stop=(h == 3)))
                    pen = em.cnt["pe"]
                    if k == 0:
                        pe_k0 = pen
                    dv_prev_x = em("dv", lambda e, xdst=xdst: e.tensor_copy(
                        out=xdst[:], in_=pX[:]), {"pe": pen})
                    if k < 6:
                        if k < 5:
                            for h in range(HC):
                                em("pe", lambda e, h=h, p=p: e.matmul(
                                    pE[:, h * 128:(h + 1) * 128],
                                    lhsT=b2[p][:, h * 128:(h + 1) * 128],
                                    rhs=a2[p][:, h * 128:(h + 1) * 128],
                                    start=(h == 0), stop=(h == 3)),
                                    {"ac": ac_prev_ab} if h == 0 else None)
                        for h in range(HC):
                            em("pe", lambda e, h=h, p=p: e.matmul(
                                pF[:, h * 128:(h + 1) * 128],
                                lhsT=a2[p][:, h * 128:(h + 1) * 128],
                                rhs=b2[p][:, h * 128:(h + 1) * 128],
                                start=(h == 0), stop=(h == 3)),
                                {"ac": ac_prev_ab} if h == 0 else None)
                        pen = em.cnt["pe"]
                        if k < 5:
                            ac_prev_ab = em("ac", lambda e, p=p: e.copy(
                                out=a2[1 - p][:], in_=pE[:]), {"pe": pen})
                        dv_prev_b = em("dv", lambda e, p=p: e.tensor_copy(
                            out=b2[1 - p][:], in_=pF[:]), {"pe": pen, "dv": dv_prev_b})
                        if k >= 5:
                            ac_prev_ab = em.cnt["ac"]
                    xcur = xdst
                info["dv_wkp"] = dv_prev_x
                wkh = {"dv": dv_prev_x, "ac": ac_b}
                for i, h in enumerate((0, 2, 1, 3)):
                    em("pe", lambda e, h=h, par=par: e.transpose(
                        pCb[(h % 2) * 64:(h % 2) * 64 + 64,
                            512 + (h // 2) * 128: 512 + (h // 2 + 1) * 128],
                        wkp[par][:, h * 128 + 64:h * 128 + 128], sid[:]),
                        wkh if i == 0 else None)
                pen = em.cnt["pe"]
                ac_kht = em("ac", lambda e, par=par: e.copy(
                    out=kht[par][:], in_=pCb[:, 512:768]), {"pe": pen})
                info["ac_kht"] = ac_kht
                info["pe_k0"] = pe_k0
                info["pe_all"] = em.cnt["pe"]
                pre_info[g] = info

            def khtv(par, h):
                return kht[par][(h % 2) * 64:(h % 2) * 64 + 64,
                                (h // 2) * 128:(h // 2 + 1) * 128]

            def ser(g):
                par = g % 2
                info = pre_info[g]
                if g == 0:
                    dv_u = 0
                    for h in range(HC):
                        dv_u = em("dv", lambda e, h=h, par=par: e.tensor_copy(
                            out=u2[:, h * 64:(h + 1) * 64],
                            in_=wkp[par][:, h * 128:h * 128 + 64]),
                            {"dv": info["dv_wkp"]} if h == 0 else None)
                else:
                    wu = {"ac": max(info["ac_kht"], sstate["s_ready"]),
                          "dv": max(sstate["dv_u"],
                                    pre_info[g + 1]["dv_wkp"] if (g + 1) in pre_info else 0)}
                    for h in range(HC):
                        em("pe", lambda e, h=h, par=par: e.matmul(
                            pu_slot(h),
                            lhsT=khtv(par, h), rhs=s2_h(h),
                            start=True, stop=True),
                            wu if h == 0 else None)
                    pen = em.cnt["pe"]
                    dv_u = 0
                    for h in range(HC):
                        dv_u = em("dv", lambda e, h=h, par=par: e.tensor_tensor(
                            out=u2[:, h * 64:(h + 1) * 64],
                            in0=wkp[par][:, h * 128:h * 128 + 64],
                            in1=pu_slot(h),
                            op=OP.subtract),
                            {"pe": pen, "dv": info["dvx0"]} if h == 0 else None)
                sstate["dv_u"] = dv_u
                wo_ = {"dv": max(dv_u, info["dv_g"]), "ac": sstate["ac_o"]}
                for h in range(HC):
                    em("pe", lambda e, h=h, par=par, g=g: e.matmul(
                        pG[:, 256 + h * 64: 256 + (h + 1) * 64],
                        lhsT=gm2[par][:, h * 128:(h + 1) * 128],
                        rhs=u2[:, h * 64:(h + 1) * 64],
                        start=True, stop=(g == 0)),
                        wo_ if h == 0 else None)
                    if g > 0:
                        em("pe", lambda e, h=h, par=par: e.matmul(
                            pG[:, 256 + h * 64: 256 + (h + 1) * 64],
                            lhsT=kt_h(qt2[par], h), rhs=s2_h(h),
                            start=False, stop=True))
                pen = em.cnt["pe"]
                wao = {"pe": pen}
                if g >= 2:
                    wao["gp"] = gp_o[g - 2]
                ac_o = em("ac", lambda e, par=par: e.copy(
                    out=o2[par][:], in_=pG[:, 256:512]), wao)
                sstate["ac_o"] = ac_o
                gp_o[g] = em("gp", lambda e, g=g, par=par: e.dma_start(
                    out=o_loc[g * C:(g + 1) * C, :], in_=o2[par][:]), {"ac": ac_o})
                ws = {"dv": max(dv_u, sstate["dv_s"])}
                for h in range(HC):
                    em("pe", lambda e, h=h, par=par: e.matmul(
                        pS2[0:64, h * 64:(h + 1) * 64],
                        lhsT=ktm[par][:, h * 64:(h + 1) * 64],
                        rhs=u2[:, h * 64:(h + 1) * 64],
                        start=(h == 0), stop=(h == 3)),
                        ws if h == 0 else None)
                pen = em.cnt["pe"]
                if g == 0:
                    dv_s = em("dv", lambda e: e.tensor_copy(out=s_f32, in_=pS2[:]),
                              {"pe": pen})
                else:
                    dv_s = em("dv", lambda e: e.tensor_add(out=s_f32, in0=s_f32, in1=pS2[:]),
                              {"pe": pen})
                sstate["dv_s"] = dv_s
                sv = s_f32.rearrange("p (h d) -> p h d", d=64)
                em("ac", lambda e, sv=sv: e.copy(
                    out=s2bf[0:64, :].rearrange("p (b d) -> p b d", d=64),
                    in_=sv[:, 0:HC:2, :]), {"dv": dv_s})
                sstate["s_ready"] = em("ac", lambda e, sv=sv: e.copy(
                    out=s2bf[64:128, :].rearrange("p (b d) -> p b d", d=64),
                    in_=sv[:, 1:HC:2, :]), None)
                ser_info[g] = {"pe": em.cnt["pe"], "dv": em.cnt["dv"], "ac": em.cnt["ac"]}

            pre(0)
            pre(1)
            for g in range(NCH):
                ser(g)
                if g + 2 < NCH:
                    pre(g + 2)
            gp_o_all = em.cnt["gp"]

            if em.stream == "po":
                em.eng.wait_ge(sems["gp"], gp_o_all)
                em.eng.collective_compute(
                    "AllGather", mybir.AluOpType.bypass,
                    replica_groups=[[0, 1, 2, 3], [4, 5, 6, 7]],
                    ins=[o_loc.opt()], outs=[o_gath.opt()],
                ).then_inc(sems["cc"])

            dv_f0 = em("dv", lambda e: e.memset(f_res[:], 0.0),
                       {"ac": em.cnt["ac"], "gp": wo_gp["st"]})
            prev_exp = {"pe": em.cnt["pe"], "dv": dv_f0, "ac": em.cnt["ac"],
                        "gp": em.cnt["gp"], "sy": em.cnt["sy"]}
            gp_h = 0
            hpar = 0
            for e_i in range(E):
                wrow = dict(prev_exp)
                em("sy", lambda e, e_i=e_i: e.dma_start(
                    out=grow_row[0:1, 0:1024], in_=gam_in[0:1, e_i * D:(e_i + 1) * D]), wrow)
                em("sy", lambda e, e_i=e_i: e.dma_start(
                    out=grow_row[0:1, 1024:2048], in_=bet_in[0:1, e_i * D:(e_i + 1) * D]))
                sy_r = em("sy", lambda e, e_i=e_i: e.dma_start(
                    out=grow_row[0:1, 2048:3072], in_=bo_in[0:1, e_i * D:(e_i + 1) * D]))
                dv_bo = em("dv", lambda e: e.tensor_copy(
                    out=bo_row[:], in_=grow_row[0:1, 2048:3072]),
                    {"sy": sy_r, "dv": prev_exp["dv"]})
                for half in range(2):
                    em("pe", lambda e, half=half: e.matmul(
                        pE[:, 0:512], lhsT=ones32[:],
                        rhs=grow_row[0:1, half * 512: half * 512 + 512],
                        start=True, stop=True),
                        {"sy": sy_r, "pe": prev_exp["pe"], "ac": prev_exp["ac"],
                         "dv": prev_exp["dv"]} if half == 0 else {"ac": em.cnt["ac"]})
                    em("ac", lambda e, half=half: e.copy(
                        out=grep_bf[:, half * 512:(half + 1) * 512], in_=pE[:]),
                        {"pe": em.cnt["pe"]})
                    em("pe", lambda e, half=half: e.matmul(
                        pE[:, 0:512], lhsT=ones32[:],
                        rhs=grow_row[0:1, 1024 + half * 512: 1024 + half * 512 + 512],
                        start=True, stop=True), {"ac": em.cnt["ac"]})
                    em("ac", lambda e, half=half: e.copy(
                        out=brep_bf[:, half * 512:(half + 1) * 512], in_=pE[:]),
                        {"pe": em.cnt["pe"]})
                ac_gb = em.cnt["ac"]

                prev_cast = {"dv": em.cnt["dv"], "ac": em.cnt["ac"]}
                for fc in range(4):
                    casts = []
                    for gate in range(2):
                        win = wi_in if gate == 0 else wg_in
                        wbf = wi_bf if gate == 0 else wg_bf
                        syw = em("sy", lambda e, win=win, e_i=e_i, fc=fc: e.dma_start(
                            out=wstage[:].rearrange("p (j n) -> p j n", n=1024),
                            in_=win[e_i].rearrange("(j p) n -> p j n", p=128)
                            [:, :, fc * 1024:(fc + 1) * 1024]),
                            {"dv": prev_cast["dv"], "ac": prev_cast["ac"],
                             "pe": em.cnt["pe"]})
                        dvc = em("dv", lambda e, wbf=wbf: e.tensor_copy(
                            out=wbf[:, 0:4096], in_=wstage[:, 0:4096]), {"sy": syw})
                        acc = em("ac", lambda e, wbf=wbf: e.copy(
                            out=wbf[:, 4096:8192], in_=wstage[:, 4096:8192]), {"sy": syw})
                        prev_cast = {"dv": dvc, "ac": acc}
                        casts.append((dvc, acc))
                    (dv_wi, ac_wi), (dv_wg, ac_wg) = casts
                    for st in range(4):
                        for ffb in range(8):
                            for j in range(8):
                                em("pe", lambda e, j=j, ffb=ffb, st=st: e.matmul(
                                    pE[:, 0:512],
                                    lhsT=wi_bf[:, j * 1024 + ffb * 128: j * 1024 + (ffb + 1) * 128],
                                    rhs=xnt_res[:, j * QTOK + st * 512: j * QTOK + (st + 1) * 512],
                                    start=(j == 0), stop=(j == 7)),
                                    {"dv": dv_wi, "ac": max(ac_wi, em.cnt["ac"])} if j == 0 else None)
                            pe_hi = em.cnt["pe"]
                            for j in range(8):
                                em("pe", lambda e, j=j, ffb=ffb, st=st: e.matmul(
                                    pF[:, 0:512],
                                    lhsT=wg_bf[:, j * 1024 + ffb * 128: j * 1024 + (ffb + 1) * 128],
                                    rhs=xnt_res[:, j * QTOK + st * 512: j * QTOK + (st + 1) * 512],
                                    start=(j == 0), stop=(j == 7)),
                                    {"dv": dv_wg, "ac": ac_wg} if j == 0 else None)
                            pe_hg = em.cnt["pe"]
                            bcol = e_i * 32 + fc * 8 + ffb
                            a1 = em("ac", lambda e, bcol=bcol: e.activation(
                                out=hg_bf, in_=pE[:], func=AF.Gelu,
                                bias=bi_sb[:, bcol:bcol + 1]), {"pe": pe_hi, "dv": em.cnt["dv"]})
                            a2_ = em("ac", lambda e, bcol=bcol: e.activation(
                                out=hs_bf, in_=pF[:], func=AF.Sigmoid,
                                bias=bg_sb[:, bcol:bcol + 1]), {"pe": pe_hg})
                            dvh = em("dv", lambda e, hpar=hpar: e.tensor_mul(
                                out=h1_bf[hpar], in0=hg_bf, in1=hs_bf),
                                {"ac": a2_, "gp": gp_h, "dv": em.cnt["dv"]})
                            gp_h = em("gp", lambda e, st=st, fc=fc, ffb=ffb, hpar=hpar, e_i=e_i:
                                      e.dma_start(out=h_scr[st, fc * 8 + ffb], in_=h1_bf[hpar]),
                                      {"dv": dvh})
                            hpar = 1 - hpar

                epi_dv = em.cnt["dv"]
                epi_ac = em.cnt["ac"]
                for dh in range(2):
                    sy_wo = em("sy", lambda e, dh=dh, e_i=e_i: e.dma_start(
                        out=wo_half.rearrange("p (f n) -> p f n", n=512),
                        in_=wo_bscr[e_i].rearrange("(j p) n -> p j n", p=128)
                        [:, :, dh * 512:(dh + 1) * 512]),
                        {"pe": em.cnt["pe"], "dv": em.cnt["dv"], "ac": em.cnt["ac"],
                         "gp": wo_gp["st"]})
                    for st in range(4):
                        sy_sl = em("sy", lambda e, st=st: e.dma_start(
                            out=slab.rearrange("p (f n) -> p f n", n=512),
                            in_=h_scr[st].rearrange("f p n -> p f n")),
                            {"gp": gp_h, "pe": em.cnt["pe"]})
                        for t4 in range(4):
                            ltt = st * 4 + t4
                            for kq in range(4):
                                for j in range(8):
                                    fb = kq * 8 + j
                                    em("pe", lambda e, fb=fb, t4=t4, dh=dh: e.matmul(
                                        (pA if dh == 0 else pB)[:, 0:512],
                                        lhsT=slab[:, fb * 512 + t4 * 128: fb * 512 + t4 * 128 + 128],
                                        rhs=wo_half[:, fb * 512:(fb + 1) * 512],
                                        start=(kq == 0 and j == 0), stop=False),
                                        {"sy": max(sy_sl, sy_wo), "ac": epi_ac,
                                         "dv": epi_dv} if (kq == 0 and j == 0) else None)
                            for j4 in range(4):
                                em("pe", lambda e, j4=j4, dh=dh, ltt=ltt: e.matmul(
                                    (pA if dh == 0 else pB)[:, j4 * 128:(j4 + 1) * 128],
                                    lhsT=xnt_res[:, (dh * 4 + j4) * QTOK + ltt * 128:
                                                 (dh * 4 + j4) * QTOK + ltt * 128 + 128],
                                    rhs=sid[:], start=False, stop=False))
                            em("pe", lambda e, dh=dh: e.matmul(
                                (pA if dh == 0 else pB)[:, 0:512], lhsT=ones1[:],
                                rhs=bo_row[0:1, dh * 512: dh * 512 + 512],
                                start=False, stop=True))
                            pe_r = em.cnt["pe"]
                            if dh == 0:
                                epi_ac = em("ac", lambda e, ltt=ltt: e.copy(
                                    out=r0sp[ltt], in_=pA[:]), {"pe": pe_r})
                                epi_dv = em.cnt["dv"]
                            else:
                                ccol = ltt * E + e_i
                                d1 = em("dv", lambda e, ltt=ltt: e.reduce_sum(
                                    out=small[:, 0:1], in_=r0sp[ltt], axis=AX.X),
                                    {"pe": pe_r, "dv": em.cnt["dv"], "ac": em.cnt["ac"]})
                                d2 = em("dv", lambda e: e.reduce_sum(
                                    out=small[:, 1:2], in_=pB[:], axis=AX.X), {"dv": d1})
                                d3 = em("dv", lambda e: e.tensor_add(
                                    out=small[:, 2:3], in0=small[:, 0:1], in1=small[:, 1:2]),
                                    {"dv": d2})
                                a1 = em("ac", lambda e, ltt=ltt: e.activation(
                                    out=pE[:], in_=r0sp[ltt], func=AF.Square,
                                    accum_out=small[:, 3:4]), {"dv": d3, "ac": em.cnt["ac"]})
                                a2_ = em("ac", lambda e: e.activation(
                                    out=pF[:], in_=pB[:], func=AF.Square,
                                    accum_out=small[:, 4:5]), {"ac": a1})
                                d4 = em("dv", lambda e: e.tensor_add(
                                    out=small[:, 5:6], in0=small[:, 3:4], in1=small[:, 4:5]),
                                    {"ac": a2_, "dv": d3})
                                d5 = em("dv", lambda e: e.tensor_scalar(
                                    out=small[:, 6:7], in0=small[:, 2:3], scalar1=1.0 / D,
                                    scalar2=None, op0=OP.mult), {"dv": d4})
                                d6 = em("dv", lambda e: e.tensor_mul(
                                    out=small[:, 7:8], in0=small[:, 6:7], in1=small[:, 6:7]),
                                    {"dv": d5})
                                d7 = em("dv", lambda e: e.scalar_tensor_tensor(
                                    out=small[:, 8:9], in0=small[:, 5:6], scalar=1.0 / D,
                                    in1=small[:, 7:8], op0=OP.mult, op1=OP.subtract), {"dv": d6})
                                a3 = em("ac", lambda e: e.activation(
                                    out=small[:, 9:10], in_=small[:, 8:9], func=AF.Sqrt,
                                    bias=eps_sb[:, 1:2]), {"dv": d7})
                                d8 = em("dv", lambda e: e.reciprocal(
                                    out=small[:, 10:11], in_=small[:, 9:10]), {"ac": a3})
                                d9 = em("dv", lambda e, ccol=ccol: e.tensor_mul(
                                    out=small[:, 11:12], in0=small[:, 10:11],
                                    in1=c_sb[:, ccol:ccol + 1]), {"dv": d8})
                                d10 = em("dv", lambda e: e.tensor_scalar(
                                    out=small[:, 12:13], in0=small[:, 6:7],
                                    scalar1=small[:, 11:12], scalar2=-1.0,
                                    op0=OP.mult, op1=OP.mult), {"dv": d9})
                                a4 = em("ac", lambda e, ltt=ltt: e.activation(
                                    out=yc_bf[:, 0:512], in_=r0sp[ltt], func=AF.Identity,
                                    scale=small[:, 11:12], bias=small[:, 12:13]), {"dv": d10})
                                a5 = em("ac", lambda e: e.activation(
                                    out=yc_bf[:, 512:1024], in_=pB[:], func=AF.Identity,
                                    scale=small[:, 11:12], bias=small[:, 12:13]), {"ac": a4})
                                d11 = em("dv", lambda e: e.tensor_mul(
                                    out=t2_bf, in0=yc_bf, in1=grep_bf[:]), {"ac": a5})
                                d12 = em("dv", lambda e, ltt=ltt, ccol=ccol: e.scalar_tensor_tensor(
                                    out=f_res[:, ltt * D:(ltt + 1) * D], in0=brep_bf[:],
                                    scalar=c_sb[:, ccol:ccol + 1],
                                    in1=f_res[:, ltt * D:(ltt + 1) * D],
                                    op0=OP.mult, op1=OP.add),
                                    {"dv": d11, "po": em.cnt["po"]})
                                po1 = em("po", lambda e, ltt=ltt: e.tensor_add(
                                    out=f_res[:, ltt * D:(ltt + 1) * D],
                                    in0=f_res[:, ltt * D:(ltt + 1) * D], in1=t2_bf),
                                    {"dv": d12})
                                epi_ac = em.cnt["ac"]
                                epi_dv = em.cnt["dv"]
                prev_exp = {"pe": em.cnt["pe"], "dv": em.cnt["dv"], "ac": em.cnt["ac"],
                            "gp": em.cnt["gp"], "sy": em.cnt["sy"]}

            po_all = em.cnt["po"]
            w_all = {"pe": em.cnt["pe"], "dv": em.cnt["dv"], "ac": em.cnt["ac"],
                     "gp": em.cnt["gp"], "po": po_all}
            syn = em("sy", lambda e: e.dma_start(
                out=wstage[:].rearrange("p (j n) -> p j n", n=1024),
                in_=wout_in.rearrange("(j p) n -> p j n", p=128)), w_all)
            dvw = em("dv", lambda e: e.tensor_copy(out=wout_bf[:, 0:4096],
                                                   in_=wstage[:, 0:4096]),
                     {"sy": syn, "dv": em.cnt["dv"], "ac": em.cnt["ac"], "po": po_all})
            acw = em("ac", lambda e: e.copy(out=wout_bf[:, 4096:8192],
                                            in_=wstage[:, 4096:8192]),
                     {"sy": syn, "ac": em.cnt["ac"]})
            dv_sel = dvw
            for r in range(4):
                for j2 in range(2):
                    jj = r * 2 + j2
                    sy_t = 0
                    for qq in range(4):
                        sy_t = em("sy", lambda e, r=r, j2=j2, qq=qq: e.dma_start_transpose(
                            out=tp4[:, qq * 2048:(qq + 1) * 2048],
                            in_=o_gath[r * T + qq * 2048: r * T + (qq + 1) * 2048,
                                       j2 * 128:(j2 + 1) * 128]),
                            {"cc": 1, "dv": dv_sel} if qq == 0 else None)
                    dv_sel = em("dv", lambda e, jj=jj: e.tensor_scalar(
                        out=oT[:, jj * 2048:(jj + 1) * 2048], in0=tp4[:, 0:2048],
                        scalar1=qsel_sb[:, 0:1], scalar2=None, op0=OP.mult),
                        {"sy": sy_t, "dv": dv_sel})
                    for qq in range(1, 4):
                        dv_sel = em("dv", lambda e, qq=qq, jj=jj: e.scalar_tensor_tensor(
                            out=oT[:, jj * 2048:(jj + 1) * 2048],
                            in0=tp4[:, qq * 2048:(qq + 1) * 2048],
                            scalar=qsel_sb[:, qq:qq + 1],
                            in1=oT[:, jj * 2048:(jj + 1) * 2048],
                            op0=OP.mult, op1=OP.add), {"dv": dv_sel})
            prev_ac = acw
            prev_dm = 0
            prev_da = 0
            sy_out = 0
            for ltt in range(QNT):
                par = ltt % 2
                sy_x = em("sy", lambda e, ltt=ltt, par=par: e.dma_start(
                    out=xs[par][:], in_=xq[ltt * 128:(ltt + 1) * 128, :]),
                    {"dv": max(dvw, prev_da), "ac": acw})
                for half in range(2):
                    for j in range(8):
                        em("pe", lambda e, j=j, half=half, ltt=ltt: e.matmul(
                            pA[:, 0:512],
                            lhsT=oT[:, j * 2048 + ltt * 128: j * 2048 + ltt * 128 + 128],
                            rhs=wout_bf[:, j * 1024 + half * 512: j * 1024 + half * 512 + 512],
                            start=(j == 0), stop=(j == 7)),
                            {"dv": dv_sel, "ac": prev_ac} if j == 0 else None)
                    prev_ac = em("ac", lambda e, half=half: e.copy(
                        out=s32t[:, half * 512:(half + 1) * 512], in_=pA[:]),
                        {"pe": em.cnt["pe"], "dv": prev_dm})
                a_g = em("ac", lambda e, ltt=ltt: e.activation(
                    out=p45_g, in_=f_res[:, ltt * D:(ltt + 1) * D], func=AF.Silu),
                    {"po": po_all, "dv": prev_dm})
                prev_dm = em("dv", lambda e: e.tensor_mul(out=scr32, in0=s32t, in1=p45_g),
                             {"ac": a_g, "dv": prev_da})
                prev_da = em("dv", lambda e, par=par: e.tensor_add(
                    out=scr32b, in0=xs[par][:], in1=scr32),
                    {"dv": prev_dm, "sy": max(sy_x, sy_out)})
                sy_out = em("sy", lambda e, ltt=ltt: e.dma_start(
                    out=out[ltt * 128:(ltt + 1) * 128, :], in_=scr32b), {"dv": prev_da})
            em.wait("sy", {"sy": em.cnt["sy"], "gp": em.cnt["gp"]})

        @block.tensor
        def _(tensor):
            walk(Em("pe", tensor, sems))

        @block.scalar
        def _(scalar):
            walk(Em("ac", scalar, sems))

        @block.vector
        def _(vector):
            walk(Em("dv", vector, sems))

        @block.gpsimd
        def _(gpsimd):
            walk(Em("po", gpsimd, sems))

        @block.sync
        def _(sync):
            walk(Em("sy", sync, sems))

    return nc


_NC_CACHE = [None]


def kernel(L0_attn_x, L0_attn_Wq, L0_attn_Wk, L0_attn_Wv, L0_attn_Wbeta, L0_attn_b_beta,
           L0_attn_Wout, L0_ffn_Wr, L0_ffn_br, L0_ffn_Wi, L0_ffn_bi, L0_ffn_Wg, L0_ffn_bg,
           L0_ffn_Wo, L0_ffn_bo, L0_ffn_gamma, L0_ffn_beta):
    x = np.ascontiguousarray(np.asarray(L0_attn_x, dtype=np.float32))
    f32 = lambda a: np.ascontiguousarray(np.asarray(a, dtype=np.float32))
    Wq, Wk, Wv = f32(L0_attn_Wq), f32(L0_attn_Wk), f32(L0_attn_Wv)
    Wbeta, b_beta, Wout = f32(L0_attn_Wbeta), f32(L0_attn_b_beta), f32(L0_attn_Wout)
    Wr, br = f32(L0_ffn_Wr), f32(L0_ffn_br)
    Wi, bi, Wg, bg = f32(L0_ffn_Wi), f32(L0_ffn_bi), f32(L0_ffn_Wg), f32(L0_ffn_bg)
    Wo, bo = f32(L0_ffn_Wo), f32(L0_ffn_bo)
    gamma, beta = f32(L0_ffn_gamma), f32(L0_ffn_beta)

    if _NC_CACHE[0] is None:
        _NC_CACHE[0] = build_nc()
    nc = _NC_CACHE[0]

    wr_r = np.ascontiguousarray(Wr.reshape(8, 128, E).transpose(1, 0, 2).reshape(128, 64))
    bi_r = np.ascontiguousarray(bi.reshape(E, 32, 128).transpose(2, 0, 1).reshape(128, E * 32))
    bg_r = np.ascontiguousarray(bg.reshape(E, 32, 128).transpose(2, 0, 1).reshape(128, E * 32))
    bo_r = np.ascontiguousarray(bo.reshape(1, E * D))
    gam_r = np.ascontiguousarray(gamma.reshape(1, E * D))
    bet_r = np.ascontiguousarray(beta.reshape(1, E * D))
    br_r = np.ascontiguousarray(br.reshape(1, E))
    ident = np.eye(128, dtype=np.float32).astype(ml_dtypes.bfloat16)
    ident32 = np.eye(128, dtype=np.float32)
    ones_bf = np.ones((1, 128), np.float32).astype(ml_dtypes.bfloat16)
    ones_f = np.ones((1, 128), np.float32)
    m_tril = -np.tril(np.ones((128, 128), np.float32), -1)
    m_triu = np.triu(np.ones((128, 128), np.float32), 0)

    common = dict(wout_in=Wout, wr_in=wr_r, br_in=br_r, wi_in=Wi, wg_in=Wg, wo_in=Wo,
                  bi_in=bi_r, bg_in=bg_r, bo_in=bo_r, gam_in=gam_r, bet_in=bet_r,
                  ident_in=ident, ident32_in=ident32, ones32_in=ones_f,
                  mtril_in=m_tril, mtriu_in=m_triu, ones1_in=ones_bf)
    in_maps = []
    for c in range(8):
        b, r = c // 4, c % 4
        qs = np.zeros((128, 4), np.float32)
        qs[:, r] = 1.0
        m = dict(common)
        m["xb"] = x[b]
        m["xq"] = np.ascontiguousarray(x[b, r * QTOK:(r + 1) * QTOK])
        m["qsel"] = qs
        m["wq_in"] = np.ascontiguousarray(Wq[:, r * HD:(r + 1) * HD])
        m["wk_in"] = np.ascontiguousarray(Wk[:, r * HD:(r + 1) * HD])
        m["wvb_in"] = np.ascontiguousarray(
            np.concatenate([Wv[:, r * HD:(r + 1) * HD],
                            Wbeta[:, r * HC:(r + 1) * HC]], axis=1))
        m["bbeta_in"] = np.ascontiguousarray(b_beta[r * HC:(r + 1) * HC].reshape(1, HC))
        in_maps.append(m)

    res = run_bass_kernel_spmd(nc, in_maps, core_ids=list(range(8)))
    outp = np.empty((B, T, D), np.float32)
    for c in range(8):
        b, r = c // 4, c % 4
        outp[b, r * QTOK:(r + 1) * QTOK] = res.results[c]["out"]
    return outp


# version 13
# speedup vs baseline: 1.4651x; 1.0892x over previous
import numpy as np
import ml_dtypes
import concourse.bass as bass
import concourse.mybir as mybir
from concourse.bass_utils import run_bass_kernel_spmd
from contextlib import ExitStack

F32 = mybir.dt.float32
BF16 = mybir.dt.bfloat16
AF = mybir.ActivationFunctionType
OP = mybir.AluOpType
AX = mybir.AxisListType

B, T, D, H, Dh, E, FF, C = 2, 8192, 1024, 16, 64, 8, 4096, 128
HC = 4
HD = HC * Dh
NCH = T // C
NT = T // 128
QTOK, QNT = 2048, 16
LN_EPS, K_EPS, FFN_EPS = 1e-5, 1e-12, 1e-12


class Em:
    M = {"pe": "pe", "ac": "ac", "dv": "dv", "po": "po", "sy": "sy", "gp": "po"}

    def __init__(self, stream, eng, sems):
        self.stream = stream
        self.eng = eng
        self.sems = sems
        self.cnt = {k: 0 for k in ("pe", "ac", "dv", "po", "sy", "gp")}

    def __call__(self, which, fn, waits=None):
        inc = 16 if which in ("sy", "gp") else 1
        self.cnt[which] += inc
        if Em.M[which] == self.stream:
            if waits:
                for sname, n in waits.items():
                    if n > 0:
                        self.eng.wait_ge(self.sems[sname], n)
            fn(self.eng).then_inc(self.sems[which], inc)
        return self.cnt[which]

    def wait(self, stream, waits):
        if stream == self.stream and waits:
            for sname, n in waits.items():
                if n > 0:
                    self.eng.wait_ge(self.sems[sname], n)


def build_nc():
    nc = bass.Bass("TRN2", target_bir_lowering=False, debug=False)

    xb = nc.dram_tensor("xb", [T, D], F32, kind="ExternalInput").ap()
    xq = nc.dram_tensor("xq", [QTOK, D], F32, kind="ExternalInput").ap()
    qsel = nc.dram_tensor("qsel", [128, 4], F32, kind="ExternalInput").ap()
    wq_in = nc.dram_tensor("wq_in", [D, HD], F32, kind="ExternalInput").ap()
    wk_in = nc.dram_tensor("wk_in", [D, HD], F32, kind="ExternalInput").ap()
    wvb_in = nc.dram_tensor("wvb_in", [D, HD + HC], F32, kind="ExternalInput").ap()
    wout_in = nc.dram_tensor("wout_in", [D, D], F32, kind="ExternalInput").ap()
    bbeta_in = nc.dram_tensor("bbeta_in", [1, HC], F32, kind="ExternalInput").ap()
    wr_in = nc.dram_tensor("wr_in", [128, 64], F32, kind="ExternalInput").ap()
    br_in = nc.dram_tensor("br_in", [1, E], F32, kind="ExternalInput").ap()
    wi_in = nc.dram_tensor("wi_in", [E, D, FF], F32, kind="ExternalInput").ap()
    wg_in = nc.dram_tensor("wg_in", [E, D, FF], F32, kind="ExternalInput").ap()
    wo_in = nc.dram_tensor("wo_in", [E, FF, D], F32, kind="ExternalInput").ap()
    bi_in = nc.dram_tensor("bi_in", [128, E * 32], F32, kind="ExternalInput").ap()
    bg_in = nc.dram_tensor("bg_in", [128, E * 32], F32, kind="ExternalInput").ap()
    bo_in = nc.dram_tensor("bo_in", [1, E * D], F32, kind="ExternalInput").ap()
    gam_in = nc.dram_tensor("gam_in", [1, E * D], F32, kind="ExternalInput").ap()
    bet_in = nc.dram_tensor("bet_in", [1, E * D], F32, kind="ExternalInput").ap()
    ident_in = nc.dram_tensor("ident_in", [128, 128], BF16, kind="ExternalInput").ap()
    ident32_in = nc.dram_tensor("ident32_in", [128, 128], F32, kind="ExternalInput").ap()
    ones32_in = nc.dram_tensor("ones32_in", [1, 128], F32, kind="ExternalInput").ap()
    mtril_in = nc.dram_tensor("mtril_in", [128, 128], F32, kind="ExternalInput").ap()
    mtriu_in = nc.dram_tensor("mtriu_in", [128, 128], F32, kind="ExternalInput").ap()
    ones1_in = nc.dram_tensor("ones1_in", [1, 128], BF16, kind="ExternalInput").ap()

    out = nc.dram_tensor("out", [QTOK, D], F32, kind="ExternalOutput").ap()

    q_scr = nc.dram_tensor("q_scr", [T, HD], BF16, kind="Internal").ap()
    k_scr = nc.dram_tensor("k_scr", [T, HD], BF16, kind="Internal").ap()
    v_scr = nc.dram_tensor("v_scr", [T, HD], BF16, kind="Internal").ap()
    o_loc = nc.dram_tensor("o_loc", [T, HD], BF16, kind="Internal").ap()
    o_gath = nc.dram_tensor("o_gath", [4 * T, HD], BF16, kind="Internal").ap()
    h_scr = nc.dram_tensor("h_scr", [4, 32, 128, 512], BF16, kind="Internal").ap()
    wo_bscr = nc.dram_tensor("wo_bscr", [E, FF, D], BF16, kind="Internal").ap()

    with ExitStack() as ctx:
        def sbt(name, shape, dt):
            return ctx.enter_context(nc.sbuf_tensor(name, shape, dt))

        def pst(name, shape, dt):
            return ctx.enter_context(nc.psum_tensor(name, shape, dt))

        sid = sbt("sid", [128, 128], BF16)
        sid32 = sbt("sid32", [128, 128], F32)
        ones32 = sbt("ones32", [1, 128], F32)
        mtril = sbt("mtril", [128, 128], F32)
        mtriu = sbt("mtriu", [128, 128], F32)
        ones1 = sbt("ones1", [1, 128], BF16)
        wr_f = sbt("wr_f", [128, 64], F32)
        br_f = sbt("br_f", [1, E], F32)
        bb_bf = sbt("bb_bf", [1, HC], BF16)
        bi_sb = sbt("bi_sb", [128, E * 32], F32)
        bg_sb = sbt("bg_sb", [128, E * 32], F32)
        grow_row = sbt("grow_row", [1, 3072], F32)
        eps_sb = sbt("eps_sb", [128, 2], F32)
        bo_row = sbt("bo_row", [1, D], BF16)
        qsel_sb = sbt("qsel_sb", [128, 4], F32)
        beta_all = sbt("beta_all", [128, NCH * HC], F32)
        c_sb = sbt("c_sb", [128, QNT * E], F32)
        small = sbt("small", [128, 64], F32)
        xnt_res = sbt("xnt_res", [128, 8 * QTOK], BF16)
        f_res = sbt("f_res", [128, QNT * D], BF16)
        grep_bf = sbt("grep_bf", [128, D], BF16)
        brep_bf = sbt("brep_bf", [128, D], BF16)

        ar_f = sbt("ar_f", [128, 8192], F32)
        ar_bf = sbt("ar_bf", [128, 36864], BF16)

        wstage = ar_f[:, 0:8192]
        xs = [ar_f[:, i * 1024:(i + 1) * 1024] for i in range(2)]
        scr32 = ar_f[:, 2048:3072]
        scr32b = ar_f[:, 3072:4096]
        s_f32 = ar_f[0:64, 4096:4352]
        s32t = ar_f[:, 4096:5120]
        wst_wo = [ar_f[:, 4096 + i * 2048: 4096 + (i + 1) * 2048] for i in range(2)]
        r0sp = [ar_f[:, i * 512:(i + 1) * 512] for i in range(16)]

        wq_bf = ar_bf[:, 0:2048]
        wk_bf = ar_bf[:, 2048:4096]
        wvb_bf = ar_bf[:, 4096:6176]
        xn_bf = ar_bf[:, 6176:7200]
        xqt_bf = ar_bf[:, 7200:8224]
        qe_bf = ar_bf[:, 8224:8480]
        ve_bf = ar_bf[:, 8480:8736]
        kn_bf = ar_bf[:, 8736:8992]
        wob_st = [ar_bf[:, 8992 + i * 2048: 8992 + (i + 1) * 2048] for i in range(2)]
        P2O = 16384
        ktm = [ar_bf[:, P2O + i * 256: P2O + (i + 1) * 256] for i in range(2)]
        qtm = [ar_bf[:, P2O + 512 + i * 256: P2O + 512 + (i + 1) * 256] for i in range(2)]
        vtm = [ar_bf[:, P2O + 1024 + i * 256: P2O + 1024 + (i + 1) * 256] for i in range(2)]
        kt2 = [ar_bf[:, P2O + 1536 + i * 256: P2O + 1536 + (i + 1) * 256] for i in range(2)]
        qt2 = [ar_bf[:, P2O + 2048 + i * 256: P2O + 2048 + (i + 1) * 256] for i in range(2)]
        kht = [ar_bf[:, P2O + 2560 + i * 256: P2O + 2560 + (i + 1) * 256] for i in range(2)]
        gm2 = [ar_bf[:, P2O + 3072 + i * 512: P2O + 3072 + (i + 1) * 512] for i in range(2)]
        wkp = [ar_bf[:, P2O + 4096 + i * 512: P2O + 4096 + (i + 1) * 512] for i in range(2)]
        a2 = [ar_bf[:, P2O + 5120 + i * 512: P2O + 5120 + (i + 1) * 512] for i in range(2)]
        b2 = [ar_bf[:, P2O + 6144 + i * 512: P2O + 6144 + (i + 1) * 512] for i in range(2)]
        x2 = [ar_bf[:, P2O + 7168 + i * 512: P2O + 7168 + (i + 1) * 512] for i in range(2)]
        x0b = ar_bf[:, P2O + 8192: P2O + 8704]
        u2 = ar_bf[:, P2O + 8704: P2O + 8960]
        o2 = [ar_bf[:, P2O + 8960 + i * 256: P2O + 8960 + (i + 1) * 256] for i in range(2)]
        s2bf = ar_bf[:, P2O + 9472: P2O + 9600]
        wout_bf = ar_bf[:, 0:8192]
        tp4 = ar_bf[:, 8192:16384]
        oT = ar_bf[:, 16384:32768]
        p45_g = ar_bf[:, 32768:33792]
        wi_bf = ar_bf[:, 0:8192]
        wg_bf = ar_bf[:, 8192:16384]
        slab = ar_bf[:, 16384:32768]
        wo_half = ar_bf[:, 0:16384]
        hgs = [ar_bf[:, 30720 + i * 512: 30720 + (i + 1) * 512] for i in range(4)]
        hs_bf = ar_bf[:, 34816:35328]
        h1_bf = [ar_bf[:, 35328 + i * 512: 35328 + (i + 1) * 512] for i in range(2)]
        yc_bf = ar_bf[:, 32768:33792]
        t2_bf = ar_bf[:, 33792:34816]

        pA = pst("pA", [128, 512], F32)
        pB = pst("pB", [128, 512], F32)
        pE = pst("pE", [128, 512], F32)
        pF = pst("pF", [128, 512], F32)
        pX = pst("pX", [128, 512], F32)
        pG = pst("pG", [128, 512], F32)
        pS2 = pst("pS2", [64, 256], F32)
        pCb = pst("pCb", [128, 1024], BF16)

        sems = {}
        for name in ("pe", "ac", "dv", "po", "sy", "gp", "cc"):
            sems[name] = ctx.enter_context(nc.semaphore(name))
        block = ctx.enter_context(nc.Block())

        wo_rounds = [(e_i, jp) for e_i in range(E) for jp in range(16)]

        def walk(em: Em):
            for dst, src in ((sid, ident_in), (sid32, ident32_in), (ones32, ones32_in),
                             (mtril, mtril_in), (mtriu, mtriu_in), (ones1, ones1_in),
                             (wr_f, wr_in), (br_f, br_in), (bi_sb, bi_in), (bg_sb, bg_in),
                             (qsel_sb, qsel)):
                em("sy", lambda e, d=dst, s=src: e.dma_start(out=d[:], in_=s))
            em("dv", lambda e: e.memset(eps_sb[:, 0:1], LN_EPS))
            em("dv", lambda e: e.memset(eps_sb[:, 1:2], FFN_EPS))
            syn = em("sy", lambda e: e.dma_start(out=wstage[0:1, 0:HC], in_=bbeta_in))
            dvn = em("dv", lambda e: e.tensor_copy(out=bb_bf[:], in_=wstage[0:1, 0:HC]),
                     {"sy": syn})
            prev = {"dv": dvn, "ac": 0}
            for win, wbf, wdt in ((wq_in, wq_bf, HD), (wk_in, wk_bf, HD),
                                  (wvb_in, wvb_bf, HD + HC)):
                syn = em("sy", lambda e, w=win, wdt=wdt: e.dma_start(
                    out=wstage[:, 0:8 * wdt].rearrange("p (j n) -> p j n", n=wdt),
                    in_=w.rearrange("(j p) n -> p j n", p=128)), dict(prev))
                dvh = em("dv", lambda e, wb=wbf, wdt=wdt: e.tensor_copy(
                    out=wb[:, 0:4 * wdt], in_=wstage[:, 0:4 * wdt]), {"sy": syn})
                ach = em("ac", lambda e, wb=wbf, wdt=wdt: e.copy(
                    out=wb[:, 4 * wdt:8 * wdt], in_=wstage[:, 4 * wdt:8 * wdt]),
                    {"sy": syn})
                prev = {"dv": dvh, "ac": ach}

            busy = {0: {"dv": prev["dv"], "ac": prev["ac"]},
                    1: {"dv": prev["dv"], "ac": prev["ac"]}}
            wo_gp = {"ld": 0, "st": 0, "dv": prev["dv"], "ac": prev["ac"]}
            wo_round_idx = [0]

            def wo_precast_round():
                if wo_round_idx[0] >= len(wo_rounds):
                    return
                ri = wo_round_idx[0]
                wo_round_idx[0] += 1
                e_i, jp = wo_rounds[ri]
                par = ri % 2
                gpl = em("gp", lambda e: e.dma_start(
                    out=wst_wo[par].rearrange("p (j n) -> p j n", n=1024),
                    in_=wo_in[e_i].rearrange("(j p) n -> p j n", p=128)[:, jp * 2:jp * 2 + 2, :]),
                    {"dv": wo_gp["dv"], "ac": wo_gp["ac"]})
                dvc = em("dv", lambda e: e.tensor_copy(
                    out=wob_st[par][:, 0:1024], in_=wst_wo[par][:, 0:1024]),
                    {"gp": gpl})
                acc = em("ac", lambda e: e.copy(
                    out=wob_st[par][:, 1024:2048], in_=wst_wo[par][:, 1024:2048]),
                    {"gp": gpl})
                gps = em("gp", lambda e: e.dma_start(
                    out=wo_bscr[e_i].rearrange("(j p) n -> p j n", p=128)[:, jp * 2:jp * 2 + 2, :],
                    in_=wob_st[par].rearrange("p (j n) -> p j n", n=1024)),
                    {"dv": dvc, "ac": acc})
                if par == 1:
                    wo_gp["dv"] = dvc
                    wo_gp["ac"] = acc
                    wo_gp["st"] = gps

            def ln_tile(src, row, par, dst_bf):
                w = dict(busy[par])
                syn = em("sy", lambda e: e.dma_start(out=xs[par][:], in_=src[row:row + 128, :]),
                         w if w else None)
                dv1 = em("dv", lambda e: e.reduce_sum(out=small[:, 0:1], in_=xs[par][:],
                                                      axis=AX.X),
                         {"sy": syn, "dv": em.cnt["dv"], "ac": em.cnt["ac"]})
                ac1 = em("ac", lambda e: e.activation(out=scr32[:], in_=xs[par][:],
                                                      func=AF.Square, accum_out=small[:, 1:2]),
                         {"sy": syn, "dv": dv1})
                dv2 = em("dv", lambda e: e.tensor_scalar(out=small[:, 2:3], in0=small[:, 0:1],
                                                         scalar1=1.0 / D, scalar2=None,
                                                         op0=OP.mult), {"dv": dv1, "ac": ac1})
                dv3 = em("dv", lambda e: e.tensor_mul(out=small[:, 3:4], in0=small[:, 2:3],
                                                      in1=small[:, 2:3]), {"dv": dv2})
                dv4 = em("dv", lambda e: e.scalar_tensor_tensor(
                    out=small[:, 4:5], in0=small[:, 1:2], scalar=1.0 / D, in1=small[:, 3:4],
                    op0=OP.mult, op1=OP.subtract), {"dv": dv3})
                ac2 = em("ac", lambda e: e.activation(out=small[:, 5:6], in_=small[:, 4:5],
                                                      func=AF.Sqrt, bias=eps_sb[:, 0:1]), {"dv": dv4})
                dv5 = em("dv", lambda e: e.reciprocal(out=small[:, 6:7], in_=small[:, 5:6]),
                         {"ac": ac2})
                dv6 = em("dv", lambda e: e.tensor_scalar(out=small[:, 7:8], in0=small[:, 2:3],
                                                         scalar1=small[:, 6:7], scalar2=-1.0,
                                                         op0=OP.mult, op1=OP.mult), {"dv": dv5})
                ac3 = em("ac", lambda e: e.activation(out=dst_bf, in_=xs[par][:], func=AF.Identity,
                                                      bias=small[:, 7:8], scale=small[:, 6:7]),
                         {"dv": dv6})
                return syn, ac3

            def transpose8(src_bf, dsts, f32=False):
                last = 0
                pbank = pA if f32 else pCb
                ii = sid32 if f32 else sid
                for halfi in range(2):
                    w_ac = em.cnt["ac"]
                    for j in range(4):
                        jj = halfi * 4 + j
                        em("pe", lambda e, j=j, jj=jj: e.transpose(
                            pbank[:, j * 128:(j + 1) * 128],
                            src_bf[:, jj * 128:(jj + 1) * 128], ii[:]),
                            {"ac": w_ac, "dv": em.cnt["dv"]} if j == 0 else None)
                    pen = em.cnt["pe"]
                    for j in range(4):
                        jj = halfi * 4 + j
                        last = em("ac", lambda e, j=j, jj=jj: e.copy(
                            out=dsts[jj], in_=pbank[:, j * 128:(j + 1) * 128]),
                            {"pe": pen} if j == 0 else None)
                return last

            st_q = st_v = st_k = 0
            dv_kn_prev = 0
            for tt in range(NT):
                row = tt * 128
                par = tt % 2
                syn, ac_ln = ln_tile(xb, row, par, xn_bf)
                ac_tr = transpose8(xn_bf, [xqt_bf[:, j * 128:(j + 1) * 128] for j in range(8)])

                for j in range(8):
                    em("pe", lambda e, j=j: e.matmul(
                        pA[:, 0:HD], lhsT=xqt_bf[:, j * 128:(j + 1) * 128],
                        rhs=wq_bf[:, j * HD:(j + 1) * HD],
                        start=(j == 0), stop=(j == 7)),
                        {"ac": ac_tr} if j == 0 else None)
                ac_q = em("ac", lambda e: e.copy(out=qe_bf[:], in_=pA[:, 0:HD]),
                          {"pe": em.cnt["pe"], "sy": st_q})
                st_q = em("sy", lambda e, row=row: e.dma_start(
                    out=q_scr[row:row + 128, :], in_=qe_bf), {"ac": ac_q})

                for j in range(7):
                    em("pe", lambda e, j=j: e.matmul(
                        pX[:, 0:HD + HC], lhsT=xqt_bf[:, j * 128:(j + 1) * 128],
                        rhs=wvb_bf[:, j * (HD + HC):(j + 1) * (HD + HC)],
                        start=(j == 0), stop=False),
                        {"ac": ac_tr} if j == 0 else None)
                em("pe", lambda e: e.matmul(pX[:, HD:HD + HC], lhsT=ones1[:], rhs=bb_bf[:],
                                            start=False, stop=False))
                em("pe", lambda e: e.matmul(
                    pX[:, 0:HD + HC], lhsT=xqt_bf[:, 7 * 128:8 * 128],
                    rhs=wvb_bf[:, 7 * (HD + HC):8 * (HD + HC)],
                    start=False, stop=True))
                ac_v = em("ac", lambda e: e.copy(out=ve_bf[:], in_=pX[:, 0:HD]),
                          {"pe": em.cnt["pe"], "sy": st_v})
                em("ac", lambda e, tt=tt: e.activation(
                    out=beta_all[:, tt * HC:(tt + 1) * HC], in_=pX[:, HD:HD + HC],
                    func=AF.Sigmoid))
                st_v = em("sy", lambda e, row=row: e.dma_start(
                    out=v_scr[row:row + 128, :], in_=ve_bf), {"ac": em.cnt["ac"]})

                for j in range(8):
                    em("pe", lambda e, j=j: e.matmul(
                        pB[:, 0:HD], lhsT=xqt_bf[:, j * 128:(j + 1) * 128],
                        rhs=wk_bf[:, j * HD:(j + 1) * HD],
                        start=(j == 0), stop=(j == 7)),
                        {"dv": dv_kn_prev} if j == 0 else None)
                pe_k = em.cnt["pe"]
                ac_k1 = em("ac", lambda e: e.activation(out=scr32b[:, 0:HD], in_=pB[:, 0:HD],
                                                        func=AF.Square),
                           {"pe": pe_k, "dv": em.cnt["dv"]})
                dvk = em("dv", lambda e: e.reduce_sum(
                    out=small[:, 16:16 + HC],
                    in_=scr32b[:, 0:HD].rearrange("p (h d) -> p h d", d=64),
                    axis=AX.X), {"ac": ac_k1, "dv": em.cnt["dv"]})
                dvk = em("dv", lambda e: e.tensor_single_scalar(
                    out=small[:, 20:20 + HC], in_=small[:, 16:16 + HC],
                    scalar=K_EPS * K_EPS, op=OP.max), {"dv": dvk})
                ack = em("ac", lambda e: e.activation(out=small[:, 24:24 + HC],
                                                      in_=small[:, 20:20 + HC],
                                                      func=AF.Sqrt), {"dv": dvk})
                dvk = em("dv", lambda e: e.reciprocal(out=small[:, 28:28 + HC],
                                                      in_=small[:, 24:24 + HC]), {"ac": ack})
                for hh in range(HC):
                    dvk = em("dv", lambda e, hh=hh: e.tensor_scalar(
                        out=kn_bf[:, hh * 64:(hh + 1) * 64], in0=pB[:, hh * 64:(hh + 1) * 64],
                        scalar1=small[:, 28 + hh:29 + hh], scalar2=None, op0=OP.mult),
                        {"dv": dvk, "sy": st_k} if hh == 0 else None)
                dv_kn_prev = dvk
                st_k = em("sy", lambda e, row=row: e.dma_start(
                    out=k_scr[row:row + 128, :], in_=kn_bf), {"dv": dvk})
                wo_precast_round()
                wo_precast_round()
                busy[par] = {"pe": em.cnt["pe"], "dv": em.cnt["dv"], "ac": em.cnt["ac"]}
            st_qkv = max(st_q, st_v, st_k)

            for ltt in range(QNT):
                row = ltt * 128
                par = ltt % 2
                syn, ac_ln = ln_tile(xq, row, par, xn_bf)
                ac_tr = transpose8(
                    xn_bf, [xnt_res[:, j * QTOK + ltt * 128: j * QTOK + ltt * 128 + 128]
                            for j in range(8)])
                dvf = em("dv", lambda e: e.tensor_scalar(
                    out=scr32[:], in0=xs[par][:], scalar1=small[:, 2:3], scalar2=small[:, 6:7],
                    op0=OP.subtract, op1=OP.mult), {"dv": em.cnt["dv"], "ac": ac_ln})
                ac_t32 = transpose8(scr32, [scr32b[:, j * 128:(j + 1) * 128] for j in range(8)],
                                    f32=True)
                for j in range(8):
                    em("pe", lambda e, j=j: e.matmul(
                        pX[:, 0:E], lhsT=scr32b[:, j * 128:(j + 1) * 128],
                        rhs=wr_f[:, j * E:(j + 1) * E],
                        start=(j == 0), stop=False), {"ac": ac_t32} if j == 0 else None)
                em("pe", lambda e: e.matmul(pX[:, 0:E], lhsT=ones32[:], rhs=br_f[:],
                                            start=False, stop=True))
                ac_lg = em("ac", lambda e: e.copy(out=small[:, 8:16], in_=pX[:, 0:E]),
                           {"pe": em.cnt["pe"]})
                d1 = em("dv", lambda e: e.reduce_max(out=small[:, 16:17], in_=small[:, 8:16],
                                                     axis=AX.X), {"ac": ac_lg, "dv": em.cnt["dv"]})
                d2 = em("dv", lambda e: e.tensor_scalar(out=small[:, 17:18], in0=small[:, 16:17],
                                                        scalar1=-1.0, scalar2=None, op0=OP.mult),
                        {"dv": d1})
                d3 = em("dv", lambda e: e.tensor_scalar(out=small[:, 24:32], in0=small[:, 8:16],
                                                        scalar1=small[:, 16:17], scalar2=None,
                                                        op0=OP.is_equal), {"dv": d2})
                d4 = em("dv", lambda e: e.scalar_tensor_tensor(
                    out=small[:, 32:40], in0=small[:, 24:32], scalar=-1e30, in1=small[:, 8:16],
                    op0=OP.mult, op1=OP.add), {"dv": d3})
                d5 = em("dv", lambda e: e.reduce_max(out=small[:, 18:19], in_=small[:, 32:40],
                                                     axis=AX.X), {"dv": d4})
                d6 = em("dv", lambda e: e.tensor_scalar(out=small[:, 40:48], in0=small[:, 8:16],
                                                        scalar1=small[:, 18:19], scalar2=None,
                                                        op0=OP.is_ge), {"dv": d5})
                a7 = em("ac", lambda e: e.activation(out=small[:, 48:56], in_=small[:, 8:16],
                                                     func=AF.Exp, bias=small[:, 17:18]),
                        {"dv": d6})
                d8 = em("dv", lambda e: e.tensor_mul(out=small[:, 56:64], in0=small[:, 48:56],
                                                     in1=small[:, 40:48]), {"ac": a7})
                d9 = em("dv", lambda e: e.reduce_sum(out=small[:, 19:20], in_=small[:, 56:64],
                                                     axis=AX.X), {"dv": d8})
                d10 = em("dv", lambda e: e.reciprocal(out=small[:, 20:21], in_=small[:, 19:20]),
                         {"dv": d9})
                em("dv", lambda e, ltt=ltt: e.tensor_scalar(
                    out=c_sb[:, ltt * E:(ltt + 1) * E], in0=small[:, 56:64],
                    scalar1=small[:, 20:21], scalar2=None, op0=OP.mult), {"dv": d10})
                wo_precast_round()
                wo_precast_round()
                busy[par] = {"pe": em.cnt["pe"], "dv": em.cnt["dv"], "ac": em.cnt["ac"]}
            while wo_round_idx[0] < len(wo_rounds):
                wo_precast_round()

            def kt_h(buf, h):
                return buf[(h % 2) * 64:(h % 2) * 64 + 64, (h // 2) * 128:(h // 2) * 128 + 128]

            def s2_h(h):
                return s2bf[(h % 2) * 64:(h % 2) * 64 + 64, (h // 2) * 64:(h // 2) * 64 + 64]

            def ag_slot(h, off):
                return (pA if h % 2 == 0 else pB)[:, off + (h // 2) * 128:
                                                  off + (h // 2) * 128 + 128]

            def pu_slot(h):
                return (pG if h % 2 == 0 else pX)[:, (h // 2) * 64:(h // 2) * 64 + 64]

            first_w = {"sy": st_qkv, "dv": em.cnt["dv"], "ac": em.cnt["ac"],
                       "pe": em.cnt["pe"]}
            pre_info = {}
            ser_info = {}
            sstate = {"s_ready": 0, "dv_s": 0, "dv_u": 0, "ac_o": 0}
            gp_o = {}

            def pre(g):
                par = g % 2
                info = {}
                w0 = {"sy": st_qkv}
                if g >= 2:
                    w0["pe"] = ser_info[g - 2]["pe"]
                else:
                    w0.update({k: first_w[k] for k in ("pe", "dv", "ac")})
                row = g * C
                syn_k = em("sy", lambda e, row=row, par=par: e.dma_start(
                    out=ktm[par][:], in_=k_scr[row:row + C, :]), w0)
                syn_q = em("sy", lambda e, row=row, par=par: e.dma_start(
                    out=qtm[par][:], in_=q_scr[row:row + C, :]))
                syn_v = em("sy", lambda e, row=row, par=par: e.dma_start(
                    out=vtm[par][:], in_=v_scr[row:row + C, :]))
                info["syn_k"] = syn_k
                wx0 = {"sy": syn_v, "dv": em.cnt["dv"]}
                if g >= 1:
                    wx0["pe"] = pre_info[g - 1]["pe_k0"]
                dvx0 = 0
                for h in range(HC):
                    dvx0 = em("dv", lambda e, h=h, g=g, par=par: e.tensor_scalar(
                        out=x0b[:, h * 128:h * 128 + 64],
                        in0=vtm[par][:, h * 64:(h + 1) * 64],
                        scalar1=beta_all[:, g * HC + h:g * HC + h + 1],
                        scalar2=None, op0=OP.mult), wx0 if h == 0 else None)
                    dvx0 = em("dv", lambda e, h=h, g=g, par=par: e.tensor_scalar(
                        out=x0b[:, h * 128 + 64:h * 128 + 128],
                        in0=ktm[par][:, h * 64:(h + 1) * 64],
                        scalar1=beta_all[:, g * HC + h:g * HC + h + 1],
                        scalar2=None, op0=OP.mult), None)
                info["dvx0"] = dvx0
                wtr = {"sy": syn_q,
                       "ac": pre_info[g - 1]["ac_kht"] if g >= 1 else first_w["ac"]}
                for bb in range(2):
                    em("pe", lambda e, bb=bb, par=par: e.transpose(
                        pCb[:, bb * 128:(bb + 1) * 128],
                        ktm[par][:, bb * 128:(bb + 1) * 128], sid[:]),
                        wtr if bb == 0 else None)
                pen = em.cnt["pe"]
                ac_kt = em("ac", lambda e, par=par: e.copy(out=kt2[par][:], in_=pCb[:, 0:256]),
                           {"pe": pen})
                for bb in range(2):
                    em("pe", lambda e, bb=bb, par=par: e.transpose(
                        pCb[:, 256 + bb * 128: 256 + (bb + 1) * 128],
                        qtm[par][:, bb * 128:(bb + 1) * 128], sid[:]),
                        {"ac": ac_kt} if bb == 0 else None)
                pen = em.cnt["pe"]
                ac_qt = em("ac", lambda e, par=par: e.copy(out=qt2[par][:], in_=pCb[:, 256:512]),
                           {"pe": pen})
                info["ac_qt"] = ac_qt
                wag = {"ac": ac_kt, "dv": sstate["dv_u"]}
                if g >= 1:
                    wag["dv"] = max(wag["dv"], pre_info[g - 1]["dv_g"])
                if g < 2:
                    wag["dv"] = max(wag["dv"], first_w["dv"])
                for h in range(HC):
                    em("pe", lambda e, h=h, par=par: e.matmul(
                        ag_slot(h, 0),
                        lhsT=kt_h(kt2[par], h), rhs=kt_h(kt2[par], h),
                        start=True, stop=True),
                        wag if h == 0 else None)
                for h in range(HC):
                    em("pe", lambda e, h=h, par=par: e.matmul(
                        ag_slot(h, 256),
                        lhsT=kt_h(kt2[par], h), rhs=kt_h(qt2[par], h),
                        start=True, stop=True),
                        {"ac": ac_qt} if h == 0 else None)
                pe_ag = em.cnt["pe"]
                wm = {"pe": pe_ag, "dv": em.cnt["dv"]}
                if g >= 1:
                    wm["pe"] = max(pe_ag, pre_info[g - 1]["pe_all"])
                dv_a = 0
                for h in range(HC):
                    dv_a = em("dv", lambda e, h=h, g=g: e.scalar_tensor_tensor(
                        out=a2[0][:, h * 128:(h + 1) * 128],
                        in0=ag_slot(h, 0),
                        scalar=beta_all[:, g * HC + h:g * HC + h + 1],
                        in1=mtril[:], op0=OP.mult, op1=OP.mult),
                        wm if h == 0 else None)
                dv_g = dv_a
                for h in range(HC):
                    dv_g = em("dv", lambda e, h=h, par=par: e.tensor_tensor(
                        out=gm2[par][:, h * 128:(h + 1) * 128],
                        in0=ag_slot(h, 256), in1=mtriu[:], op=OP.mult),
                        {"pe": pe_ag, "dv": dv_g} if h == 0 else None)
                info["dv_g"] = dv_g
                wb0 = {"dv": dv_a, "ac": ac_qt}
                for h in range(HC):
                    em("pe", lambda e, h=h: e.transpose(
                        pCb[:, 512 + h * 128: 512 + (h + 1) * 128],
                        a2[0][:, h * 128:(h + 1) * 128], sid[:]),
                        wb0 if h == 0 else None)
                pen = em.cnt["pe"]
                ac_b = em("ac", lambda e: e.copy(out=b2[0][:], in_=pCb[:, 512:1024]),
                          {"pe": pen})
                xcur = x0b
                ac_prev_ab = ac_b
                dv_prev_x = dvx0
                dv_prev_b = 0
                pe_k0 = 0
                for k in range(7):
                    p = k % 2
                    xdst = wkp[par] if k == 6 else x2[k % 2]
                    wk_ = {"dv": max(dv_prev_x, dv_prev_b), "ac": ac_prev_ab}
                    for h in range(HC):
                        em("pe", lambda e, h=h, xcur=xcur: e.matmul(
                            pX[:, h * 128:(h + 1) * 128], lhsT=sid[:],
                            rhs=xcur[:, h * 128:(h + 1) * 128],
                            start=(h == 0), stop=False),
                            wk_ if h == 0 else None)
                    for h in range(HC):
                        em("pe", lambda e, h=h, p=p, xcur=xcur: e.matmul(
                            pX[:, h * 128:(h + 1) * 128],
                            lhsT=b2[p][:, h * 128:(h + 1) * 128],
                            rhs=xcur[:, h * 128:(h + 1) * 128],
                            start=False, stop=(h == 3)))
                    pen = em.cnt["pe"]
                    if k == 0:
                        pe_k0 = pen
                    dv_prev_x = em("dv", lambda e, xdst=xdst: e.tensor_copy(
                        out=xdst[:], in_=pX[:]), {"pe": pen})
                    if k < 6:
                        if k < 5:
                            for h in range(HC):
                                em("pe", lambda e, h=h, p=p: e.matmul(
                                    pE[:, h * 128:(h + 1) * 128],
                                    lhsT=b2[p][:, h * 128:(h + 1) * 128],
                                    rhs=a2[p][:, h * 128:(h + 1) * 128],
                                    start=(h == 0), stop=(h == 3)),
                                    {"ac": ac_prev_ab} if h == 0 else None)
                        for h in range(HC):
                            em("pe", lambda e, h=h, p=p: e.matmul(
                                pF[:, h * 128:(h + 1) * 128],
                                lhsT=a2[p][:, h * 128:(h + 1) * 128],
                                rhs=b2[p][:, h * 128:(h + 1) * 128],
                                start=(h == 0), stop=(h == 3)),
                                {"ac": ac_prev_ab} if h == 0 else None)
                        pen = em.cnt["pe"]
                        if k < 5:
                            ac_prev_ab = em("ac", lambda e, p=p: e.copy(
                                out=a2[1 - p][:], in_=pE[:]), {"pe": pen})
                        dv_prev_b = em("dv", lambda e, p=p: e.tensor_copy(
                            out=b2[1 - p][:], in_=pF[:]), {"pe": pen, "dv": dv_prev_b})
                        if k >= 5:
                            ac_prev_ab = em.cnt["ac"]
                    xcur = xdst
                info["dv_wkp"] = dv_prev_x
                wkh = {"dv": dv_prev_x, "ac": ac_b}
                for i, h in enumerate((0, 2, 1, 3)):
                    em("pe", lambda e, h=h, par=par: e.transpose(
                        pCb[(h % 2) * 64:(h % 2) * 64 + 64,
                            512 + (h // 2) * 128: 512 + (h // 2 + 1) * 128],
                        wkp[par][:, h * 128 + 64:h * 128 + 128], sid[:]),
                        wkh if i == 0 else None)
                pen = em.cnt["pe"]
                ac_kht = em("ac", lambda e, par=par: e.copy(
                    out=kht[par][:], in_=pCb[:, 512:768]), {"pe": pen})
                info["ac_kht"] = ac_kht
                info["pe_k0"] = pe_k0
                info["pe_all"] = em.cnt["pe"]
                pre_info[g] = info

            def khtv(par, h):
                return kht[par][(h % 2) * 64:(h % 2) * 64 + 64,
                                (h // 2) * 128:(h // 2 + 1) * 128]

            def ser(g):
                par = g % 2
                info = pre_info[g]
                if g == 0:
                    dv_u = 0
                    for h in range(HC):
                        dv_u = em("dv", lambda e, h=h, par=par: e.tensor_copy(
                            out=u2[:, h * 64:(h + 1) * 64],
                            in_=wkp[par][:, h * 128:h * 128 + 64]),
                            {"dv": info["dv_wkp"]} if h == 0 else None)
                else:
                    wu = {"ac": max(info["ac_kht"], sstate["s_ready"]),
                          "dv": max(sstate["dv_u"],
                                    pre_info[g + 1]["dv_wkp"] if (g + 1) in pre_info else 0)}
                    for h in range(HC):
                        em("pe", lambda e, h=h, par=par: e.matmul(
                            pu_slot(h),
                            lhsT=khtv(par, h), rhs=s2_h(h),
                            start=True, stop=True),
                            wu if h == 0 else None)
                    pen = em.cnt["pe"]
                    dv_u = 0
                    for h in range(HC):
                        dv_u = em("dv", lambda e, h=h, par=par: e.tensor_tensor(
                            out=u2[:, h * 64:(h + 1) * 64],
                            in0=wkp[par][:, h * 128:h * 128 + 64],
                            in1=pu_slot(h),
                            op=OP.subtract),
                            {"pe": pen, "dv": info["dvx0"]} if h == 0 else None)
                sstate["dv_u"] = dv_u
                wo_ = {"dv": max(dv_u, info["dv_g"]), "ac": sstate["ac_o"]}
                for h in range(HC):
                    em("pe", lambda e, h=h, par=par, g=g: e.matmul(
                        pG[:, 256 + h * 64: 256 + (h + 1) * 64],
                        lhsT=gm2[par][:, h * 128:(h + 1) * 128],
                        rhs=u2[:, h * 64:(h + 1) * 64],
                        start=True, stop=(g == 0)),
                        wo_ if h == 0 else None)
                    if g > 0:
                        em("pe", lambda e, h=h, par=par: e.matmul(
                            pG[:, 256 + h * 64: 256 + (h + 1) * 64],
                            lhsT=kt_h(qt2[par], h), rhs=s2_h(h),
                            start=False, stop=True))
                pen = em.cnt["pe"]
                wao = {"pe": pen}
                if g >= 2:
                    wao["gp"] = gp_o[g - 2]
                ac_o = em("ac", lambda e, par=par: e.copy(
                    out=o2[par][:], in_=pG[:, 256:512]), wao)
                sstate["ac_o"] = ac_o
                gp_o[g] = em("gp", lambda e, g=g, par=par: e.dma_start(
                    out=o_loc[g * C:(g + 1) * C, :], in_=o2[par][:]), {"ac": ac_o})
                ws = {"dv": max(dv_u, sstate["dv_s"])}
                for h in range(HC):
                    em("pe", lambda e, h=h, par=par: e.matmul(
                        pS2[0:64, h * 64:(h + 1) * 64],
                        lhsT=ktm[par][:, h * 64:(h + 1) * 64],
                        rhs=u2[:, h * 64:(h + 1) * 64],
                        start=(h == 0), stop=(h == 3)),
                        ws if h == 0 else None)
                pen = em.cnt["pe"]
                if g == 0:
                    dv_s = em("dv", lambda e: e.tensor_copy(out=s_f32, in_=pS2[:]),
                              {"pe": pen})
                else:
                    dv_s = em("dv", lambda e: e.tensor_add(out=s_f32, in0=s_f32, in1=pS2[:]),
                              {"pe": pen})
                sstate["dv_s"] = dv_s
                sv = s_f32.rearrange("p (h d) -> p h d", d=64)
                em("ac", lambda e, sv=sv: e.copy(
                    out=s2bf[0:64, :].rearrange("p (b d) -> p b d", d=64),
                    in_=sv[:, 0:HC:2, :]), {"dv": dv_s})
                sstate["s_ready"] = em("ac", lambda e, sv=sv: e.copy(
                    out=s2bf[64:128, :].rearrange("p (b d) -> p b d", d=64),
                    in_=sv[:, 1:HC:2, :]), None)
                ser_info[g] = {"pe": em.cnt["pe"], "dv": em.cnt["dv"], "ac": em.cnt["ac"]}

            pre(0)
            pre(1)
            for g in range(NCH):
                ser(g)
                if g + 2 < NCH:
                    pre(g + 2)
            gp_o_all = em.cnt["gp"]

            if em.stream == "po":
                em.eng.wait_ge(sems["gp"], gp_o_all)
                em.eng.collective_compute(
                    "AllGather", mybir.AluOpType.bypass,
                    replica_groups=[[0, 1, 2, 3], [4, 5, 6, 7]],
                    ins=[o_loc.opt()], outs=[o_gath.opt()],
                ).then_inc(sems["cc"])

            dv_f0 = em("dv", lambda e: e.memset(f_res[:], 0.0),
                       {"ac": em.cnt["ac"], "gp": wo_gp["st"]})
            prev_exp = {"pe": em.cnt["pe"], "dv": dv_f0, "ac": em.cnt["ac"],
                        "gp": em.cnt["gp"], "sy": em.cnt["sy"]}
            gp_h = 0
            hpar = 0
            for e_i in range(E):
                wrow = dict(prev_exp)
                em("sy", lambda e, e_i=e_i: e.dma_start(
                    out=grow_row[0:1, 0:1024], in_=gam_in[0:1, e_i * D:(e_i + 1) * D]), wrow)
                em("sy", lambda e, e_i=e_i: e.dma_start(
                    out=grow_row[0:1, 1024:2048], in_=bet_in[0:1, e_i * D:(e_i + 1) * D]))
                sy_r = em("sy", lambda e, e_i=e_i: e.dma_start(
                    out=grow_row[0:1, 2048:3072], in_=bo_in[0:1, e_i * D:(e_i + 1) * D]))
                dv_bo = em("dv", lambda e: e.tensor_copy(
                    out=bo_row[:], in_=grow_row[0:1, 2048:3072]),
                    {"sy": sy_r, "dv": prev_exp["dv"]})
                for half in range(2):
                    em("pe", lambda e, half=half: e.matmul(
                        pE[:, 0:512], lhsT=ones32[:],
                        rhs=grow_row[0:1, half * 512: half * 512 + 512],
                        start=True, stop=True),
                        {"sy": sy_r, "pe": prev_exp["pe"], "ac": prev_exp["ac"],
                         "dv": prev_exp["dv"]} if half == 0 else {"ac": em.cnt["ac"]})
                    em("ac", lambda e, half=half: e.copy(
                        out=grep_bf[:, half * 512:(half + 1) * 512], in_=pE[:]),
                        {"pe": em.cnt["pe"]})
                    em("pe", lambda e, half=half: e.matmul(
                        pE[:, 0:512], lhsT=ones32[:],
                        rhs=grow_row[0:1, 1024 + half * 512: 1024 + half * 512 + 512],
                        start=True, stop=True), {"ac": em.cnt["ac"]})
                    em("ac", lambda e, half=half: e.copy(
                        out=brep_bf[:, half * 512:(half + 1) * 512], in_=pE[:]),
                        {"pe": em.cnt["pe"]})
                ac_gb = em.cnt["ac"]

                prev_cast = {"dv": em.cnt["dv"], "ac": em.cnt["ac"]}
                for fc in range(4):
                    casts = []
                    for gate in range(2):
                        win = wi_in if gate == 0 else wg_in
                        wbf = wi_bf if gate == 0 else wg_bf
                        syw = em("sy", lambda e, win=win, e_i=e_i, fc=fc: e.dma_start(
                            out=wstage[:].rearrange("p (j n) -> p j n", n=1024),
                            in_=win[e_i].rearrange("(j p) n -> p j n", p=128)
                            [:, :, fc * 1024:(fc + 1) * 1024]),
                            {"dv": prev_cast["dv"], "ac": prev_cast["ac"],
                             "pe": em.cnt["pe"]})
                        dvc = em("dv", lambda e, wbf=wbf: e.tensor_copy(
                            out=wbf[:, 0:4096], in_=wstage[:, 0:4096]), {"sy": syw})
                        acc = em("ac", lambda e, wbf=wbf: e.copy(
                            out=wbf[:, 4096:8192], in_=wstage[:, 4096:8192]), {"sy": syw})
                        prev_cast = {"dv": dvc, "ac": acc}
                        casts.append((dvc, acc))
                    (dv_wi, ac_wi), (dv_wg, ac_wg) = casts
                    for st in range(4):
                        for ffb in range(8):
                            for j in range(8):
                                em("pe", lambda e, j=j, ffb=ffb, st=st: e.matmul(
                                    pE[:, 0:512],
                                    lhsT=wi_bf[:, j * 1024 + ffb * 128: j * 1024 + (ffb + 1) * 128],
                                    rhs=xnt_res[:, j * QTOK + st * 512: j * QTOK + (st + 1) * 512],
                                    start=(j == 0), stop=(j == 7)),
                                    {"dv": dv_wi, "ac": max(ac_wi, em.cnt["ac"])} if j == 0 else None)
                            pe_hi = em.cnt["pe"]
                            for j in range(8):
                                em("pe", lambda e, j=j, ffb=ffb, st=st: e.matmul(
                                    pF[:, 0:512],
                                    lhsT=wg_bf[:, j * 1024 + ffb * 128: j * 1024 + (ffb + 1) * 128],
                                    rhs=xnt_res[:, j * QTOK + st * 512: j * QTOK + (st + 1) * 512],
                                    start=(j == 0), stop=(j == 7)),
                                    {"dv": dv_wg, "ac": ac_wg} if j == 0 else None)
                            pe_hg = em.cnt["pe"]
                            bcol = e_i * 32 + fc * 8 + ffb
                            a1 = em("ac", lambda e, bcol=bcol: e.activation(
                                out=hgs[0], in_=pE[:], func=AF.Gelu,
                                bias=bi_sb[:, bcol:bcol + 1]), {"pe": pe_hi, "dv": em.cnt["dv"]})
                            a2_ = em("ac", lambda e, bcol=bcol: e.activation(
                                out=hs_bf, in_=pF[:], func=AF.Sigmoid,
                                bias=bg_sb[:, bcol:bcol + 1]), {"pe": pe_hg})
                            dvh = em("dv", lambda e, hpar=hpar: e.tensor_mul(
                                out=h1_bf[hpar], in0=hgs[0], in1=hs_bf),
                                {"ac": a2_, "gp": gp_h, "dv": em.cnt["dv"]})
                            gp_h = em("gp", lambda e, st=st, fc=fc, ffb=ffb, hpar=hpar, e_i=e_i:
                                      e.dma_start(out=h_scr[st, fc * 8 + ffb], in_=h1_bf[hpar]),
                                      {"dv": dvh})
                            hpar = 1 - hpar

                epi_dv = em.cnt["dv"]
                epi_ac = em.cnt["ac"]
                for dh in range(2):
                    wo_sub = {}
                    for kq in range(4):
                        wo_sub[kq] = em("sy", lambda e, dh=dh, e_i=e_i, kq=kq: e.dma_start(
                            out=wo_half[:, kq * 4096:(kq + 1) * 4096]
                            .rearrange("p (f n) -> p f n", n=512),
                            in_=wo_bscr[e_i].rearrange("(j p) n -> p j n", p=128)
                            [:, kq * 8:(kq + 1) * 8, dh * 512:(dh + 1) * 512]),
                            {"pe": em.cnt["pe"], "dv": em.cnt["dv"], "ac": em.cnt["ac"],
                             "gp": wo_gp["st"]} if kq == 0 else None)
                    if dh == 1:
                        epi_b = {0: epi_ac, 1: epi_ac}
                        epi_bd = {0: epi_dv, 1: epi_dv}
                    for st in range(4):
                        sl_sub = {}
                        for kq in range(4):
                            sl_sub[kq] = em("sy", lambda e, st=st, kq=kq: e.dma_start(
                                out=slab[:, kq * 4096:(kq + 1) * 4096]
                                .rearrange("p (f n) -> p f n", n=512),
                                in_=h_scr[st, kq * 8:(kq + 1) * 8].rearrange("f p n -> p f n")),
                                {"gp": gp_h, "pe": em.cnt["pe"]} if kq == 0 else None)
                        for t4 in range(4):
                            ltt = st * 4 + t4
                            if dh == 0:
                                pbank = pA
                                bid = 0
                            else:
                                bid = ltt % 2
                                pbank = pB if bid == 0 else pA
                            for kq in range(4):
                                for j in range(8):
                                    fb = kq * 8 + j
                                    w = None
                                    if kq == 0 and j == 0:
                                        w = {"sy": max(sl_sub[0], wo_sub[0]),
                                             "ac": (epi_ac if dh == 0 else epi_b[bid]),
                                             "dv": (epi_dv if dh == 0 else epi_bd[bid])}
                                    elif j == 0 and t4 == 0:
                                        w = {"sy": max(sl_sub[kq], wo_sub[kq])}
                                    em("pe", lambda e, fb=fb, t4=t4, pbank=pbank: e.matmul(
                                        pbank[:, 0:512],
                                        lhsT=slab[:, fb * 512 + t4 * 128: fb * 512 + t4 * 128 + 128],
                                        rhs=wo_half[:, fb * 512:(fb + 1) * 512],
                                        start=(kq == 0 and j == 0), stop=False), w)
                            for j4 in range(4):
                                em("pe", lambda e, j4=j4, dh=dh, ltt=ltt, pbank=pbank: e.matmul(
                                    pbank[:, j4 * 128:(j4 + 1) * 128],
                                    lhsT=xnt_res[:, (dh * 4 + j4) * QTOK + ltt * 128:
                                                 (dh * 4 + j4) * QTOK + ltt * 128 + 128],
                                    rhs=sid[:], start=False, stop=False))
                            em("pe", lambda e, dh=dh, pbank=pbank: e.matmul(
                                pbank[:, 0:512], lhsT=ones1[:],
                                rhs=bo_row[0:1, dh * 512: dh * 512 + 512],
                                start=False, stop=True))
                            pe_r = em.cnt["pe"]
                            if dh == 0:
                                epi_ac = em("ac", lambda e, ltt=ltt: e.copy(
                                    out=r0sp[ltt], in_=pA[:]), {"pe": pe_r})
                                epi_dv = em.cnt["dv"]
                            else:
                                ccol = ltt * E + e_i
                                d1 = em("dv", lambda e, ltt=ltt: e.reduce_sum(
                                    out=small[:, 0:1], in_=r0sp[ltt], axis=AX.X),
                                    {"pe": pe_r, "dv": em.cnt["dv"], "ac": em.cnt["ac"]})
                                d2 = em("dv", lambda e, pbank=pbank: e.reduce_sum(
                                    out=small[:, 1:2], in_=pbank[:], axis=AX.X), {"dv": d1})
                                d3 = em("dv", lambda e: e.tensor_add(
                                    out=small[:, 2:3], in0=small[:, 0:1], in1=small[:, 1:2]),
                                    {"dv": d2})
                                a1 = em("ac", lambda e, ltt=ltt: e.activation(
                                    out=pE[:], in_=r0sp[ltt], func=AF.Square,
                                    accum_out=small[:, 3:4]), {"dv": d3, "ac": em.cnt["ac"]})
                                a2_ = em("ac", lambda e, pbank=pbank: e.activation(
                                    out=pF[:], in_=pbank[:], func=AF.Square,
                                    accum_out=small[:, 4:5]), {"ac": a1})
                                d4 = em("dv", lambda e: e.tensor_add(
                                    out=small[:, 5:6], in0=small[:, 3:4], in1=small[:, 4:5]),
                                    {"ac": a2_, "dv": d3})
                                d5 = em("dv", lambda e: e.tensor_scalar(
                                    out=small[:, 6:7], in0=small[:, 2:3], scalar1=1.0 / D,
                                    scalar2=None, op0=OP.mult), {"dv": d4})
                                d6 = em("dv", lambda e: e.tensor_mul(
                                    out=small[:, 7:8], in0=small[:, 6:7], in1=small[:, 6:7]),
                                    {"dv": d5})
                                d7 = em("dv", lambda e: e.scalar_tensor_tensor(
                                    out=small[:, 8:9], in0=small[:, 5:6], scalar=1.0 / D,
                                    in1=small[:, 7:8], op0=OP.mult, op1=OP.subtract), {"dv": d6})
                                a3 = em("ac", lambda e: e.activation(
                                    out=small[:, 9:10], in_=small[:, 8:9], func=AF.Sqrt,
                                    bias=eps_sb[:, 1:2]), {"dv": d7})
                                d8 = em("dv", lambda e: e.reciprocal(
                                    out=small[:, 10:11], in_=small[:, 9:10]), {"ac": a3})
                                d9 = em("dv", lambda e, ccol=ccol: e.tensor_mul(
                                    out=small[:, 11:12], in0=small[:, 10:11],
                                    in1=c_sb[:, ccol:ccol + 1]), {"dv": d8})
                                d10 = em("dv", lambda e: e.tensor_scalar(
                                    out=small[:, 12:13], in0=small[:, 6:7],
                                    scalar1=small[:, 11:12], scalar2=-1.0,
                                    op0=OP.mult, op1=OP.mult), {"dv": d9})
                                a4 = em("ac", lambda e, ltt=ltt: e.activation(
                                    out=yc_bf[:, 0:512], in_=r0sp[ltt], func=AF.Identity,
                                    scale=small[:, 11:12], bias=small[:, 12:13]), {"dv": d10})
                                a5 = em("ac", lambda e, pbank=pbank: e.activation(
                                    out=yc_bf[:, 512:1024], in_=pbank[:], func=AF.Identity,
                                    scale=small[:, 11:12], bias=small[:, 12:13]), {"ac": a4})
                                d11 = em("dv", lambda e: e.tensor_mul(
                                    out=t2_bf, in0=yc_bf, in1=grep_bf[:]), {"ac": a5})
                                d12 = em("dv", lambda e, ltt=ltt, ccol=ccol: e.scalar_tensor_tensor(
                                    out=f_res[:, ltt * D:(ltt + 1) * D], in0=brep_bf[:],
                                    scalar=c_sb[:, ccol:ccol + 1],
                                    in1=f_res[:, ltt * D:(ltt + 1) * D],
                                    op0=OP.mult, op1=OP.add),
                                    {"dv": d11, "po": em.cnt["po"]})
                                po1 = em("po", lambda e, ltt=ltt: e.tensor_add(
                                    out=f_res[:, ltt * D:(ltt + 1) * D],
                                    in0=f_res[:, ltt * D:(ltt + 1) * D], in1=t2_bf),
                                    {"dv": d12})
                                epi_b[bid] = em.cnt["ac"]
                                epi_bd[bid] = em.cnt["dv"]
                prev_exp = {"pe": em.cnt["pe"], "dv": em.cnt["dv"], "ac": em.cnt["ac"],
                            "gp": em.cnt["gp"], "sy": em.cnt["sy"]}

            po_all = em.cnt["po"]
            w_all = {"pe": em.cnt["pe"], "dv": em.cnt["dv"], "ac": em.cnt["ac"],
                     "gp": em.cnt["gp"], "po": po_all}
            syn = em("sy", lambda e: e.dma_start(
                out=wstage[:].rearrange("p (j n) -> p j n", n=1024),
                in_=wout_in.rearrange("(j p) n -> p j n", p=128)), w_all)
            dvw = em("dv", lambda e: e.tensor_copy(out=wout_bf[:, 0:4096],
                                                   in_=wstage[:, 0:4096]),
                     {"sy": syn, "dv": em.cnt["dv"], "ac": em.cnt["ac"], "po": po_all})
            acw = em("ac", lambda e: e.copy(out=wout_bf[:, 4096:8192],
                                            in_=wstage[:, 4096:8192]),
                     {"sy": syn, "ac": em.cnt["ac"]})
            dv_sel = dvw
            for r in range(4):
                for j2 in range(2):
                    jj = r * 2 + j2
                    sy_t = 0
                    for qq in range(4):
                        sy_t = em("sy", lambda e, r=r, j2=j2, qq=qq: e.dma_start_transpose(
                            out=tp4[:, qq * 2048:(qq + 1) * 2048],
                            in_=o_gath[r * T + qq * 2048: r * T + (qq + 1) * 2048,
                                       j2 * 128:(j2 + 1) * 128]),
                            {"cc": 1, "dv": dv_sel} if qq == 0 else None)
                    dv_sel = em("dv", lambda e, jj=jj: e.tensor_scalar(
                        out=oT[:, jj * 2048:(jj + 1) * 2048], in0=tp4[:, 0:2048],
                        scalar1=qsel_sb[:, 0:1], scalar2=None, op0=OP.mult),
                        {"sy": sy_t, "dv": dv_sel})
                    for qq in range(1, 4):
                        dv_sel = em("dv", lambda e, qq=qq, jj=jj: e.scalar_tensor_tensor(
                            out=oT[:, jj * 2048:(jj + 1) * 2048],
                            in0=tp4[:, qq * 2048:(qq + 1) * 2048],
                            scalar=qsel_sb[:, qq:qq + 1],
                            in1=oT[:, jj * 2048:(jj + 1) * 2048],
                            op0=OP.mult, op1=OP.add), {"dv": dv_sel})
            prev_ac = acw
            prev_dm = 0
            prev_da = 0
            sy_out = 0
            for ltt in range(QNT):
                par = ltt % 2
                sy_x = em("sy", lambda e, ltt=ltt, par=par: e.dma_start(
                    out=xs[par][:], in_=xq[ltt * 128:(ltt + 1) * 128, :]),
                    {"dv": max(dvw, prev_da), "ac": acw})
                for half in range(2):
                    for j in range(8):
                        em("pe", lambda e, j=j, half=half, ltt=ltt: e.matmul(
                            pA[:, 0:512],
                            lhsT=oT[:, j * 2048 + ltt * 128: j * 2048 + ltt * 128 + 128],
                            rhs=wout_bf[:, j * 1024 + half * 512: j * 1024 + half * 512 + 512],
                            start=(j == 0), stop=(j == 7)),
                            {"dv": dv_sel, "ac": prev_ac} if j == 0 else None)
                    prev_ac = em("ac", lambda e, half=half: e.copy(
                        out=s32t[:, half * 512:(half + 1) * 512], in_=pA[:]),
                        {"pe": em.cnt["pe"], "dv": prev_dm})
                a_g = em("ac", lambda e, ltt=ltt: e.activation(
                    out=p45_g, in_=f_res[:, ltt * D:(ltt + 1) * D], func=AF.Silu),
                    {"po": po_all, "dv": prev_dm})
                prev_dm = em("dv", lambda e: e.tensor_mul(out=scr32, in0=s32t, in1=p45_g),
                             {"ac": a_g, "dv": prev_da})
                prev_da = em("dv", lambda e, par=par: e.tensor_add(
                    out=scr32b, in0=xs[par][:], in1=scr32),
                    {"dv": prev_dm, "sy": max(sy_x, sy_out)})
                sy_out = em("sy", lambda e, ltt=ltt: e.dma_start(
                    out=out[ltt * 128:(ltt + 1) * 128, :], in_=scr32b), {"dv": prev_da})
            em.wait("sy", {"sy": em.cnt["sy"], "gp": em.cnt["gp"]})

        @block.tensor
        def _(tensor):
            walk(Em("pe", tensor, sems))

        @block.scalar
        def _(scalar):
            walk(Em("ac", scalar, sems))

        @block.vector
        def _(vector):
            walk(Em("dv", vector, sems))

        @block.gpsimd
        def _(gpsimd):
            walk(Em("po", gpsimd, sems))

        @block.sync
        def _(sync):
            walk(Em("sy", sync, sems))

    return nc


_NC_CACHE = [None]


def kernel(L0_attn_x, L0_attn_Wq, L0_attn_Wk, L0_attn_Wv, L0_attn_Wbeta, L0_attn_b_beta,
           L0_attn_Wout, L0_ffn_Wr, L0_ffn_br, L0_ffn_Wi, L0_ffn_bi, L0_ffn_Wg, L0_ffn_bg,
           L0_ffn_Wo, L0_ffn_bo, L0_ffn_gamma, L0_ffn_beta):
    x = np.ascontiguousarray(np.asarray(L0_attn_x, dtype=np.float32))
    f32 = lambda a: np.ascontiguousarray(np.asarray(a, dtype=np.float32))
    Wq, Wk, Wv = f32(L0_attn_Wq), f32(L0_attn_Wk), f32(L0_attn_Wv)
    Wbeta, b_beta, Wout = f32(L0_attn_Wbeta), f32(L0_attn_b_beta), f32(L0_attn_Wout)
    Wr, br = f32(L0_ffn_Wr), f32(L0_ffn_br)
    Wi, bi, Wg, bg = f32(L0_ffn_Wi), f32(L0_ffn_bi), f32(L0_ffn_Wg), f32(L0_ffn_bg)
    Wo, bo = f32(L0_ffn_Wo), f32(L0_ffn_bo)
    gamma, beta = f32(L0_ffn_gamma), f32(L0_ffn_beta)

    if _NC_CACHE[0] is None:
        _NC_CACHE[0] = build_nc()
    nc = _NC_CACHE[0]

    wr_r = np.ascontiguousarray(Wr.reshape(8, 128, E).transpose(1, 0, 2).reshape(128, 64))
    bi_r = np.ascontiguousarray(bi.reshape(E, 32, 128).transpose(2, 0, 1).reshape(128, E * 32))
    bg_r = np.ascontiguousarray(bg.reshape(E, 32, 128).transpose(2, 0, 1).reshape(128, E * 32))
    bo_r = np.ascontiguousarray(bo.reshape(1, E * D))
    gam_r = np.ascontiguousarray(gamma.reshape(1, E * D))
    bet_r = np.ascontiguousarray(beta.reshape(1, E * D))
    br_r = np.ascontiguousarray(br.reshape(1, E))
    ident = np.eye(128, dtype=np.float32).astype(ml_dtypes.bfloat16)
    ident32 = np.eye(128, dtype=np.float32)
    ones_bf = np.ones((1, 128), np.float32).astype(ml_dtypes.bfloat16)
    ones_f = np.ones((1, 128), np.float32)
    m_tril = -np.tril(np.ones((128, 128), np.float32), -1)
    m_triu = np.triu(np.ones((128, 128), np.float32), 0)

    common = dict(wout_in=Wout, wr_in=wr_r, br_in=br_r, wi_in=Wi, wg_in=Wg, wo_in=Wo,
                  bi_in=bi_r, bg_in=bg_r, bo_in=bo_r, gam_in=gam_r, bet_in=bet_r,
                  ident_in=ident, ident32_in=ident32, ones32_in=ones_f,
                  mtril_in=m_tril, mtriu_in=m_triu, ones1_in=ones_bf)
    in_maps = []
    for c in range(8):
        b, r = c // 4, c % 4
        qs = np.zeros((128, 4), np.float32)
        qs[:, r] = 1.0
        m = dict(common)
        m["xb"] = x[b]
        m["xq"] = np.ascontiguousarray(x[b, r * QTOK:(r + 1) * QTOK])
        m["qsel"] = qs
        m["wq_in"] = np.ascontiguousarray(Wq[:, r * HD:(r + 1) * HD])
        m["wk_in"] = np.ascontiguousarray(Wk[:, r * HD:(r + 1) * HD])
        m["wvb_in"] = np.ascontiguousarray(
            np.concatenate([Wv[:, r * HD:(r + 1) * HD],
                            Wbeta[:, r * HC:(r + 1) * HC]], axis=1))
        m["bbeta_in"] = np.ascontiguousarray(b_beta[r * HC:(r + 1) * HC].reshape(1, HC))
        in_maps.append(m)

    res = run_bass_kernel_spmd(nc, in_maps, core_ids=list(range(8)))
    outp = np.empty((B, T, D), np.float32)
    for c in range(8):
        b, r = c // 4, c % 4
        outp[b, r * QTOK:(r + 1) * QTOK] = res.results[c]["out"]
    return outp


# version 14 (final)
# speedup vs baseline: 1.5492x; 1.0574x over previous
import numpy as np
import ml_dtypes
import concourse.bass as bass
import concourse.mybir as mybir
from concourse.bass_utils import run_bass_kernel_spmd
from contextlib import ExitStack

F32 = mybir.dt.float32
BF16 = mybir.dt.bfloat16
AF = mybir.ActivationFunctionType
OP = mybir.AluOpType
AX = mybir.AxisListType

B, T, D, H, Dh, E, FF, C = 2, 8192, 1024, 16, 64, 8, 4096, 128
HC = 4
HD = HC * Dh
NCH = T // C
NT = T // 128
QTOK, QNT = 2048, 16
LN_EPS, K_EPS, FFN_EPS = 1e-5, 1e-12, 1e-12


class Em:
    M = {"pe": "pe", "ac": "ac", "dv": "dv", "po": "po", "sy": "sy", "gp": "po"}

    def __init__(self, stream, eng, sems):
        self.stream = stream
        self.eng = eng
        self.sems = sems
        self.cnt = {k: 0 for k in ("pe", "ac", "dv", "po", "sy", "gp")}

    def __call__(self, which, fn, waits=None):
        inc = 16 if which in ("sy", "gp") else 1
        self.cnt[which] += inc
        if Em.M[which] == self.stream:
            if waits:
                for sname, n in waits.items():
                    if n > 0:
                        self.eng.wait_ge(self.sems[sname], n)
            fn(self.eng).then_inc(self.sems[which], inc)
        return self.cnt[which]

    def wait(self, stream, waits):
        if stream == self.stream and waits:
            for sname, n in waits.items():
                if n > 0:
                    self.eng.wait_ge(self.sems[sname], n)


def build_nc():
    nc = bass.Bass("TRN2", target_bir_lowering=False, debug=False)

    xb = nc.dram_tensor("xb", [T, D], F32, kind="ExternalInput").ap()
    xq = nc.dram_tensor("xq", [QTOK, D], F32, kind="ExternalInput").ap()
    qsel = nc.dram_tensor("qsel", [128, 4], F32, kind="ExternalInput").ap()
    wq_in = nc.dram_tensor("wq_in", [D, HD], F32, kind="ExternalInput").ap()
    wk_in = nc.dram_tensor("wk_in", [D, HD], F32, kind="ExternalInput").ap()
    wvb_in = nc.dram_tensor("wvb_in", [D, HD + HC], F32, kind="ExternalInput").ap()
    wout_in = nc.dram_tensor("wout_in", [D, D], F32, kind="ExternalInput").ap()
    bbeta_in = nc.dram_tensor("bbeta_in", [1, HC], F32, kind="ExternalInput").ap()
    wr_in = nc.dram_tensor("wr_in", [128, 64], F32, kind="ExternalInput").ap()
    br_in = nc.dram_tensor("br_in", [1, E], F32, kind="ExternalInput").ap()
    wi_in = nc.dram_tensor("wi_in", [E, D, FF], F32, kind="ExternalInput").ap()
    wg_in = nc.dram_tensor("wg_in", [E, D, FF], F32, kind="ExternalInput").ap()
    wo_in = nc.dram_tensor("wo_in", [E, FF, D], F32, kind="ExternalInput").ap()
    bi_in = nc.dram_tensor("bi_in", [128, E * 32], F32, kind="ExternalInput").ap()
    bg_in = nc.dram_tensor("bg_in", [128, E * 32], F32, kind="ExternalInput").ap()
    bo_in = nc.dram_tensor("bo_in", [1, E * D], F32, kind="ExternalInput").ap()
    gam_in = nc.dram_tensor("gam_in", [1, E * D], F32, kind="ExternalInput").ap()
    bet_in = nc.dram_tensor("bet_in", [1, E * D], F32, kind="ExternalInput").ap()
    ident_in = nc.dram_tensor("ident_in", [128, 128], BF16, kind="ExternalInput").ap()
    ident32_in = nc.dram_tensor("ident32_in", [128, 128], F32, kind="ExternalInput").ap()
    ones32_in = nc.dram_tensor("ones32_in", [1, 128], F32, kind="ExternalInput").ap()
    mtril_in = nc.dram_tensor("mtril_in", [128, 128], F32, kind="ExternalInput").ap()
    mtriu_in = nc.dram_tensor("mtriu_in", [128, 128], F32, kind="ExternalInput").ap()
    ones1_in = nc.dram_tensor("ones1_in", [1, 128], BF16, kind="ExternalInput").ap()

    out = nc.dram_tensor("out", [QTOK, D], F32, kind="ExternalOutput").ap()

    q_scr = nc.dram_tensor("q_scr", [T, HD], BF16, kind="Internal").ap()
    k_scr = nc.dram_tensor("k_scr", [T, HD], BF16, kind="Internal").ap()
    v_scr = nc.dram_tensor("v_scr", [T, HD], BF16, kind="Internal").ap()
    o_loc = nc.dram_tensor("o_loc", [T, HD], BF16, kind="Internal").ap()
    o_gath = nc.dram_tensor("o_gath", [4 * T, HD], BF16, kind="Internal").ap()
    h_scr = nc.dram_tensor("h_scr", [4, 32, 128, 512], BF16, kind="Internal").ap()
    wo_bscr = nc.dram_tensor("wo_bscr", [E, FF, D], BF16, kind="Internal").ap()

    with ExitStack() as ctx:
        def sbt(name, shape, dt):
            return ctx.enter_context(nc.sbuf_tensor(name, shape, dt))

        def pst(name, shape, dt):
            return ctx.enter_context(nc.psum_tensor(name, shape, dt))

        sid = sbt("sid", [128, 128], BF16)
        sid32 = sbt("sid32", [128, 128], F32)
        ones32 = sbt("ones32", [1, 128], F32)
        mtril = sbt("mtril", [128, 128], F32)
        mtriu = sbt("mtriu", [128, 128], F32)
        ones1 = sbt("ones1", [1, 128], BF16)
        wr_f = sbt("wr_f", [128, 64], F32)
        br_f = sbt("br_f", [1, E], F32)
        bb_bf = sbt("bb_bf", [1, HC], BF16)
        bi_sb = sbt("bi_sb", [128, E * 32], F32)
        bg_sb = sbt("bg_sb", [128, E * 32], F32)
        grow_row = sbt("grow_row", [1, 3072], F32)
        eps_sb = sbt("eps_sb", [128, 2], F32)
        bo_row = sbt("bo_row", [1, D], BF16)
        qsel_sb = sbt("qsel_sb", [128, 4], F32)
        beta_all = sbt("beta_all", [128, NCH * HC], F32)
        c_sb = sbt("c_sb", [128, QNT * E], F32)
        small = sbt("small", [128, 64], F32)
        xnt_res = sbt("xnt_res", [128, 8 * QTOK], BF16)
        f_res = sbt("f_res", [128, QNT * D], BF16)
        grep_bf = sbt("grep_bf", [128, D], BF16)
        brep_bf = sbt("brep_bf", [128, D], BF16)

        ar_f = sbt("ar_f", [128, 8192], F32)
        ar_bf = sbt("ar_bf", [128, 36864], BF16)

        wstage = ar_f[:, 0:8192]
        xs = [ar_f[:, i * 1024:(i + 1) * 1024] for i in range(2)]
        scr32 = ar_f[:, 2048:3072]
        scr32b = ar_f[:, 3072:4096]
        s_f32 = ar_f[0:64, 4096:4352]
        s32t = ar_f[:, 4096:5120]
        wst_wo = [ar_f[:, 4096 + i * 2048: 4096 + (i + 1) * 2048] for i in range(2)]
        r0sp = [ar_f[:, i * 512:(i + 1) * 512] for i in range(16)]

        wq_bf = ar_bf[:, 0:2048]
        wk_bf = ar_bf[:, 2048:4096]
        wvb_bf = ar_bf[:, 4096:6176]
        xn_bf = ar_bf[:, 6176:7200]
        xqt_bf = ar_bf[:, 7200:8224]
        qe_bf = ar_bf[:, 8224:8480]
        ve_bf = ar_bf[:, 8480:8736]
        kn_bf = ar_bf[:, 8736:8992]
        wob_st = [ar_bf[:, 8992 + i * 2048: 8992 + (i + 1) * 2048] for i in range(2)]
        P2O = 16384
        ktm = [ar_bf[:, P2O + i * 256: P2O + (i + 1) * 256] for i in range(2)]
        qtm = [ar_bf[:, P2O + 512 + i * 256: P2O + 512 + (i + 1) * 256] for i in range(2)]
        vtm = [ar_bf[:, P2O + 1024 + i * 256: P2O + 1024 + (i + 1) * 256] for i in range(2)]
        kt2 = [ar_bf[:, P2O + 1536 + i * 256: P2O + 1536 + (i + 1) * 256] for i in range(2)]
        qt2 = [ar_bf[:, P2O + 2048 + i * 256: P2O + 2048 + (i + 1) * 256] for i in range(2)]
        kht = [ar_bf[:, P2O + 2560 + i * 256: P2O + 2560 + (i + 1) * 256] for i in range(2)]
        gm2 = [ar_bf[:, P2O + 3072 + i * 512: P2O + 3072 + (i + 1) * 512] for i in range(2)]
        wkp = [ar_bf[:, P2O + 4096 + i * 512: P2O + 4096 + (i + 1) * 512] for i in range(2)]
        a2 = [ar_bf[:, P2O + 5120 + i * 512: P2O + 5120 + (i + 1) * 512] for i in range(2)]
        b2 = [ar_bf[:, P2O + 6144 + i * 512: P2O + 6144 + (i + 1) * 512] for i in range(2)]
        x2 = [ar_bf[:, P2O + 7168 + i * 512: P2O + 7168 + (i + 1) * 512] for i in range(2)]
        x0b = ar_bf[:, P2O + 8192: P2O + 8704]
        u2 = ar_bf[:, P2O + 8704: P2O + 8960]
        o2 = [ar_bf[:, P2O + 8960 + i * 256: P2O + 8960 + (i + 1) * 256] for i in range(2)]
        s2bf = ar_bf[:, P2O + 9472: P2O + 9600]
        wout_bf = ar_bf[:, 0:8192]
        tp4 = ar_bf[:, 8192:16384]
        oT = ar_bf[:, 16384:32768]
        p45_g = ar_bf[:, 32768:33792]
        wi_bf = ar_bf[:, 0:8192]
        wg_bf = ar_bf[:, 8192:16384]
        slab = ar_bf[:, 16384:32768]
        wo_half = ar_bf[:, 0:16384]
        hgs = [ar_bf[:, 30720 + i * 512: 30720 + (i + 1) * 512] for i in range(4)]
        hs_bf = ar_bf[:, 34816:35328]
        h1_bf = [ar_bf[:, 35328 + i * 512: 35328 + (i + 1) * 512] for i in range(2)]
        yc_bf = ar_bf[:, 32768:33792]
        t2_bf = ar_bf[:, 33792:34816]

        pA = pst("pA", [128, 512], F32)
        pB = pst("pB", [128, 512], F32)
        pE = pst("pE", [128, 512], F32)
        pF = pst("pF", [128, 512], F32)
        pX = pst("pX", [128, 512], F32)
        pG = pst("pG", [128, 512], F32)
        pS2 = pst("pS2", [64, 256], F32)
        pCb = pst("pCb", [128, 1024], BF16)

        sems = {}
        for name in ("pe", "ac", "dv", "po", "sy", "gp", "cc"):
            sems[name] = ctx.enter_context(nc.semaphore(name))
        block = ctx.enter_context(nc.Block())

        wo_rounds = [(e_i, jp) for e_i in range(E) for jp in range(16)]

        def walk(em: Em):
            for dst, src in ((sid, ident_in), (sid32, ident32_in), (ones32, ones32_in),
                             (mtril, mtril_in), (mtriu, mtriu_in), (ones1, ones1_in),
                             (wr_f, wr_in), (br_f, br_in), (bi_sb, bi_in), (bg_sb, bg_in),
                             (qsel_sb, qsel)):
                em("sy", lambda e, d=dst, s=src: e.dma_start(out=d[:], in_=s))
            em("dv", lambda e: e.memset(eps_sb[:, 0:1], LN_EPS))
            em("dv", lambda e: e.memset(eps_sb[:, 1:2], FFN_EPS))
            syn = em("sy", lambda e: e.dma_start(out=wstage[0:1, 0:HC], in_=bbeta_in))
            dvn = em("dv", lambda e: e.tensor_copy(out=bb_bf[:], in_=wstage[0:1, 0:HC]),
                     {"sy": syn})
            prev = {"dv": dvn, "ac": 0}
            for win, wbf, wdt in ((wq_in, wq_bf, HD), (wk_in, wk_bf, HD),
                                  (wvb_in, wvb_bf, HD + HC)):
                syn = em("sy", lambda e, w=win, wdt=wdt: e.dma_start(
                    out=wstage[:, 0:8 * wdt].rearrange("p (j n) -> p j n", n=wdt),
                    in_=w.rearrange("(j p) n -> p j n", p=128)), dict(prev))
                dvh = em("dv", lambda e, wb=wbf, wdt=wdt: e.tensor_copy(
                    out=wb[:, 0:4 * wdt], in_=wstage[:, 0:4 * wdt]), {"sy": syn})
                ach = em("ac", lambda e, wb=wbf, wdt=wdt: e.copy(
                    out=wb[:, 4 * wdt:8 * wdt], in_=wstage[:, 4 * wdt:8 * wdt]),
                    {"sy": syn})
                prev = {"dv": dvh, "ac": ach}

            busy = {0: {"dv": prev["dv"], "ac": prev["ac"]},
                    1: {"dv": prev["dv"], "ac": prev["ac"]}}
            wo_gp = {"ld": 0, "st": 0, "dv": prev["dv"], "ac": prev["ac"]}
            wo_round_idx = [0]

            def wo_precast_round():
                if wo_round_idx[0] >= len(wo_rounds):
                    return
                ri = wo_round_idx[0]
                wo_round_idx[0] += 1
                e_i, jp = wo_rounds[ri]
                par = ri % 2
                gpl = em("gp", lambda e: e.dma_start(
                    out=wst_wo[par].rearrange("p (j n) -> p j n", n=1024),
                    in_=wo_in[e_i].rearrange("(j p) n -> p j n", p=128)[:, jp * 2:jp * 2 + 2, :]),
                    {"dv": wo_gp["dv"], "ac": wo_gp["ac"]})
                dvc = em("dv", lambda e: e.tensor_copy(
                    out=wob_st[par][:, 0:1024], in_=wst_wo[par][:, 0:1024]),
                    {"gp": gpl})
                acc = em("ac", lambda e: e.copy(
                    out=wob_st[par][:, 1024:2048], in_=wst_wo[par][:, 1024:2048]),
                    {"gp": gpl})
                gps = em("gp", lambda e: e.dma_start(
                    out=wo_bscr[e_i].rearrange("(j p) n -> p j n", p=128)[:, jp * 2:jp * 2 + 2, :],
                    in_=wob_st[par].rearrange("p (j n) -> p j n", n=1024)),
                    {"dv": dvc, "ac": acc})
                if par == 1:
                    wo_gp["dv"] = dvc
                    wo_gp["ac"] = acc
                    wo_gp["st"] = gps

            def ln_tile(src, row, par, dst_bf):
                w = dict(busy[par])
                syn = em("sy", lambda e: e.dma_start(out=xs[par][:], in_=src[row:row + 128, :]),
                         w if w else None)
                dv1 = em("dv", lambda e: e.reduce_sum(out=small[:, 0:1], in_=xs[par][:],
                                                      axis=AX.X),
                         {"sy": syn, "dv": em.cnt["dv"], "ac": em.cnt["ac"]})
                ac1 = em("ac", lambda e: e.activation(out=scr32[:], in_=xs[par][:],
                                                      func=AF.Square, accum_out=small[:, 1:2]),
                         {"sy": syn, "dv": dv1})
                dv2 = em("dv", lambda e: e.tensor_scalar(out=small[:, 2:3], in0=small[:, 0:1],
                                                         scalar1=1.0 / D, scalar2=None,
                                                         op0=OP.mult), {"dv": dv1, "ac": ac1})
                dv3 = em("dv", lambda e: e.tensor_mul(out=small[:, 3:4], in0=small[:, 2:3],
                                                      in1=small[:, 2:3]), {"dv": dv2})
                dv4 = em("dv", lambda e: e.scalar_tensor_tensor(
                    out=small[:, 4:5], in0=small[:, 1:2], scalar=1.0 / D, in1=small[:, 3:4],
                    op0=OP.mult, op1=OP.subtract), {"dv": dv3})
                ac2 = em("ac", lambda e: e.activation(out=small[:, 5:6], in_=small[:, 4:5],
                                                      func=AF.Sqrt, bias=eps_sb[:, 0:1]), {"dv": dv4})
                dv5 = em("dv", lambda e: e.reciprocal(out=small[:, 6:7], in_=small[:, 5:6]),
                         {"ac": ac2})
                dv6 = em("dv", lambda e: e.tensor_scalar(out=small[:, 7:8], in0=small[:, 2:3],
                                                         scalar1=small[:, 6:7], scalar2=-1.0,
                                                         op0=OP.mult, op1=OP.mult), {"dv": dv5})
                ac3 = em("ac", lambda e: e.activation(out=dst_bf, in_=xs[par][:], func=AF.Identity,
                                                      bias=small[:, 7:8], scale=small[:, 6:7]),
                         {"dv": dv6})
                return syn, ac3

            def transpose8(src_bf, dsts, f32=False):
                last = 0
                pbank = pA if f32 else pCb
                ii = sid32 if f32 else sid
                for halfi in range(2):
                    w_ac = em.cnt["ac"]
                    for j in range(4):
                        jj = halfi * 4 + j
                        em("pe", lambda e, j=j, jj=jj: e.transpose(
                            pbank[:, j * 128:(j + 1) * 128],
                            src_bf[:, jj * 128:(jj + 1) * 128], ii[:]),
                            {"ac": w_ac, "dv": em.cnt["dv"]} if j == 0 else None)
                    pen = em.cnt["pe"]
                    for j in range(4):
                        jj = halfi * 4 + j
                        last = em("ac", lambda e, j=j, jj=jj: e.copy(
                            out=dsts[jj], in_=pbank[:, j * 128:(j + 1) * 128]),
                            {"pe": pen} if j == 0 else None)
                return last

            st_q = st_v = st_k = 0
            dv_kn_prev = 0
            for tt in range(NT):
                row = tt * 128
                par = tt % 2
                syn, ac_ln = ln_tile(xb, row, par, xn_bf)
                ac_tr = transpose8(xn_bf, [xqt_bf[:, j * 128:(j + 1) * 128] for j in range(8)])

                for j in range(8):
                    em("pe", lambda e, j=j: e.matmul(
                        pA[:, 0:HD], lhsT=xqt_bf[:, j * 128:(j + 1) * 128],
                        rhs=wq_bf[:, j * HD:(j + 1) * HD],
                        start=(j == 0), stop=(j == 7)),
                        {"ac": ac_tr} if j == 0 else None)
                ac_q = em("ac", lambda e: e.copy(out=qe_bf[:], in_=pA[:, 0:HD]),
                          {"pe": em.cnt["pe"], "sy": st_q})
                st_q = em("sy", lambda e, row=row: e.dma_start(
                    out=q_scr[row:row + 128, :], in_=qe_bf), {"ac": ac_q})

                for j in range(7):
                    em("pe", lambda e, j=j: e.matmul(
                        pX[:, 0:HD + HC], lhsT=xqt_bf[:, j * 128:(j + 1) * 128],
                        rhs=wvb_bf[:, j * (HD + HC):(j + 1) * (HD + HC)],
                        start=(j == 0), stop=False),
                        {"ac": ac_tr} if j == 0 else None)
                em("pe", lambda e: e.matmul(pX[:, HD:HD + HC], lhsT=ones1[:], rhs=bb_bf[:],
                                            start=False, stop=False))
                em("pe", lambda e: e.matmul(
                    pX[:, 0:HD + HC], lhsT=xqt_bf[:, 7 * 128:8 * 128],
                    rhs=wvb_bf[:, 7 * (HD + HC):8 * (HD + HC)],
                    start=False, stop=True))
                ac_v = em("ac", lambda e: e.copy(out=ve_bf[:], in_=pX[:, 0:HD]),
                          {"pe": em.cnt["pe"], "sy": st_v})
                em("ac", lambda e, tt=tt: e.activation(
                    out=beta_all[:, tt * HC:(tt + 1) * HC], in_=pX[:, HD:HD + HC],
                    func=AF.Sigmoid))
                st_v = em("sy", lambda e, row=row: e.dma_start(
                    out=v_scr[row:row + 128, :], in_=ve_bf), {"ac": em.cnt["ac"]})

                for j in range(8):
                    em("pe", lambda e, j=j: e.matmul(
                        pB[:, 0:HD], lhsT=xqt_bf[:, j * 128:(j + 1) * 128],
                        rhs=wk_bf[:, j * HD:(j + 1) * HD],
                        start=(j == 0), stop=(j == 7)),
                        {"dv": dv_kn_prev} if j == 0 else None)
                pe_k = em.cnt["pe"]
                ac_k1 = em("ac", lambda e: e.activation(out=scr32b[:, 0:HD], in_=pB[:, 0:HD],
                                                        func=AF.Square),
                           {"pe": pe_k, "dv": em.cnt["dv"]})
                dvk = em("dv", lambda e: e.reduce_sum(
                    out=small[:, 16:16 + HC],
                    in_=scr32b[:, 0:HD].rearrange("p (h d) -> p h d", d=64),
                    axis=AX.X), {"ac": ac_k1, "dv": em.cnt["dv"]})
                dvk = em("dv", lambda e: e.tensor_single_scalar(
                    out=small[:, 20:20 + HC], in_=small[:, 16:16 + HC],
                    scalar=K_EPS * K_EPS, op=OP.max), {"dv": dvk})
                ack = em("ac", lambda e: e.activation(out=small[:, 24:24 + HC],
                                                      in_=small[:, 20:20 + HC],
                                                      func=AF.Sqrt), {"dv": dvk})
                dvk = em("dv", lambda e: e.reciprocal(out=small[:, 28:28 + HC],
                                                      in_=small[:, 24:24 + HC]), {"ac": ack})
                for hh in range(HC):
                    dvk = em("dv", lambda e, hh=hh: e.tensor_scalar(
                        out=kn_bf[:, hh * 64:(hh + 1) * 64], in0=pB[:, hh * 64:(hh + 1) * 64],
                        scalar1=small[:, 28 + hh:29 + hh], scalar2=None, op0=OP.mult),
                        {"dv": dvk, "sy": st_k} if hh == 0 else None)
                dv_kn_prev = dvk
                st_k = em("sy", lambda e, row=row: e.dma_start(
                    out=k_scr[row:row + 128, :], in_=kn_bf), {"dv": dvk})
                wo_precast_round()
                wo_precast_round()
                busy[par] = {"pe": em.cnt["pe"], "dv": em.cnt["dv"], "ac": em.cnt["ac"]}
            st_qkv = max(st_q, st_v, st_k)

            for ltt in range(QNT):
                row = ltt * 128
                par = ltt % 2
                syn, ac_ln = ln_tile(xq, row, par, xn_bf)
                ac_tr = transpose8(
                    xn_bf, [xnt_res[:, j * QTOK + ltt * 128: j * QTOK + ltt * 128 + 128]
                            for j in range(8)])
                dvf = em("dv", lambda e: e.tensor_scalar(
                    out=scr32[:], in0=xs[par][:], scalar1=small[:, 2:3], scalar2=small[:, 6:7],
                    op0=OP.subtract, op1=OP.mult), {"dv": em.cnt["dv"], "ac": ac_ln})
                ac_t32 = transpose8(scr32, [scr32b[:, j * 128:(j + 1) * 128] for j in range(8)],
                                    f32=True)
                for j in range(8):
                    em("pe", lambda e, j=j: e.matmul(
                        pX[:, 0:E], lhsT=scr32b[:, j * 128:(j + 1) * 128],
                        rhs=wr_f[:, j * E:(j + 1) * E],
                        start=(j == 0), stop=False), {"ac": ac_t32} if j == 0 else None)
                em("pe", lambda e: e.matmul(pX[:, 0:E], lhsT=ones32[:], rhs=br_f[:],
                                            start=False, stop=True))
                ac_lg = em("ac", lambda e: e.copy(out=small[:, 8:16], in_=pX[:, 0:E]),
                           {"pe": em.cnt["pe"]})
                d1 = em("dv", lambda e: e.reduce_max(out=small[:, 16:17], in_=small[:, 8:16],
                                                     axis=AX.X), {"ac": ac_lg, "dv": em.cnt["dv"]})
                d2 = em("dv", lambda e: e.tensor_scalar(out=small[:, 17:18], in0=small[:, 16:17],
                                                        scalar1=-1.0, scalar2=None, op0=OP.mult),
                        {"dv": d1})
                d3 = em("dv", lambda e: e.tensor_scalar(out=small[:, 24:32], in0=small[:, 8:16],
                                                        scalar1=small[:, 16:17], scalar2=None,
                                                        op0=OP.is_equal), {"dv": d2})
                d4 = em("dv", lambda e: e.scalar_tensor_tensor(
                    out=small[:, 32:40], in0=small[:, 24:32], scalar=-1e30, in1=small[:, 8:16],
                    op0=OP.mult, op1=OP.add), {"dv": d3})
                d5 = em("dv", lambda e: e.reduce_max(out=small[:, 18:19], in_=small[:, 32:40],
                                                     axis=AX.X), {"dv": d4})
                d6 = em("dv", lambda e: e.tensor_scalar(out=small[:, 40:48], in0=small[:, 8:16],
                                                        scalar1=small[:, 18:19], scalar2=None,
                                                        op0=OP.is_ge), {"dv": d5})
                a7 = em("ac", lambda e: e.activation(out=small[:, 48:56], in_=small[:, 8:16],
                                                     func=AF.Exp, bias=small[:, 17:18]),
                        {"dv": d6})
                d8 = em("dv", lambda e: e.tensor_mul(out=small[:, 56:64], in0=small[:, 48:56],
                                                     in1=small[:, 40:48]), {"ac": a7})
                d9 = em("dv", lambda e: e.reduce_sum(out=small[:, 19:20], in_=small[:, 56:64],
                                                     axis=AX.X), {"dv": d8})
                d10 = em("dv", lambda e: e.reciprocal(out=small[:, 20:21], in_=small[:, 19:20]),
                         {"dv": d9})
                em("dv", lambda e, ltt=ltt: e.tensor_scalar(
                    out=c_sb[:, ltt * E:(ltt + 1) * E], in0=small[:, 56:64],
                    scalar1=small[:, 20:21], scalar2=None, op0=OP.mult), {"dv": d10})
                wo_precast_round()
                wo_precast_round()
                busy[par] = {"pe": em.cnt["pe"], "dv": em.cnt["dv"], "ac": em.cnt["ac"]}
            while wo_round_idx[0] < len(wo_rounds):
                wo_precast_round()

            def kt_h(buf, h):
                return buf[(h % 2) * 64:(h % 2) * 64 + 64, (h // 2) * 128:(h // 2) * 128 + 128]

            def s2_h(h):
                return s2bf[(h % 2) * 64:(h % 2) * 64 + 64, (h // 2) * 64:(h // 2) * 64 + 64]

            def ag_slot(h, off):
                return (pA if h % 2 == 0 else pB)[:, off + (h // 2) * 128:
                                                  off + (h // 2) * 128 + 128]

            def pu_slot(h):
                return (pG if h % 2 == 0 else pX)[:, (h // 2) * 64:(h // 2) * 64 + 64]

            first_w = {"sy": st_qkv, "dv": em.cnt["dv"], "ac": em.cnt["ac"],
                       "pe": em.cnt["pe"]}
            pre_info = {}
            ser_info = {}
            sstate = {"s_ready": 0, "dv_s": 0, "dv_u": 0, "ac_o": 0}
            gp_o = {}

            def pre(g):
                par = g % 2
                info = {}
                w0 = {"sy": st_qkv}
                if g >= 2:
                    w0["pe"] = ser_info[g - 2]["pe"]
                else:
                    w0.update({k: first_w[k] for k in ("pe", "dv", "ac")})
                row = g * C
                syn_k = em("sy", lambda e, row=row, par=par: e.dma_start(
                    out=ktm[par][:], in_=k_scr[row:row + C, :]), w0)
                syn_q = em("sy", lambda e, row=row, par=par: e.dma_start(
                    out=qtm[par][:], in_=q_scr[row:row + C, :]))
                syn_v = em("sy", lambda e, row=row, par=par: e.dma_start(
                    out=vtm[par][:], in_=v_scr[row:row + C, :]))
                info["syn_k"] = syn_k
                wx0 = {"sy": syn_v, "dv": em.cnt["dv"]}
                if g >= 1:
                    wx0["pe"] = pre_info[g - 1]["pe_k0"]
                dvx0 = 0
                for h in range(HC):
                    dvx0 = em("dv", lambda e, h=h, g=g, par=par: e.tensor_scalar(
                        out=x0b[:, h * 128:h * 128 + 64],
                        in0=vtm[par][:, h * 64:(h + 1) * 64],
                        scalar1=beta_all[:, g * HC + h:g * HC + h + 1],
                        scalar2=None, op0=OP.mult), wx0 if h == 0 else None)
                    dvx0 = em("dv", lambda e, h=h, g=g, par=par: e.tensor_scalar(
                        out=x0b[:, h * 128 + 64:h * 128 + 128],
                        in0=ktm[par][:, h * 64:(h + 1) * 64],
                        scalar1=beta_all[:, g * HC + h:g * HC + h + 1],
                        scalar2=None, op0=OP.mult), None)
                info["dvx0"] = dvx0
                wtr = {"sy": syn_q,
                       "ac": pre_info[g - 1]["ac_kht"] if g >= 1 else first_w["ac"]}
                for bb in range(2):
                    em("pe", lambda e, bb=bb, par=par: e.transpose(
                        pCb[:, bb * 128:(bb + 1) * 128],
                        ktm[par][:, bb * 128:(bb + 1) * 128], sid[:]),
                        wtr if bb == 0 else None)
                pen = em.cnt["pe"]
                ac_kt = em("ac", lambda e, par=par: e.copy(out=kt2[par][:], in_=pCb[:, 0:256]),
                           {"pe": pen})
                for bb in range(2):
                    em("pe", lambda e, bb=bb, par=par: e.transpose(
                        pCb[:, 256 + bb * 128: 256 + (bb + 1) * 128],
                        qtm[par][:, bb * 128:(bb + 1) * 128], sid[:]),
                        {"ac": ac_kt} if bb == 0 else None)
                pen = em.cnt["pe"]
                ac_qt = em("ac", lambda e, par=par: e.copy(out=qt2[par][:], in_=pCb[:, 256:512]),
                           {"pe": pen})
                info["ac_qt"] = ac_qt
                wag = {"ac": ac_kt, "dv": sstate["dv_u"]}
                if g >= 1:
                    wag["dv"] = max(wag["dv"], pre_info[g - 1]["dv_g"])
                if g < 2:
                    wag["dv"] = max(wag["dv"], first_w["dv"])
                for h in range(HC):
                    em("pe", lambda e, h=h, par=par: e.matmul(
                        ag_slot(h, 0),
                        lhsT=kt_h(kt2[par], h), rhs=kt_h(kt2[par], h),
                        start=True, stop=True),
                        wag if h == 0 else None)
                for h in range(HC):
                    em("pe", lambda e, h=h, par=par: e.matmul(
                        ag_slot(h, 256),
                        lhsT=kt_h(kt2[par], h), rhs=kt_h(qt2[par], h),
                        start=True, stop=True),
                        {"ac": ac_qt} if h == 0 else None)
                pe_ag = em.cnt["pe"]
                wm = {"pe": pe_ag, "dv": em.cnt["dv"]}
                if g >= 1:
                    wm["pe"] = max(pe_ag, pre_info[g - 1]["pe_all"])
                dv_a = 0
                for h in range(HC):
                    dv_a = em("dv", lambda e, h=h, g=g: e.scalar_tensor_tensor(
                        out=a2[0][:, h * 128:(h + 1) * 128],
                        in0=ag_slot(h, 0),
                        scalar=beta_all[:, g * HC + h:g * HC + h + 1],
                        in1=mtril[:], op0=OP.mult, op1=OP.mult),
                        wm if h == 0 else None)
                dv_g = dv_a
                for h in range(HC):
                    dv_g = em("dv", lambda e, h=h, par=par: e.tensor_tensor(
                        out=gm2[par][:, h * 128:(h + 1) * 128],
                        in0=ag_slot(h, 256), in1=mtriu[:], op=OP.mult),
                        {"pe": pe_ag, "dv": dv_g} if h == 0 else None)
                info["dv_g"] = dv_g
                wb0 = {"dv": dv_a, "ac": ac_qt}
                for h in range(HC):
                    em("pe", lambda e, h=h: e.transpose(
                        pCb[:, 512 + h * 128: 512 + (h + 1) * 128],
                        a2[0][:, h * 128:(h + 1) * 128], sid[:]),
                        wb0 if h == 0 else None)
                pen = em.cnt["pe"]
                ac_b = em("ac", lambda e: e.copy(out=b2[0][:], in_=pCb[:, 512:1024]),
                          {"pe": pen})
                xcur = x0b
                ac_prev_ab = ac_b
                dv_prev_x = dvx0
                dv_prev_b = 0
                pe_k0 = 0
                for k in range(7):
                    p = k % 2
                    xdst = wkp[par] if k == 6 else x2[k % 2]
                    wk_ = {"dv": max(dv_prev_x, dv_prev_b), "ac": ac_prev_ab}
                    for h in range(HC):
                        em("pe", lambda e, h=h, xcur=xcur: e.matmul(
                            pX[:, h * 128:(h + 1) * 128], lhsT=sid[:],
                            rhs=xcur[:, h * 128:(h + 1) * 128],
                            start=(h == 0), stop=False),
                            wk_ if h == 0 else None)
                    for h in range(HC):
                        em("pe", lambda e, h=h, p=p, xcur=xcur: e.matmul(
                            pX[:, h * 128:(h + 1) * 128],
                            lhsT=b2[p][:, h * 128:(h + 1) * 128],
                            rhs=xcur[:, h * 128:(h + 1) * 128],
                            start=False, stop=(h == 3)))
                    pen = em.cnt["pe"]
                    if k == 0:
                        pe_k0 = pen
                    dv_prev_x = em("dv", lambda e, xdst=xdst: e.tensor_copy(
                        out=xdst[:], in_=pX[:]), {"pe": pen})
                    if k < 6:
                        if k < 5:
                            for h in range(HC):
                                em("pe", lambda e, h=h, p=p: e.matmul(
                                    pE[:, h * 128:(h + 1) * 128],
                                    lhsT=b2[p][:, h * 128:(h + 1) * 128],
                                    rhs=a2[p][:, h * 128:(h + 1) * 128],
                                    start=(h == 0), stop=(h == 3)),
                                    {"ac": ac_prev_ab} if h == 0 else None)
                        for h in range(HC):
                            em("pe", lambda e, h=h, p=p: e.matmul(
                                pF[:, h * 128:(h + 1) * 128],
                                lhsT=a2[p][:, h * 128:(h + 1) * 128],
                                rhs=b2[p][:, h * 128:(h + 1) * 128],
                                start=(h == 0), stop=(h == 3)),
                                {"ac": ac_prev_ab} if h == 0 else None)
                        pen = em.cnt["pe"]
                        if k < 5:
                            ac_prev_ab = em("ac", lambda e, p=p: e.copy(
                                out=a2[1 - p][:], in_=pE[:]), {"pe": pen})
                        dv_prev_b = em("dv", lambda e, p=p: e.tensor_copy(
                            out=b2[1 - p][:], in_=pF[:]), {"pe": pen, "dv": dv_prev_b})
                        if k >= 5:
                            ac_prev_ab = em.cnt["ac"]
                    xcur = xdst
                info["dv_wkp"] = dv_prev_x
                wkh = {"dv": dv_prev_x, "ac": ac_b}
                for i, h in enumerate((0, 2, 1, 3)):
                    em("pe", lambda e, h=h, par=par: e.transpose(
                        pCb[(h % 2) * 64:(h % 2) * 64 + 64,
                            512 + (h // 2) * 128: 512 + (h // 2 + 1) * 128],
                        wkp[par][:, h * 128 + 64:h * 128 + 128], sid[:]),
                        wkh if i == 0 else None)
                pen = em.cnt["pe"]
                ac_kht = em("ac", lambda e, par=par: e.copy(
                    out=kht[par][:], in_=pCb[:, 512:768]), {"pe": pen})
                info["ac_kht"] = ac_kht
                info["pe_k0"] = pe_k0
                info["pe_all"] = em.cnt["pe"]
                pre_info[g] = info

            def khtv(par, h):
                return kht[par][(h % 2) * 64:(h % 2) * 64 + 64,
                                (h // 2) * 128:(h // 2 + 1) * 128]

            def ser(g):
                par = g % 2
                info = pre_info[g]
                if g == 0:
                    dv_u = 0
                    for h in range(HC):
                        dv_u = em("dv", lambda e, h=h, par=par: e.tensor_copy(
                            out=u2[:, h * 64:(h + 1) * 64],
                            in_=wkp[par][:, h * 128:h * 128 + 64]),
                            {"dv": info["dv_wkp"]} if h == 0 else None)
                else:
                    wu = {"ac": max(info["ac_kht"], sstate["s_ready"]),
                          "dv": max(sstate["dv_u"],
                                    pre_info[g + 1]["dv_wkp"] if (g + 1) in pre_info else 0)}
                    for h in range(HC):
                        em("pe", lambda e, h=h, par=par: e.matmul(
                            pu_slot(h),
                            lhsT=khtv(par, h), rhs=s2_h(h),
                            start=True, stop=True),
                            wu if h == 0 else None)
                    pen = em.cnt["pe"]
                    dv_u = 0
                    for h in range(HC):
                        dv_u = em("dv", lambda e, h=h, par=par: e.tensor_tensor(
                            out=u2[:, h * 64:(h + 1) * 64],
                            in0=wkp[par][:, h * 128:h * 128 + 64],
                            in1=pu_slot(h),
                            op=OP.subtract),
                            {"pe": pen, "dv": info["dvx0"]} if h == 0 else None)
                sstate["dv_u"] = dv_u
                wo_ = {"dv": max(dv_u, info["dv_g"]), "ac": sstate["ac_o"]}
                for h in range(HC):
                    em("pe", lambda e, h=h, par=par, g=g: e.matmul(
                        pG[:, 256 + h * 64: 256 + (h + 1) * 64],
                        lhsT=gm2[par][:, h * 128:(h + 1) * 128],
                        rhs=u2[:, h * 64:(h + 1) * 64],
                        start=True, stop=(g == 0)),
                        wo_ if h == 0 else None)
                    if g > 0:
                        em("pe", lambda e, h=h, par=par: e.matmul(
                            pG[:, 256 + h * 64: 256 + (h + 1) * 64],
                            lhsT=kt_h(qt2[par], h), rhs=s2_h(h),
                            start=False, stop=True))
                pen = em.cnt["pe"]
                wao = {"pe": pen}
                if g >= 2:
                    wao["gp"] = gp_o[g - 2]
                ac_o = em("ac", lambda e, par=par: e.copy(
                    out=o2[par][:], in_=pG[:, 256:512]), wao)
                sstate["ac_o"] = ac_o
                gp_o[g] = em("gp", lambda e, g=g, par=par: e.dma_start(
                    out=o_loc[g * C:(g + 1) * C, :], in_=o2[par][:]), {"ac": ac_o})
                ws = {"dv": max(dv_u, sstate["dv_s"])}
                for h in range(HC):
                    em("pe", lambda e, h=h, par=par: e.matmul(
                        pS2[0:64, h * 64:(h + 1) * 64],
                        lhsT=ktm[par][:, h * 64:(h + 1) * 64],
                        rhs=u2[:, h * 64:(h + 1) * 64],
                        start=(h == 0), stop=(h == 3)),
                        ws if h == 0 else None)
                pen = em.cnt["pe"]
                if g == 0:
                    dv_s = em("dv", lambda e: e.tensor_copy(out=s_f32, in_=pS2[:]),
                              {"pe": pen})
                else:
                    dv_s = em("dv", lambda e: e.tensor_add(out=s_f32, in0=s_f32, in1=pS2[:]),
                              {"pe": pen})
                sstate["dv_s"] = dv_s
                sv = s_f32.rearrange("p (h d) -> p h d", d=64)
                em("ac", lambda e, sv=sv: e.copy(
                    out=s2bf[0:64, :].rearrange("p (b d) -> p b d", d=64),
                    in_=sv[:, 0:HC:2, :]), {"dv": dv_s})
                sstate["s_ready"] = em("ac", lambda e, sv=sv: e.copy(
                    out=s2bf[64:128, :].rearrange("p (b d) -> p b d", d=64),
                    in_=sv[:, 1:HC:2, :]), None)
                ser_info[g] = {"pe": em.cnt["pe"], "dv": em.cnt["dv"], "ac": em.cnt["ac"]}

            pre(0)
            pre(1)
            for g in range(NCH):
                ser(g)
                if g + 2 < NCH:
                    pre(g + 2)
            gp_o_all = em.cnt["gp"]

            if em.stream == "po":
                em.eng.wait_ge(sems["gp"], gp_o_all)
                em.eng.collective_compute(
                    "AllGather", mybir.AluOpType.bypass,
                    replica_groups=[[0, 1, 2, 3], [4, 5, 6, 7]],
                    ins=[o_loc.opt()], outs=[o_gath.opt()],
                ).then_inc(sems["cc"])

            dv_f0 = em("dv", lambda e: e.memset(f_res[:], 0.0),
                       {"ac": em.cnt["ac"], "gp": wo_gp["st"]})
            prev_exp = {"pe": em.cnt["pe"], "dv": dv_f0, "ac": em.cnt["ac"],
                        "gp": em.cnt["gp"], "sy": em.cnt["sy"]}
            gp_h = 0
            hpar = 0
            for e_i in range(E):
                wrow = dict(prev_exp)
                em("sy", lambda e, e_i=e_i: e.dma_start(
                    out=grow_row[0:1, 0:1024], in_=gam_in[0:1, e_i * D:(e_i + 1) * D]), wrow)
                em("sy", lambda e, e_i=e_i: e.dma_start(
                    out=grow_row[0:1, 1024:2048], in_=bet_in[0:1, e_i * D:(e_i + 1) * D]))
                sy_r = em("sy", lambda e, e_i=e_i: e.dma_start(
                    out=grow_row[0:1, 2048:3072], in_=bo_in[0:1, e_i * D:(e_i + 1) * D]))
                dv_bo = em("dv", lambda e: e.tensor_copy(
                    out=bo_row[:], in_=grow_row[0:1, 2048:3072]),
                    {"sy": sy_r, "dv": prev_exp["dv"]})
                for half in range(2):
                    em("pe", lambda e, half=half: e.matmul(
                        pE[:, 0:512], lhsT=ones32[:],
                        rhs=grow_row[0:1, half * 512: half * 512 + 512],
                        start=True, stop=True),
                        {"sy": sy_r, "pe": prev_exp["pe"], "ac": prev_exp["ac"],
                         "dv": prev_exp["dv"]} if half == 0 else {"ac": em.cnt["ac"]})
                    em("ac", lambda e, half=half: e.copy(
                        out=grep_bf[:, half * 512:(half + 1) * 512], in_=pE[:]),
                        {"pe": em.cnt["pe"]})
                    em("pe", lambda e, half=half: e.matmul(
                        pE[:, 0:512], lhsT=ones32[:],
                        rhs=grow_row[0:1, 1024 + half * 512: 1024 + half * 512 + 512],
                        start=True, stop=True), {"ac": em.cnt["ac"]})
                    em("ac", lambda e, half=half: e.copy(
                        out=brep_bf[:, half * 512:(half + 1) * 512], in_=pE[:]),
                        {"pe": em.cnt["pe"]})
                ac_gb = em.cnt["ac"]

                prev_cast = {"dv": em.cnt["dv"], "ac": em.cnt["ac"]}
                banks_i = [pE, pF, pA, pB]
                bank_ac = {f4: em.cnt["ac"] for f4 in range(4)}
                dv_hsmul = em.cnt["dv"]
                for fc in range(4):
                    casts = []
                    for gate in range(2):
                        win = wi_in if gate == 0 else wg_in
                        wbf = wi_bf if gate == 0 else wg_bf
                        syw = em("sy", lambda e, win=win, e_i=e_i, fc=fc: e.dma_start(
                            out=wstage[:].rearrange("p (j n) -> p j n", n=1024),
                            in_=win[e_i].rearrange("(j p) n -> p j n", p=128)
                            [:, :, fc * 1024:(fc + 1) * 1024]),
                            {"dv": prev_cast["dv"], "ac": prev_cast["ac"],
                             "pe": em.cnt["pe"]})
                        dvc = em("dv", lambda e, wbf=wbf: e.tensor_copy(
                            out=wbf[:, 0:4096], in_=wstage[:, 0:4096]), {"sy": syw})
                        acc = em("ac", lambda e, wbf=wbf: e.copy(
                            out=wbf[:, 4096:8192], in_=wstage[:, 4096:8192]), {"sy": syw})
                        prev_cast = {"dv": dvc, "ac": acc}
                        casts.append((dvc, acc))
                    (dv_wi, ac_wi), (dv_wg, ac_wg) = casts
                    for st in range(4):
                        for g4 in range(2):
                            for f4 in range(4):
                                ffb = g4 * 4 + f4
                                for j in range(8):
                                    em("pe", lambda e, j=j, ffb=ffb, st=st, f4=f4: e.matmul(
                                        banks_i[f4][:, 0:512],
                                        lhsT=wi_bf[:, j * 1024 + ffb * 128: j * 1024 + (ffb + 1) * 128],
                                        rhs=xnt_res[:, j * QTOK + st * 512: j * QTOK + (st + 1) * 512],
                                        start=(j == 0), stop=(j == 7)),
                                        {"dv": dv_wi, "ac": max(ac_wi, bank_ac[f4])}
                                        if j == 0 else None)
                            pe_i4 = em.cnt["pe"]
                            gels = {}
                            for f4 in range(4):
                                bcol = e_i * 32 + fc * 8 + g4 * 4 + f4
                                gels[f4] = em("ac", lambda e, f4=f4, bcol=bcol: e.activation(
                                    out=hgs[f4], in_=banks_i[f4][:, 0:512], func=AF.Gelu,
                                    bias=bi_sb[:, bcol:bcol + 1]),
                                    {"pe": pe_i4, "dv": em.cnt["dv"],
                                     "po": em.cnt["po"]} if f4 == 0 else None)
                            for f4 in range(4):
                                ffb = g4 * 4 + f4
                                for j in range(8):
                                    em("pe", lambda e, j=j, ffb=ffb, st=st, f4=f4: e.matmul(
                                        banks_i[f4][:, 0:512],
                                        lhsT=wg_bf[:, j * 1024 + ffb * 128: j * 1024 + (ffb + 1) * 128],
                                        rhs=xnt_res[:, j * QTOK + st * 512: j * QTOK + (st + 1) * 512],
                                        start=(j == 0), stop=(j == 7)),
                                        {"dv": dv_wg, "ac": max(ac_wg, gels[f4])}
                                        if j == 0 else None)
                            pe_g4 = em.cnt["pe"]
                            for f4 in range(4):
                                bcol = e_i * 32 + fc * 8 + g4 * 4 + f4
                                sg = em("ac", lambda e, f4=f4, bcol=bcol: e.activation(
                                    out=hs_bf, in_=banks_i[f4][:, 0:512], func=AF.Sigmoid,
                                    bias=bg_sb[:, bcol:bcol + 1]),
                                    {"pe": pe_g4, "dv": dv_hsmul})
                                dv_hsmul = em("dv", lambda e, f4=f4, hpar=hpar: e.tensor_mul(
                                    out=h1_bf[hpar], in0=hgs[f4], in1=hs_bf),
                                    {"ac": sg, "gp": gp_h, "dv": em.cnt["dv"]})
                                gp_h = em("gp", lambda e, st=st, fc=fc, g4=g4, f4=f4,
                                          hpar=hpar, e_i=e_i: e.dma_start(
                                          out=h_scr[st, fc * 8 + g4 * 4 + f4],
                                          in_=h1_bf[hpar]),
                                          {"dv": dv_hsmul})
                                bank_ac[f4] = sg
                                hpar = 1 - hpar

                epi_dv = em.cnt["dv"]
                epi_ac = em.cnt["ac"]
                for dh in range(2):
                    wo_sub = {}
                    for kq in range(4):
                        wo_sub[kq] = em("sy", lambda e, dh=dh, e_i=e_i, kq=kq: e.dma_start(
                            out=wo_half[:, kq * 4096:(kq + 1) * 4096]
                            .rearrange("p (f n) -> p f n", n=512),
                            in_=wo_bscr[e_i].rearrange("(j p) n -> p j n", p=128)
                            [:, kq * 8:(kq + 1) * 8, dh * 512:(dh + 1) * 512]),
                            {"pe": em.cnt["pe"], "dv": em.cnt["dv"], "ac": em.cnt["ac"],
                             "gp": wo_gp["st"]} if kq == 0 else None)
                    if dh == 1:
                        epi_b = {0: epi_ac, 1: epi_ac}
                        epi_bd = {0: epi_dv, 1: epi_dv}
                    for st in range(4):
                        sl_sub = {}
                        for kq in range(4):
                            sl_sub[kq] = em("sy", lambda e, st=st, kq=kq: e.dma_start(
                                out=slab[:, kq * 4096:(kq + 1) * 4096]
                                .rearrange("p (f n) -> p f n", n=512),
                                in_=h_scr[st, kq * 8:(kq + 1) * 8].rearrange("f p n -> p f n")),
                                {"gp": gp_h, "pe": em.cnt["pe"]} if kq == 0 else None)
                        for t4 in range(4):
                            ltt = st * 4 + t4
                            if dh == 0:
                                pbank = pA
                                bid = 0
                            else:
                                bid = ltt % 2
                                pbank = pB if bid == 0 else pA
                            for kq in range(4):
                                for j in range(8):
                                    fb = kq * 8 + j
                                    w = None
                                    if kq == 0 and j == 0:
                                        w = {"sy": max(sl_sub[0], wo_sub[0]),
                                             "ac": (epi_ac if dh == 0 else epi_b[bid]),
                                             "dv": (epi_dv if dh == 0 else epi_bd[bid])}
                                    elif j == 0 and t4 == 0:
                                        w = {"sy": max(sl_sub[kq], wo_sub[kq])}
                                    em("pe", lambda e, fb=fb, t4=t4, pbank=pbank: e.matmul(
                                        pbank[:, 0:512],
                                        lhsT=slab[:, fb * 512 + t4 * 128: fb * 512 + t4 * 128 + 128],
                                        rhs=wo_half[:, fb * 512:(fb + 1) * 512],
                                        start=(kq == 0 and j == 0), stop=False), w)
                            for j4 in range(4):
                                em("pe", lambda e, j4=j4, dh=dh, ltt=ltt, pbank=pbank: e.matmul(
                                    pbank[:, j4 * 128:(j4 + 1) * 128],
                                    lhsT=xnt_res[:, (dh * 4 + j4) * QTOK + ltt * 128:
                                                 (dh * 4 + j4) * QTOK + ltt * 128 + 128],
                                    rhs=sid[:], start=False, stop=False))
                            em("pe", lambda e, dh=dh, pbank=pbank: e.matmul(
                                pbank[:, 0:512], lhsT=ones1[:],
                                rhs=bo_row[0:1, dh * 512: dh * 512 + 512],
                                start=False, stop=True))
                            pe_r = em.cnt["pe"]
                            if dh == 0:
                                epi_ac = em("ac", lambda e, ltt=ltt: e.copy(
                                    out=r0sp[ltt], in_=pA[:]), {"pe": pe_r})
                                epi_dv = em.cnt["dv"]
                            else:
                                ccol = ltt * E + e_i
                                d1 = em("dv", lambda e, ltt=ltt: e.reduce_sum(
                                    out=small[:, 0:1], in_=r0sp[ltt], axis=AX.X),
                                    {"pe": pe_r, "dv": em.cnt["dv"], "ac": em.cnt["ac"]})
                                d2 = em("dv", lambda e, pbank=pbank: e.reduce_sum(
                                    out=small[:, 1:2], in_=pbank[:], axis=AX.X), {"dv": d1})
                                d3 = em("dv", lambda e: e.tensor_add(
                                    out=small[:, 2:3], in0=small[:, 0:1], in1=small[:, 1:2]),
                                    {"dv": d2})
                                a1 = em("ac", lambda e, ltt=ltt: e.activation(
                                    out=pE[:], in_=r0sp[ltt], func=AF.Square,
                                    accum_out=small[:, 3:4]), {"dv": d3, "ac": em.cnt["ac"]})
                                a2_ = em("ac", lambda e, pbank=pbank: e.activation(
                                    out=pF[:], in_=pbank[:], func=AF.Square,
                                    accum_out=small[:, 4:5]), {"ac": a1})
                                d4 = em("dv", lambda e: e.tensor_add(
                                    out=small[:, 5:6], in0=small[:, 3:4], in1=small[:, 4:5]),
                                    {"ac": a2_, "dv": d3})
                                d5 = em("dv", lambda e: e.tensor_scalar(
                                    out=small[:, 6:7], in0=small[:, 2:3], scalar1=1.0 / D,
                                    scalar2=None, op0=OP.mult), {"dv": d4})
                                d6 = em("dv", lambda e: e.tensor_mul(
                                    out=small[:, 7:8], in0=small[:, 6:7], in1=small[:, 6:7]),
                                    {"dv": d5})
                                d7 = em("dv", lambda e: e.scalar_tensor_tensor(
                                    out=small[:, 8:9], in0=small[:, 5:6], scalar=1.0 / D,
                                    in1=small[:, 7:8], op0=OP.mult, op1=OP.subtract), {"dv": d6})
                                a3 = em("ac", lambda e: e.activation(
                                    out=small[:, 9:10], in_=small[:, 8:9], func=AF.Sqrt,
                                    bias=eps_sb[:, 1:2]), {"dv": d7})
                                d8 = em("dv", lambda e: e.reciprocal(
                                    out=small[:, 10:11], in_=small[:, 9:10]), {"ac": a3})
                                d9 = em("dv", lambda e, ccol=ccol: e.tensor_mul(
                                    out=small[:, 11:12], in0=small[:, 10:11],
                                    in1=c_sb[:, ccol:ccol + 1]), {"dv": d8})
                                d10 = em("dv", lambda e: e.tensor_scalar(
                                    out=small[:, 12:13], in0=small[:, 6:7],
                                    scalar1=small[:, 11:12], scalar2=-1.0,
                                    op0=OP.mult, op1=OP.mult), {"dv": d9})
                                a4 = em("ac", lambda e, ltt=ltt: e.activation(
                                    out=yc_bf[:, 0:512], in_=r0sp[ltt], func=AF.Identity,
                                    scale=small[:, 11:12], bias=small[:, 12:13]), {"dv": d10})
                                a5 = em("ac", lambda e, pbank=pbank: e.activation(
                                    out=yc_bf[:, 512:1024], in_=pbank[:], func=AF.Identity,
                                    scale=small[:, 11:12], bias=small[:, 12:13]), {"ac": a4})
                                d11 = em("dv", lambda e: e.tensor_mul(
                                    out=t2_bf, in0=yc_bf, in1=grep_bf[:]), {"ac": a5})
                                d12 = em("dv", lambda e, ltt=ltt, ccol=ccol: e.scalar_tensor_tensor(
                                    out=f_res[:, ltt * D:(ltt + 1) * D], in0=brep_bf[:],
                                    scalar=c_sb[:, ccol:ccol + 1],
                                    in1=f_res[:, ltt * D:(ltt + 1) * D],
                                    op0=OP.mult, op1=OP.add),
                                    {"dv": d11, "po": em.cnt["po"]})
                                po1 = em("po", lambda e, ltt=ltt: e.tensor_add(
                                    out=f_res[:, ltt * D:(ltt + 1) * D],
                                    in0=f_res[:, ltt * D:(ltt + 1) * D], in1=t2_bf),
                                    {"dv": d12})
                                epi_b[bid] = em.cnt["ac"]
                                epi_bd[bid] = em.cnt["dv"]
                prev_exp = {"pe": em.cnt["pe"], "dv": em.cnt["dv"], "ac": em.cnt["ac"],
                            "gp": em.cnt["gp"], "sy": em.cnt["sy"]}

            po_all = em.cnt["po"]
            w_all = {"pe": em.cnt["pe"], "dv": em.cnt["dv"], "ac": em.cnt["ac"],
                     "gp": em.cnt["gp"], "po": po_all}
            syn = em("sy", lambda e: e.dma_start(
                out=wstage[:].rearrange("p (j n) -> p j n", n=1024),
                in_=wout_in.rearrange("(j p) n -> p j n", p=128)), w_all)
            dvw = em("dv", lambda e: e.tensor_copy(out=wout_bf[:, 0:4096],
                                                   in_=wstage[:, 0:4096]),
                     {"sy": syn, "dv": em.cnt["dv"], "ac": em.cnt["ac"], "po": po_all})
            acw = em("ac", lambda e: e.copy(out=wout_bf[:, 4096:8192],
                                            in_=wstage[:, 4096:8192]),
                     {"sy": syn, "ac": em.cnt["ac"]})
            dv_sel = dvw
            for r in range(4):
                for j2 in range(2):
                    jj = r * 2 + j2
                    sy_t = 0
                    for qq in range(4):
                        sy_t = em("sy", lambda e, r=r, j2=j2, qq=qq: e.dma_start_transpose(
                            out=tp4[:, qq * 2048:(qq + 1) * 2048],
                            in_=o_gath[r * T + qq * 2048: r * T + (qq + 1) * 2048,
                                       j2 * 128:(j2 + 1) * 128]),
                            {"cc": 1, "dv": dv_sel} if qq == 0 else None)
                    dv_sel = em("dv", lambda e, jj=jj: e.tensor_scalar(
                        out=oT[:, jj * 2048:(jj + 1) * 2048], in0=tp4[:, 0:2048],
                        scalar1=qsel_sb[:, 0:1], scalar2=None, op0=OP.mult),
                        {"sy": sy_t, "dv": dv_sel})
                    for qq in range(1, 4):
                        dv_sel = em("dv", lambda e, qq=qq, jj=jj: e.scalar_tensor_tensor(
                            out=oT[:, jj * 2048:(jj + 1) * 2048],
                            in0=tp4[:, qq * 2048:(qq + 1) * 2048],
                            scalar=qsel_sb[:, qq:qq + 1],
                            in1=oT[:, jj * 2048:(jj + 1) * 2048],
                            op0=OP.mult, op1=OP.add), {"dv": dv_sel})
            prev_ac = acw
            prev_dm = 0
            prev_da = 0
            sy_out = 0
            for ltt in range(QNT):
                par = ltt % 2
                sy_x = em("sy", lambda e, ltt=ltt, par=par: e.dma_start(
                    out=xs[par][:], in_=xq[ltt * 128:(ltt + 1) * 128, :]),
                    {"dv": max(dvw, prev_da), "ac": acw})
                for half in range(2):
                    for j in range(8):
                        em("pe", lambda e, j=j, half=half, ltt=ltt: e.matmul(
                            pA[:, 0:512],
                            lhsT=oT[:, j * 2048 + ltt * 128: j * 2048 + ltt * 128 + 128],
                            rhs=wout_bf[:, j * 1024 + half * 512: j * 1024 + half * 512 + 512],
                            start=(j == 0), stop=(j == 7)),
                            {"dv": dv_sel, "ac": prev_ac} if j == 0 else None)
                    prev_ac = em("ac", lambda e, half=half: e.copy(
                        out=s32t[:, half * 512:(half + 1) * 512], in_=pA[:]),
                        {"pe": em.cnt["pe"], "dv": prev_dm})
                a_g = em("ac", lambda e, ltt=ltt: e.activation(
                    out=p45_g, in_=f_res[:, ltt * D:(ltt + 1) * D], func=AF.Silu),
                    {"po": po_all, "dv": prev_dm})
                prev_dm = em("dv", lambda e: e.tensor_mul(out=scr32, in0=s32t, in1=p45_g),
                             {"ac": a_g, "dv": prev_da})
                prev_da = em("dv", lambda e, par=par: e.tensor_add(
                    out=scr32b, in0=xs[par][:], in1=scr32),
                    {"dv": prev_dm, "sy": max(sy_x, sy_out)})
                sy_out = em("sy", lambda e, ltt=ltt: e.dma_start(
                    out=out[ltt * 128:(ltt + 1) * 128, :], in_=scr32b), {"dv": prev_da})
            em.wait("sy", {"sy": em.cnt["sy"], "gp": em.cnt["gp"]})

        @block.tensor
        def _(tensor):
            walk(Em("pe", tensor, sems))

        @block.scalar
        def _(scalar):
            walk(Em("ac", scalar, sems))

        @block.vector
        def _(vector):
            walk(Em("dv", vector, sems))

        @block.gpsimd
        def _(gpsimd):
            walk(Em("po", gpsimd, sems))

        @block.sync
        def _(sync):
            walk(Em("sy", sync, sems))

    return nc


_NC_CACHE = [None]


def kernel(L0_attn_x, L0_attn_Wq, L0_attn_Wk, L0_attn_Wv, L0_attn_Wbeta, L0_attn_b_beta,
           L0_attn_Wout, L0_ffn_Wr, L0_ffn_br, L0_ffn_Wi, L0_ffn_bi, L0_ffn_Wg, L0_ffn_bg,
           L0_ffn_Wo, L0_ffn_bo, L0_ffn_gamma, L0_ffn_beta):
    x = np.ascontiguousarray(np.asarray(L0_attn_x, dtype=np.float32))
    f32 = lambda a: np.ascontiguousarray(np.asarray(a, dtype=np.float32))
    Wq, Wk, Wv = f32(L0_attn_Wq), f32(L0_attn_Wk), f32(L0_attn_Wv)
    Wbeta, b_beta, Wout = f32(L0_attn_Wbeta), f32(L0_attn_b_beta), f32(L0_attn_Wout)
    Wr, br = f32(L0_ffn_Wr), f32(L0_ffn_br)
    Wi, bi, Wg, bg = f32(L0_ffn_Wi), f32(L0_ffn_bi), f32(L0_ffn_Wg), f32(L0_ffn_bg)
    Wo, bo = f32(L0_ffn_Wo), f32(L0_ffn_bo)
    gamma, beta = f32(L0_ffn_gamma), f32(L0_ffn_beta)

    if _NC_CACHE[0] is None:
        _NC_CACHE[0] = build_nc()
    nc = _NC_CACHE[0]

    wr_r = np.ascontiguousarray(Wr.reshape(8, 128, E).transpose(1, 0, 2).reshape(128, 64))
    bi_r = np.ascontiguousarray(bi.reshape(E, 32, 128).transpose(2, 0, 1).reshape(128, E * 32))
    bg_r = np.ascontiguousarray(bg.reshape(E, 32, 128).transpose(2, 0, 1).reshape(128, E * 32))
    bo_r = np.ascontiguousarray(bo.reshape(1, E * D))
    gam_r = np.ascontiguousarray(gamma.reshape(1, E * D))
    bet_r = np.ascontiguousarray(beta.reshape(1, E * D))
    br_r = np.ascontiguousarray(br.reshape(1, E))
    ident = np.eye(128, dtype=np.float32).astype(ml_dtypes.bfloat16)
    ident32 = np.eye(128, dtype=np.float32)
    ones_bf = np.ones((1, 128), np.float32).astype(ml_dtypes.bfloat16)
    ones_f = np.ones((1, 128), np.float32)
    m_tril = -np.tril(np.ones((128, 128), np.float32), -1)
    m_triu = np.triu(np.ones((128, 128), np.float32), 0)

    common = dict(wout_in=Wout, wr_in=wr_r, br_in=br_r, wi_in=Wi, wg_in=Wg, wo_in=Wo,
                  bi_in=bi_r, bg_in=bg_r, bo_in=bo_r, gam_in=gam_r, bet_in=bet_r,
                  ident_in=ident, ident32_in=ident32, ones32_in=ones_f,
                  mtril_in=m_tril, mtriu_in=m_triu, ones1_in=ones_bf)
    in_maps = []
    for c in range(8):
        b, r = c // 4, c % 4
        qs = np.zeros((128, 4), np.float32)
        qs[:, r] = 1.0
        m = dict(common)
        m["xb"] = x[b]
        m["xq"] = np.ascontiguousarray(x[b, r * QTOK:(r + 1) * QTOK])
        m["qsel"] = qs
        m["wq_in"] = np.ascontiguousarray(Wq[:, r * HD:(r + 1) * HD])
        m["wk_in"] = np.ascontiguousarray(Wk[:, r * HD:(r + 1) * HD])
        m["wvb_in"] = np.ascontiguousarray(
            np.concatenate([Wv[:, r * HD:(r + 1) * HD],
                            Wbeta[:, r * HC:(r + 1) * HC]], axis=1))
        m["bbeta_in"] = np.ascontiguousarray(b_beta[r * HC:(r + 1) * HC].reshape(1, HC))
        in_maps.append(m)

    res = run_bass_kernel_spmd(nc, in_maps, core_ids=list(range(8)))
    outp = np.empty((B, T, D), np.float32)
    for c in range(8):
        b, r = c // 4, c % 4
        outp[b, r * QTOK:(r + 1) * QTOK] = res.results[c]["out"]
    return outp
